# version 12
# baseline (speedup 1.0000x reference)
import os
import numpy as np
import concourse.bass as bass
import concourse.mybir as mybir
from concourse.bass_utils import run_bass_kernel_spmd
from contextlib import ExitStack

F32 = mybir.dt.float32
BF16 = mybir.dt.bfloat16
AF = mybir.ActivationFunctionType
ALU = mybir.AluOpType
AX = mybir.AxisListType.X

KCH = int(os.environ.get("KCH", "0")) or None
KPH = int(os.environ.get("KPH", "99"))
KDBG = int(os.environ.get("KDBG", "0"))
T, D = 2048, 2048
TC = 512
NCH = T // TC
NTG = TC // 128
NDC = D // 128
NKC = D // 128
NST = 16
DTR = 128
NEXP = 32
FEXP = 768
NFC = FEXP // 128
FSH = 1536
EPS = 1e-5
W1 = 516
TH = 1024
NSB = TH // 512
CAP = 512
NSLOT = NEXP * CAP
BIGF = 4.0e6

ENGS = ("sync", "tensor", "vector", "scalar", "gpsimd")


class Sem:
    def __init__(self, nc, name):
        self.h = nc.alloc_semaphore(name=name)
        self.v = 0
        self.name = name


def build_nc():
    nc = bass.Bass("TRN2", target_bir_lowering=False)
    nc.detect_race_conditions = False

    x_h = nc.dram_tensor("x", [T, D], F32, kind="ExternalInput").ap()
    cw_h = nc.dram_tensor("conv_w", [D, 4], F32, kind="ExternalInput").ap()
    cb_h = nc.dram_tensor("conv_b", [D], F32, kind="ExternalInput").ap()
    xpw_h = nc.dram_tensor("xproj_w", [160, D], F32, kind="ExternalInput").ap()
    dtw_h = nc.dram_tensor("dtproj_w", [D, DTR], F32, kind="ExternalInput").ap()
    dtb_h = nc.dram_tensor("dtproj_b", [D], F32, kind="ExternalInput").ap()
    al_h = nc.dram_tensor("A_log", [D, NST], F32, kind="ExternalInput").ap()
    dp_h = nc.dram_tensor("D_param", [D], F32, kind="ExternalInput").ap()
    gw_h = nc.dram_tensor("gate_w", [D, NEXP], F32, kind="ExternalInput").ap()
    wg_h = nc.dram_tensor("wg", [NEXP, D, FEXP], F32, kind="ExternalInput").ap()
    wu_h = nc.dram_tensor("wu", [NEXP, D, FEXP], F32, kind="ExternalInput").ap()
    wd_h = nc.dram_tensor("wd", [NEXP, FEXP, D], F32, kind="ExternalInput").ap()
    sg_h = nc.dram_tensor("sg", [D, FSH], F32, kind="ExternalInput").ap()
    su_h = nc.dram_tensor("su", [D, FSH], F32, kind="ExternalInput").ap()
    sd_h = nc.dram_tensor("sd", [FSH, D], F32, kind="ExternalInput").ap()
    iden_h = nc.dram_tensor("iden", [128, 128], F32, kind="ExternalInput").ap()
    triu_h = nc.dram_tensor("triu", [128, 128], F32, kind="ExternalInput").ap()
    ones1_h = nc.dram_tensor("ones1", [1, 128], F32, kind="ExternalInput").ap()
    cstf_h = nc.dram_tensor("cstf", [128, 80], F32, kind="ExternalInput").ap()
    out_h = nc.dram_tensor("out", [T, D], F32, kind="ExternalOutput").ap()
    dbg = "ExternalOutput" if KDBG else "Internal"
    bc_d = nc.dram_tensor("bc_d", [32, T], F32, kind="Internal").ap()
    x2_d = nc.dram_tensor("x2_d", [T, D], F32, kind=dbg).ap()
    zb_d = nc.dram_tensor("zb_d", [NKC, 128, T], BF16, kind="Internal").ap()
    ztok_d = nc.dram_tensor("ztok_d", [T + 1, D], F32, kind=dbg).ap()
    tok_d = nc.dram_tensor("tok_d", [NSLOT + 128, 1], F32, kind=dbg).ap()
    slot6_d = nc.dram_tensor("slot6_d", [T, 8], F32, kind=dbg).ap()
    combc_d = nc.dram_tensor("combc_d", [T, 8], F32, kind=dbg).ap()
    rout_d = nc.dram_tensor("rout_d", [NSLOT + 1, D], BF16, kind=dbg).ap()
    dbga_d = nc.dram_tensor("dbga_d", [128, 12288], BF16, kind=dbg).ap()
    dbgz_d = nc.dram_tensor("dbgz_d", [128, 8192], BF16, kind=dbg).ap()
    dbgp_d = nc.dram_tensor("dbgp_d", [128, 8 * 384], F32, kind=dbg).ap()

    ctxP = ExitStack()
    ctxM = ExitStack()
    sbtP = lambda name, shape, dt=F32: ctxP.enter_context(
        nc.sbuf_tensor("sb_" + name, shape, dt))
    sbtM = lambda name, shape, dt=F32: ctxM.enter_context(
        nc.sbuf_tensor("sb_" + name, shape, dt))
    psb = lambda name, shape: ctxP.enter_context(nc.psum_tensor(name, shape, F32))

    iden = sbtP("iden", [128, 128])
    xin = sbtP("xin", [128, 2048])
    stats = sbtP("stats", [128, 16])
    sixs = sbtP("sixs", [128, (T // 128) * 32], mybir.dt.int32)
    tcolv = sbtP("tcolv", [128, T // 128])

    xpwT = sbtM("xpwT", [128, NKC * 160])
    dtpwT = sbtM("dtpwT", [128, D])
    aneg = sbtM("aneg", [128, NDC * NST])
    cw = sbtM("cw", [128, NDC * 4])
    cb = sbtM("cbv", [128, NDC])
    db = sbtM("dbv", [128, NDC])
    dpv = sbtM("dpv", [128, NDC])
    gwT = sbtM("gwT", [128, NKC * 32])
    P1 = sbtM("P1", [128, NDC * W1])
    P2 = sbtM("P2", [128, NDC * 512])
    P3 = sbtM("P3", [128, NDC * 512])
    P4 = sbtM("P4", [128, NDC * 512])
    P5 = sbtM("P5", [128, 2048])
    a2 = sbtM("a2", [128, 1024])
    bbuf = sbtM("bbuf", [128, 512])
    h2 = sbtM("h2", [128, 1024])
    ytb = sbtM("ytb", [128, 1024])
    state = sbtM("state", [128, 256])
    brep = sbtM("brep", [128, 2048])
    dtT = sbtM("dtT", [128, 512])
    zt2 = sbtM("zt2", [128, 1024], BF16)
    bcT = sbtM("bcT", [32, 512])
    halo = sbtM("halo", [128, NDC * 3])
    spl = sbtM("spl", [128, 1024])
    triu = sbtM("triu", [128, 128])
    ones1 = sbtM("ones1", [1, 128])
    cst = sbtM("cst", [128, 80])
    eoffm1 = cst[:, 0:32]
    onesf = cst[:, 32:64]
    iotac = cst[:, 64:65]
    scrT = spl[0:32, 0:512]
    tk6 = spl[:, 512:520]
    ck6 = spl[:, 520:528]
    scr_tm = a2[:, 0:128]
    m8t = a2[:, 256:288]
    selb = a2[:, 288:320]
    cmbb = a2[:, 320:352]
    ksel = a2[:, 352:384]
    msk = a2[:, 384:416]
    sl1 = a2[:, 416:448]
    sl2 = a2[:, 448:480]
    notsel = a2[:, 480:512]

    ps = [psb(f"ps{i}", [128, 512]) for i in range(6)]
    ptf = [psb(f"pt{i}", [128, 512]) for i in range(2)]
    pt = [p[:, 0:128] for p in ptf]

    s_ld = Sem(nc, "s_ld")
    s_st = Sem(nc, "s_st")
    s_pe = Sem(nc, "s_pe")
    s_dve = Sem(nc, "s_dve")
    s_act = Sem(nc, "s_act")
    s_pool = Sem(nc, "s_pool")
    s_gdma = Sem(nc, "s_gdma")
    s_gsc = Sem(nc, "s_gsc")
    s_zb = Sem(nc, "s_zb")
    s_gu = [Sem(nc, f"s_gu{i}") for i in range(6)]
    s_dn = Sem(nc, "s_dn")
    s_ix = Sem(nc, "s_ix")
    s_ep = [Sem(nc, "s_ep0"), Sem(nc, "s_ep1")]
    s_ga = [Sem(nc, "s_ga0"), Sem(nc, "s_ga1")]
    ALLSEMS = (s_ld, s_st, s_pe, s_dve, s_act, s_pool, s_gdma, s_gsc, s_zb,
               s_gu[0], s_gu[1], s_gu[2], s_gu[3], s_gu[4], s_gu[5],
               s_dn, s_ix, s_ep[0], s_ep[1], s_ga[0], s_ga[1])

    for sm in ALLSEMS:
        nc.gpsimd.sem_clear(sm.h)
    nc.all_engine_barrier()

    EM = {e: [] for e in ENGS}
    WAITS = []

    def op(eng, fn, incs=()):
        def thunk(fn=fn, incs=tuple(incs)):
            ins = fn()
            for s, k in incs:
                ins = ins.then_inc(s.h, k)
        EM[eng].append(thunk)
        for s, k in incs:
            s.v += k

    def wt(eng, sem, val):
        if val <= 0:
            return
        WAITS.append((sem, val))
        EM[eng].append(lambda sem=sem, val=val, eng=eng:
                       getattr(nc, eng).wait_ge(sem.h, val))

    E = {e: getattr(nc, e) for e in ENGS}

    bank_free = {}

    def pe_claim(bank):
        for sem, val in bank_free.get(bank, ()):
            wt("tensor", sem, val)
        bank_free[bank] = []

    def bank_consumed(bank, sem):
        bank_free.setdefault(bank, []).append((sem, sem.v))

    bfc = lambda apx: apx.bitcast(BF16)

    def bhalf(tile_ap, c0, width):
        return bfc(tile_ap)[:, 2 * c0 + 1: 2 * (c0 + width): 2]

    def wait_list(eng, lst):
        for sem, val in lst:
            wt(eng, sem, val)

    basev = sbtM("basev", [1, 32])

    op("vector", lambda: E["vector"].memset(state[:], 0.0))
    op("vector", lambda: E["vector"].memset(basev[:], 0.0))
    op("vector", lambda: E["vector"].memset(P5[:], 0.0), incs=[(s_dve, 1)])
    v_ztm0 = s_dve.v
    op("vector", lambda: E["vector"].memset(spl[:], float(T)),
       incs=[(s_dve, 1)])
    v_spl0 = s_dve.v
    wt("sync", s_dve, v_spl0)
    tok2d = tok_d.rearrange("(a b) o -> a (b o)", b=128)
    NROW = NSLOT // 128 + 1
    for r0 in range(0, NROW, 16):
        r1 = min(r0 + 16, NROW)
        op("sync", lambda r0=r0, r1=r1: E["sync"].dma_start(
            out=tok2d[r0:r1, :], in_=spl[0:r1 - r0, 0:128]),
            incs=[(s_st, 16)])
    wt("sync", s_dve, v_ztm0)
    op("sync", lambda: E["sync"].dma_start(
        out=ztok_d[T:T + 1, :], in_=P5[0:1, 0:2048]), incs=[(s_st, 16)])
    op("sync", lambda: E["sync"].dma_start(
        out=rout_d[NSLOT:NSLOT + 1, :], in_=bfc(P5[:])[0:1, 0:2048]),
        incs=[(s_st, 16)])
    v_prefill = s_st.v
    op("sync", lambda: E["sync"].dma_start(out=triu[:], in_=triu_h),
       incs=[(s_ld, 16)])
    op("sync", lambda: E["sync"].dma_start(out=ones1[:], in_=ones1_h),
       incs=[(s_ld, 16)])
    op("sync", lambda: E["sync"].dma_start(out=cst[:], in_=cstf_h),
       incs=[(s_ld, 16)])
    v_cst = s_ld.v
    for dc in range(NDC):
        last = dc == NDC - 1
        op("vector", lambda dc=dc: E["vector"].memset(
            P1[:, dc * W1: dc * W1 + 4], 0.0),
            incs=([(s_dve, 1)] if last else []))

    op("sync", lambda: E["sync"].dma_start(out=iden[:], in_=iden_h),
       incs=[(s_ld, 16)])
    op("sync", lambda: E["sync"].dma_start(
        out=cw[:].rearrange("p (c k) -> p c k", k=4),
        in_=cw_h.rearrange("(c p) k -> p c k", p=128)), incs=[(s_ld, 16)])
    op("sync", lambda: E["sync"].dma_start(
        out=aneg[:].rearrange("p (c n) -> p c n", n=NST),
        in_=al_h.rearrange("(c p) n -> p c n", p=128)), incs=[(s_ld, 16)])
    op("sync", lambda: E["sync"].dma_start(
        out=gwT[:].rearrange("p (c e) -> p c e", e=NEXP),
        in_=gw_h.rearrange("(c p) e -> p c e", p=128)), incs=[(s_ld, 16)])
    v0 = s_ld.v

    wt("scalar", s_ld, v0)
    op("scalar", lambda: E["scalar"].activation(
        out=aneg[:], in_=aneg[:], func=AF.Exp), incs=[(s_act, 1)])
    wt("vector", s_act, s_act.v)
    op("vector", lambda: E["vector"].tensor_scalar(
        out=aneg[:], in0=aneg[:], scalar1=-1.0, scalar2=None, op0=ALU.mult),
        incs=[(s_dve, 1)])

    for vec_h, dst in ((cb_h, cb), (dtb_h, db), (dp_h, dpv)):
        wt("sync", s_pe, s_pe.v)
        op("sync", lambda vec_h=vec_h: E["sync"].dma_start(
            out=bcT[0:16, 0:128], in_=vec_h.rearrange("(c p) -> c p", p=128)),
            incs=[(s_ld, 16)])
        wt("tensor", s_ld, s_ld.v)
        pe_claim("pt0")
        op("tensor", lambda: E["tensor"].transpose(
            pt[0][:, 0:16], bcT[0:16, 0:128], iden[0:16, 0:16]),
            incs=[(s_pe, 1)])
        wt("vector", s_pe, s_pe.v)
        op("vector", lambda dst=dst: E["vector"].tensor_copy(
            out=dst[:], in_=pt[0][:, 0:16]), incs=[(s_dve, 1)])
        bank_consumed("pt0", s_dve)

    op("sync", lambda: E["sync"].dma_start(out=xin[:], in_=xpw_h[0:128, :]),
       incs=[(s_ld, 16)])
    v = s_ld.v
    for kc in range(NKC):
        b = f"pt{kc % 2}"
        wt("tensor", s_ld, v)
        pe_claim(b)
        op("tensor", lambda kc=kc: E["tensor"].transpose(
            pt[kc % 2], xin[:, kc * 128:(kc + 1) * 128], iden[:]),
            incs=[(s_pe, 1)])
        wt("vector", s_pe, s_pe.v)
        op("vector", lambda kc=kc: E["vector"].tensor_copy(
            out=xpwT[:, kc * 160: kc * 160 + 128], in_=pt[kc % 2]),
            incs=[(s_dve, 1)])
        bank_consumed(b, s_dve)
    wt("sync", s_pe, s_pe.v)
    op("sync", lambda: E["sync"].dma_start(
        out=xin[0:32, :], in_=xpw_h[128:160, :]), incs=[(s_ld, 16)])
    v = s_ld.v
    for kc in range(NKC):
        b = f"pt{kc % 2}"
        wt("tensor", s_ld, v)
        pe_claim(b)
        op("tensor", lambda kc=kc: E["tensor"].transpose(
            pt[kc % 2][:, 0:32], xin[0:32, kc * 128:(kc + 1) * 128],
            iden[0:32, 0:32]), incs=[(s_pe, 1)])
        wt("vector", s_pe, s_pe.v)
        op("vector", lambda kc=kc: E["vector"].tensor_copy(
            out=xpwT[:, kc * 160 + 128: (kc + 1) * 160],
            in_=pt[kc % 2][:, 0:32]), incs=[(s_dve, 1)])
        bank_consumed(b, s_dve)

    for kc in range(NKC):
        half = (kc % 2) * 1024
        wt("sync", s_pe, s_pe.v)
        op("sync", lambda kc=kc, half=half: E["sync"].dma_start(
            out=xin[:, half:half + 128],
            in_=dtw_h[kc * 128:(kc + 1) * 128, :]), incs=[(s_ld, 16)])
        b = f"pt{kc % 2}"
        wt("tensor", s_ld, s_ld.v)
        pe_claim(b)
        op("tensor", lambda kc=kc, half=half: E["tensor"].transpose(
            pt[kc % 2], xin[:, half:half + 128], iden[:]), incs=[(s_pe, 1)])
        wt("vector", s_pe, s_pe.v)
        op("vector", lambda kc=kc: E["vector"].tensor_copy(
            out=dtpwT[:, kc * 128:(kc + 1) * 128], in_=pt[kc % 2]),
            incs=[(s_dve, 1)])
        bank_consumed(b, s_dve)

    free_xin = [(s_pe, s_pe.v)]
    free_a = [[], []]
    free_h = [[], []]
    free_yt = [[], []]
    free_brep = [[], []]
    free_spl = [(s_st, v_prefill)]
    free_dtT = []
    free_zt2 = [[], []]
    free_bcT = [(s_dve, s_dve.v)]

    v_sqred = [0]
    v_x2_spill = [0] * (NCH * NTG)
    v_zb_spill = 0
    prev_end = []

    for c in range(KCH or NCH):
        if c > 0:
            for eng in ENGS:
                for sem, val in prev_end:
                    wt(eng, sem, val)

        if KPH < 1:
            continue
        for tg in range(NTG):
            wait_list("sync", free_xin)
            rows = c * TC + tg * 128
            op("sync", lambda rows=rows: E["sync"].dma_start(
                out=xin[:], in_=x_h[rows:rows + 128, :]), incs=[(s_ld, 16)])
            v_x = s_ld.v
            wt("scalar", s_ld, v_x)
            wt("scalar", s_dve, v_sqred[0])
            if c == 0 and tg == 0:
                wt("scalar", s_st, v_prefill)
            op("scalar", lambda: E["scalar"].activation(
                out=P5[:, 0:2048], in_=xin[:], func=AF.Square),
                incs=[(s_act, 1)])
            v_sq = s_act.v
            wt("vector", s_ld, v_x)
            op("vector", lambda: E["vector"].reduce_sum(
                stats[:, 0:1], xin[:], axis=AX), incs=[(s_dve, 1)])
            wt("vector", s_dve, s_dve.v)
            op("vector", lambda: E["vector"].tensor_scalar(
                out=stats[:, 0:1], in0=stats[:, 0:1], scalar1=1.0 / D,
                scalar2=None, op0=ALU.mult))
            wt("vector", s_act, v_sq)
            op("vector", lambda: E["vector"].reduce_sum(
                stats[:, 4:5], P5[:, 0:2048], axis=AX), incs=[(s_dve, 1)])
            v_sqred[0] = s_dve.v
            wt("vector", s_dve, s_dve.v)
            op("vector", lambda: E["vector"].tensor_scalar(
                out=stats[:, 4:5], in0=stats[:, 4:5], scalar1=1.0 / D,
                scalar2=None, op0=ALU.mult))
            op("vector", lambda: E["vector"].tensor_tensor(
                out=stats[:, 8:9], in0=stats[:, 0:1], in1=stats[:, 0:1],
                op=ALU.mult), incs=[(s_dve, 1)])
            wt("vector", s_dve, s_dve.v)
            op("vector", lambda: E["vector"].tensor_tensor(
                out=stats[:, 4:5], in0=stats[:, 4:5], in1=stats[:, 8:9],
                op=ALU.subtract), incs=[(s_dve, 1)])
            wt("scalar", s_dve, s_dve.v)
            op("scalar", lambda: E["scalar"].activation(
                out=stats[:, 8:9], in_=stats[:, 4:5], func=AF.Ln, bias=np.float32(EPS)))
            op("scalar", lambda: E["scalar"].activation(
                out=stats[:, 8:9], in_=stats[:, 8:9], func=AF.Exp, scale=np.float32(-0.5)),
                incs=[(s_act, 1)])
            wt("vector", s_act, s_act.v)
            op("vector", lambda: E["vector"].tensor_scalar(
                out=xin[:], in0=xin[:], scalar1=stats[:, 0:1],
                scalar2=stats[:, 8:9], op0=ALU.subtract, op1=ALU.mult),
                incs=[(s_dve, 1)])
            v_ln = s_dve.v
            for dc in range(NDC):
                b = f"pt{dc % 2}"
                wt("tensor", s_dve, v_ln)
                pe_claim(b)
                op("tensor", lambda dc=dc: E["tensor"].transpose(
                    pt[dc % 2], xin[:, dc * 128:(dc + 1) * 128], iden[:]),
                    incs=[(s_pe, 1)])
                wt("vector", s_pe, s_pe.v)
                op("vector", lambda dc=dc, tg=tg: E["vector"].tensor_copy(
                    out=P1[:, dc * W1 + 4 + tg * 128:
                            dc * W1 + 4 + (tg + 1) * 128],
                    in_=pt[dc % 2]), incs=[(s_dve, 1)])
                bank_consumed(b, s_dve)
            free_xin = [(s_pe, s_pe.v)]

        if c > 0:
            for dc in range(NDC):
                op("vector", lambda dc=dc: E["vector"].tensor_copy(
                    out=P1[:, dc * W1 + 1: dc * W1 + 4],
                    in_=halo[:, dc * 3:(dc + 1) * 3]),
                    incs=([(s_dve, 1)] if dc == NDC - 1 else []))

        if KPH < 2:
            continue
        v_conv_act = 0
        for dc in range(NDC):
            base = dc * W1
            wait_list("vector", free_spl)
            op("vector", lambda base=base, dc=dc: E["vector"].tensor_scalar(
                out=spl[:, 0:512], in0=P1[:, base + 1: base + 513],
                scalar1=cw[:, dc * 4: dc * 4 + 1], scalar2=None, op0=ALU.mult))
            for k in range(1, 4):
                op("vector", lambda base=base, dc=dc, k=k:
                   E["vector"].scalar_tensor_tensor(
                    out=spl[:, 0:512], in0=P1[:, base + 1 + k: base + 513 + k],
                    scalar=cw[:, dc * 4 + k: dc * 4 + k + 1],
                    in1=spl[:, 0:512], op0=ALU.mult, op1=ALU.add),
                    incs=([(s_dve, 1)] if k == 3 else []))
            wt("scalar", s_dve, s_dve.v)
            op("scalar", lambda dc=dc: E["scalar"].activation(
                out=P2[:, dc * 512:(dc + 1) * 512], in_=spl[:, 0:512],
                func=AF.Silu, bias=cb[:, dc:dc + 1]), incs=[(s_act, 1)])
            free_spl = [(s_act, s_act.v)]
            v_conv_act = s_act.v
            op("vector", lambda dc=dc, base=base: E["vector"].tensor_scalar(
                out=P4[:, dc * 512:(dc + 1) * 512],
                in0=P1[:, base + 4: base + 516],
                scalar1=dpv[:, dc:dc + 1], scalar2=None, op0=ALU.mult))
            op("vector", lambda dc=dc, base=base: E["vector"].tensor_copy(
                out=halo[:, dc * 3:(dc + 1) * 3],
                in_=P1[:, base + 513: base + 516]),
                incs=([(s_dve, 1)] if dc == NDC - 1 else []))

        if KPH < 3:
            continue
        pe_claim("ps0")
        pe_claim("ps1")
        wt("tensor", s_act, v_conv_act)
        for kc in range(NKC):
            op("tensor", lambda kc=kc: E["tensor"].matmul(
                ps[0][:], xpwT[:, kc * 160: kc * 160 + 128],
                P2[:, kc * 512:(kc + 1) * 512],
                start=(kc == 0), stop=(kc == NKC - 1), skip_group_check=True),
                incs=([(s_pe, 1)] if kc == NKC - 1 else []))
            op("tensor", lambda kc=kc: E["tensor"].matmul(
                ps[1][0:32, :], xpwT[:, kc * 160 + 128: (kc + 1) * 160],
                P2[:, kc * 512:(kc + 1) * 512],
                start=(kc == 0), stop=(kc == NKC - 1), skip_group_check=True),
                incs=([(s_pe, 1)] if kc == NKC - 1 else []))
        v_xproj = s_pe.v
        wt("vector", s_pe, v_xproj)
        wait_list("vector", free_dtT)
        op("vector", lambda: E["vector"].tensor_copy(out=dtT[:], in_=ps[0][:]),
           incs=[(s_dve, 1)])
        bank_consumed("ps0", s_dve)
        v_dtT = s_dve.v
        wait_list("vector", free_bcT)
        op("vector", lambda: E["vector"].tensor_copy(
            out=bcT[:], in_=ps[1][0:32, :]), incs=[(s_dve, 1)])
        bank_consumed("ps1", s_dve)
        v_bcT = s_dve.v

        wt("sync", s_dve, v_bcT)
        op("sync", lambda c=c: E["sync"].dma_start(
            out=bc_d[:, c * TC:(c + 1) * TC], in_=bcT[:]), incs=[(s_st, 16)])
        v_bc_store = s_st.v
        free_bcT = [(s_st, v_bc_store)]

        if KPH < 4:
            continue
        v_dt_ready = 0
        for dc in range(NDC):
            b = f"ps{2 + dc % 2}"
            wt("tensor", s_dve, v_dtT)
            pe_claim(b)
            op("tensor", lambda dc=dc: E["tensor"].matmul(
                ps[2 + dc % 2][:], dtpwT[:, dc * 128:(dc + 1) * 128], dtT[:],
                start=True, stop=True), incs=[(s_pe, 1)])
            v_mm = s_pe.v
            psd = ps[2 + dc % 2]
            wt("scalar", s_pe, v_mm)
            wait_list("scalar", free_spl)
            op("scalar", lambda psd=psd, dc=dc: E["scalar"].activation(
                out=spl[:, 0:512], in_=psd[:], func=AF.Relu,
                bias=db[:, dc:dc + 1]))
            op("scalar", lambda psd=psd, dc=dc: E["scalar"].activation(
                out=spl[:, 512:1024], in_=psd[:], func=AF.Abs,
                bias=db[:, dc:dc + 1]), incs=[(s_act, 1)])
            bank_consumed(b, s_act)
            op("scalar", lambda: E["scalar"].activation(
                out=spl[:, 512:1024], in_=spl[:, 512:1024], func=AF.Exp,
                scale=np.float32(-1.0)))
            op("scalar", lambda: E["scalar"].activation(
                out=spl[:, 512:1024], in_=spl[:, 512:1024], func=AF.Ln,
                bias=np.float32(1.0)), incs=[(s_act, 1)])
            wt("vector", s_act, s_act.v)
            op("vector", lambda dc=dc: E["vector"].tensor_add(
                P3[:, dc * 512:(dc + 1) * 512], spl[:, 0:512],
                spl[:, 512:1024]), incs=[(s_dve, 1)])
            free_spl = [(s_dve, s_dve.v)]
            v_dt_ready = s_dve.v

        for dc in range(NDC):
            op("vector", lambda dc=dc: E["vector"].tensor_mul(
                P2[:, dc * 512:(dc + 1) * 512],
                P3[:, dc * 512:(dc + 1) * 512],
                P2[:, dc * 512:(dc + 1) * 512]),
                incs=([(s_dve, 1)] if dc == NDC - 1 else []))
        v_dx = s_dve.v

        if KPH < 5:
            continue
        it = 0
        v_scan_act = 0
        for n in range(NST):
            sl = n % 2
            wait_list("sync", free_brep[sl])
            wt("sync", s_st, v_bc_store)
            op("sync", lambda n=n, sl=sl, c=c: E["sync"].dma_start(
                out=brep[:, sl * 1024: sl * 1024 + 512],
                in_=bc_d[n:n + 1, c * TC:(c + 1) * TC].partition_broadcast(128)),
                incs=[(s_ld, 16)])
            op("sync", lambda n=n, sl=sl, c=c: E["sync"].dma_start(
                out=brep[:, sl * 1024 + 512: (sl + 1) * 1024],
                in_=bc_d[16 + n:17 + n, c * TC:(c + 1) * TC].partition_broadcast(128)),
                incs=[(s_ld, 16)])
            v_brep = s_ld.v
            free_brep[sl] = []
            last_b = last_c = 0
            for dc in range(NDC):
                asl = it % 2
                hsl = it % 2
                ysl = it % 2
                wait_list("scalar", free_a[asl])
                if it == 0:
                    wt("scalar", s_dve, v_dt_ready)
                op("scalar", lambda dc=dc, n=n, asl=asl:
                   E["scalar"].activation(
                    out=a2[:, asl * 512:(asl + 1) * 512],
                    in_=P3[:, dc * 512:(dc + 1) * 512], func=AF.Exp,
                    scale=aneg[:, dc * NST + n: dc * NST + n + 1]),
                    incs=[(s_act, 1)])
                free_a[asl] = []
                v_a = s_act.v
                v_scan_act = v_a
                if dc == 0:
                    wt("vector", s_ld, v_brep)
                op("vector", lambda dc=dc, sl=sl: E["vector"].tensor_mul(
                    bbuf[:], P2[:, dc * 512:(dc + 1) * 512],
                    brep[:, sl * 1024: sl * 1024 + 512]))
                wt("vector", s_act, v_a)
                wait_list("vector", free_h[hsl])
                op("vector", lambda dc=dc, n=n, asl=asl, hsl=hsl:
                   E["vector"].tensor_tensor_scan(
                    out=h2[:, hsl * 512:(hsl + 1) * 512],
                    data0=a2[:, asl * 512:(asl + 1) * 512], data1=bbuf[:],
                    initial=state[:, dc * NST + n: dc * NST + n + 1],
                    op0=ALU.mult, op1=ALU.add), incs=[(s_dve, 1)])
                free_a[asl] = [(s_dve, s_dve.v)]
                free_h[hsl] = []
                v_scan = s_dve.v
                last_b = s_dve.v
                wt("gpsimd", s_dve, v_scan)
                wait_list("gpsimd", free_yt[ysl])
                op("gpsimd", lambda sl=sl, hsl=hsl, ysl=ysl:
                   E["gpsimd"].tensor_mul(
                    ytb[:, ysl * 512:(ysl + 1) * 512],
                    h2[:, hsl * 512:(hsl + 1) * 512],
                    brep[:, sl * 1024 + 512: (sl + 1) * 1024]))
                free_yt[ysl] = []
                op("gpsimd", lambda dc=dc, n=n, hsl=hsl:
                   E["gpsimd"].tensor_copy(
                    out=state[:, dc * NST + n: dc * NST + n + 1],
                    in_=h2[:, hsl * 512 + 511: hsl * 512 + 512]),
                    incs=[(s_pool, 1)])
                free_h[hsl] = [(s_pool, s_pool.v)]
                last_c = s_pool.v
                if dc % 4 == 3:
                    op("gpsimd", lambda dc=dc, ysl=ysl: E["gpsimd"].tensor_add(
                        P4[:, dc * 512:(dc + 1) * 512],
                        P4[:, dc * 512:(dc + 1) * 512],
                        ytb[:, ysl * 512:(ysl + 1) * 512]), incs=[(s_pool, 1)])
                    free_yt[ysl] = [(s_pool, s_pool.v)]
                    last_c = s_pool.v
                else:
                    wt("vector", s_pool, s_pool.v)
                    op("vector", lambda dc=dc, ysl=ysl: E["vector"].tensor_add(
                        P4[:, dc * 512:(dc + 1) * 512],
                        P4[:, dc * 512:(dc + 1) * 512],
                        ytb[:, ysl * 512:(ysl + 1) * 512]), incs=[(s_dve, 1)])
                    free_yt[ysl] = [(s_dve, s_dve.v)]
                    last_b = s_dve.v
                it += 1
            free_brep[sl] = [(s_dve, last_b), (s_pool, last_c)]
        v_y_done_v = s_dve.v
        v_y_done_p = s_pool.v

        if KPH < 6:
            continue
        v_x2_list = [0] * NTG
        for tg in range(NTG):
            wait_list("sync", free_xin)
            rows = c * TC + tg * 128
            op("sync", lambda rows=rows: E["sync"].dma_start(
                out=xin[:], in_=x_h[rows:rows + 128, :]), incs=[(s_ld, 16)])
            v_x = s_ld.v
            for dc in range(NDC):
                b = f"pt{dc % 2}"
                wt("tensor", s_dve, v_y_done_v)
                if tg == 0 and dc == 0:
                    wt("tensor", s_pool, v_y_done_p)
                pe_claim(b)
                op("tensor", lambda dc=dc, tg=tg: E["tensor"].transpose(
                    pt[dc % 2],
                    P4[:, dc * 512 + tg * 128: dc * 512 + (tg + 1) * 128],
                    iden[:]), incs=[(s_pe, 1)])
                wt("vector", s_pe, s_pe.v)
                if dc == 0:
                    wt("vector", s_ld, v_x)
                    if tg == 0:
                        wt("vector", s_act, v_scan_act)
                op("vector", lambda dc=dc, tg=tg: E["vector"].tensor_add(
                    P3[:, tg * 2048 + dc * 128: tg * 2048 + (dc + 1) * 128],
                    pt[dc % 2], xin[:, dc * 128:(dc + 1) * 128]),
                    incs=[(s_dve, 1)])
                bank_consumed(b, s_dve)
            free_xin = [(s_dve, s_dve.v)]
            v_x2_list[tg] = s_dve.v
            wt("sync", s_dve, v_x2_list[tg])
            op("sync", lambda tg=tg, c=c: E["sync"].dma_start(
                out=x2_d[c * TC + tg * 128: c * TC + (tg + 1) * 128, :],
                in_=P3[:, tg * 2048:(tg + 1) * 2048]), incs=[(s_st, 16)])
            v_x2_spill[c * NTG + tg] = s_st.v
        v_yT_done = s_pe.v

        v_z = 0
        v_ln2_act = 0
        for tg in range(NTG):
            wt("scalar", s_dve, v_x2_list[tg])
            wt("scalar", s_dve, v_sqred[0])
            op("scalar", lambda tg=tg: E["scalar"].activation(
                out=P5[:, 0:2048], in_=P3[:, tg * 2048:(tg + 1) * 2048],
                func=AF.Square), incs=[(s_act, 1)])
            v_sq2 = s_act.v
            op("vector", lambda tg=tg: E["vector"].reduce_sum(
                stats[:, 0:1], P3[:, tg * 2048:(tg + 1) * 2048], axis=AX),
                incs=[(s_dve, 1)])
            wt("vector", s_dve, s_dve.v)
            op("vector", lambda: E["vector"].tensor_scalar(
                out=stats[:, 0:1], in0=stats[:, 0:1], scalar1=1.0 / D,
                scalar2=None, op0=ALU.mult))
            wt("vector", s_act, v_sq2)
            op("vector", lambda: E["vector"].reduce_sum(
                stats[:, 4:5], P5[:, 0:2048], axis=AX), incs=[(s_dve, 1)])
            v_sqred[0] = s_dve.v
            wt("vector", s_dve, s_dve.v)
            op("vector", lambda: E["vector"].tensor_scalar(
                out=stats[:, 4:5], in0=stats[:, 4:5], scalar1=1.0 / D,
                scalar2=None, op0=ALU.mult))
            op("vector", lambda: E["vector"].tensor_tensor(
                out=stats[:, 8:9], in0=stats[:, 0:1], in1=stats[:, 0:1],
                op=ALU.mult), incs=[(s_dve, 1)])
            wt("vector", s_dve, s_dve.v)
            op("vector", lambda: E["vector"].tensor_tensor(
                out=stats[:, 4:5], in0=stats[:, 4:5], in1=stats[:, 8:9],
                op=ALU.subtract), incs=[(s_dve, 1)])
            wt("scalar", s_dve, s_dve.v)
            op("scalar", lambda: E["scalar"].activation(
                out=stats[:, 8:9], in_=stats[:, 4:5], func=AF.Ln, bias=np.float32(EPS)))
            op("scalar", lambda: E["scalar"].activation(
                out=stats[:, 8:9], in_=stats[:, 8:9], func=AF.Exp, scale=np.float32(-0.5)),
                incs=[(s_act, 1)])
            v_ln2_act = s_act.v
            wt("vector", s_act, s_act.v)
            if tg == 0:
                wt("vector", s_pe, v_yT_done)
            op("vector", lambda tg=tg: E["vector"].tensor_scalar(
                out=P4[:, tg * 2048:(tg + 1) * 2048],
                in0=P3[:, tg * 2048:(tg + 1) * 2048],
                scalar1=stats[:, 0:1], scalar2=stats[:, 8:9],
                op0=ALU.subtract, op1=ALU.mult), incs=[(s_dve, 1)])
            v_z = s_dve.v
            wt("sync", s_dve, v_z)
            op("sync", lambda tg=tg, c=c: E["sync"].dma_start(
                out=ztok_d[c * TC + tg * 128: c * TC + (tg + 1) * 128, :],
                in_=P4[:, tg * 2048:(tg + 1) * 2048]), incs=[(s_st, 16)])

        if KPH < 7:
            continue
        v_spill_last = 0
        for tg in range(NTG):
            for dc in range(NDC):
                b = f"pt{dc % 2}"
                wt("tensor", s_dve, v_z)
                pe_claim(b)
                op("tensor", lambda tg=tg, dc=dc: E["tensor"].transpose(
                    pt[dc % 2],
                    P4[:, tg * 2048 + dc * 128: tg * 2048 + (dc + 1) * 128],
                    iden[:]), incs=[(s_pe, 1)])
                wt("vector", s_pe, s_pe.v)
                if tg == 0 and dc == 0:
                    wt("vector", s_st, v_x2_spill[c * NTG + NTG - 1])
                    wt("vector", s_act, v_ln2_act)
                op("vector", lambda tg=tg, dc=dc: E["vector"].tensor_copy(
                    out=P3[:, dc * 512 + tg * 128: dc * 512 + (tg + 1) * 128],
                    in_=pt[dc % 2]), incs=[(s_dve, 1)])
                bank_consumed(b, s_dve)
        v_zf = s_dve.v
        for dc in range(NDC):
            zsl = dc % 2
            wait_list("vector", free_zt2[zsl])
            op("vector", lambda dc=dc, zsl=zsl: E["vector"].tensor_copy(
                out=zt2[:, zsl * 512:(zsl + 1) * 512],
                in_=P3[:, dc * 512:(dc + 1) * 512]), incs=[(s_dve, 1)])
            free_zt2[zsl] = []
            wt("sync", s_dve, s_dve.v)
            op("sync", lambda dc=dc, zsl=zsl, c=c: E["sync"].dma_start(
                out=zb_d[dc, :, c * TC:(c + 1) * TC],
                in_=zt2[:, zsl * 512:(zsl + 1) * 512]), incs=[(s_st, 16)])
            free_zt2[zsl] = [(s_st, s_st.v)]
        v_zb_spill = s_st.v

        pe_claim("ps1")
        wt("tensor", s_dve, v_zf)
        for kc in range(NKC):
            op("tensor", lambda kc=kc: E["tensor"].matmul(
                ps[1][0:32, :], gwT[:, kc * 32:(kc + 1) * 32],
                P3[:, kc * 512:(kc + 1) * 512],
                start=(kc == 0), stop=(kc == NKC - 1)),
                incs=([(s_pe, 1)] if kc == NKC - 1 else []))
        v_gate = s_pe.v
        wt("scalar", s_pe, v_gate)
        op("scalar", lambda: E["scalar"].activation(
            out=scrT, in_=ps[1][0:32, :], func=AF.Sigmoid),
            incs=[(s_act, 1)])
        bank_consumed("ps1", s_act)
        v_scr = s_act.v
        for tg in range(NTG):
            tt = c * NTG + tg
            b = f"pt{tg % 2}"
            wt("tensor", s_act, v_scr)
            pe_claim(b)
            op("tensor", lambda tg=tg: E["tensor"].transpose(
                pt[tg % 2][:, 0:32], scrT[:, tg * 128:(tg + 1) * 128],
                iden[0:32, 0:32]), incs=[(s_pe, 1)])
            wt("vector", s_pe, s_pe.v)
            if c == 0 and tg == 0:
                wt("vector", s_ld, v_cst)
            op("vector", lambda tg=tg: E["vector"].tensor_copy(
                out=scr_tm[:, 0:32],
                in_=pt[tg % 2][:, 0:32]), incs=[(s_dve, 1)])
            bank_consumed(b, s_dve)
            wt("vector", s_dve, s_dve.v)
            op("vector", lambda tg=tg: E["vector"].max(
                out=m8t[:, 0:8],
                in_=scr_tm[:, 0:32]), incs=[(s_dve, 1)])
            wt("vector", s_dve, s_dve.v)
            op("vector", lambda tg=tg: E["vector"].tensor_scalar(
                out=selb[:], in0=scr_tm[:, 0:32],
                scalar1=m8t[:, 5:6],
                scalar2=None, op0=ALU.is_ge), incs=[(s_dve, 1)])
            v_selb = s_dve.v
            wt("vector", s_dve, s_dve.v)
            op("vector", lambda tg=tg: E["vector"].tensor_mul(
                cmbb[:], selb[:], scr_tm[:, 0:32]),
                incs=[(s_dve, 1)])
            wt("vector", s_dve, s_dve.v)
            op("vector", lambda tg=tg: E["vector"].reduce_sum(
                stats[:, 8:9], cmbb[:], axis=AX), incs=[(s_dve, 1)])
            wt("vector", s_dve, s_dve.v)
            op("vector", lambda tg=tg: E["vector"].tensor_scalar(
                out=stats[:, 8:9], in0=stats[:, 8:9],
                scalar1=1e-20, scalar2=None, op0=ALU.add), incs=[(s_dve, 1)])
            wt("vector", s_dve, s_dve.v)
            op("vector", lambda tg=tg: E["vector"].reciprocal(
                out=stats[:, 12:13], in_=stats[:, 8:9]),
                incs=[(s_dve, 1)])
            wt("vector", s_dve, s_dve.v)
            op("vector", lambda tg=tg: E["vector"].tensor_scalar(
                out=cmbb[:], in0=cmbb[:], scalar1=stats[:, 12:13],
                scalar2=None, op0=ALU.mult), incs=[(s_dve, 1)])
            v_cmbb = s_dve.v
            pe_claim("pt0")
            pe_claim("pt1")
            wt("tensor", s_dve, v_selb)
            op("tensor", lambda: E["tensor"].matmul(
                pt[0][:, 0:32], triu[:], selb[:],
                start=True, stop=False, skip_group_check=True))
            op("tensor", lambda: E["tensor"].matmul(
                pt[0][:, 0:32], ones1[:], basev[:],
                start=False, stop=True, skip_group_check=True),
                incs=[(s_pe, 1)])
            op("tensor", lambda: E["tensor"].matmul(
                pt[1][0:1, 0:32], triu[:, 127:128], selb[:],
                start=True, stop=True, skip_group_check=True),
                incs=[(s_pe, 1)])
            v_pref = s_pe.v
            wt("vector", s_pe, v_pref)
            op("vector", lambda: E["vector"].tensor_add(
                basev[:], basev[:], pt[1][0:1, 0:32]), incs=[(s_dve, 1)])
            bank_consumed("pt1", s_dve)
            v_base_upd = s_dve.v
            op("vector", lambda: E["vector"].tensor_tensor_scan(
                out=ksel[:], data0=onesf, data1=selb[:], initial=0.0,
                op0=ALU.mult, op1=ALU.add), incs=[(s_dve, 1)])
            wt("vector", s_dve, s_dve.v)
            op("vector", lambda: E["vector"].tensor_mul(
                ksel[:], ksel[:], selb[:]), incs=[(s_dve, 1)])
            wt("vector", s_dve, s_dve.v)
            op("vector", lambda: E["vector"].tensor_scalar(
                out=notsel[:], in0=selb[:], scalar1=-1.0, scalar2=1.0,
                op0=ALU.mult, op1=ALU.add))
            op("vector", lambda: E["vector"].tensor_tensor(
                out=sl1[:], in0=pt[0][:, 0:32], in1=eoffm1, op=ALU.add))
            op("vector", lambda: E["vector"].tensor_scalar(
                out=msk[:], in0=pt[0][:, 0:32], scalar1=CAP + 0.5,
                scalar2=None, op0=ALU.is_ge), incs=[(s_dve, 1)])
            bank_consumed("pt0", s_dve)
            wt("vector", s_dve, s_dve.v)
            op("vector", lambda: E["vector"].scalar_tensor_tensor(
                out=sl2[:], in0=notsel[:], scalar=BIGF, in1=sl1[:],
                op0=ALU.mult, op1=ALU.add), incs=[(s_dve, 1)])
            wt("vector", s_dve, s_dve.v)
            op("vector", lambda: E["vector"].scalar_tensor_tensor(
                out=sl1[:], in0=msk[:], scalar=BIGF, in1=sl2[:],
                op0=ALU.mult, op1=ALU.add), incs=[(s_dve, 1)])
            wt("vector", s_dve, s_dve.v)
            op("vector", lambda: E["vector"].tensor_scalar(
                out=sl2[:], in0=sl1[:], scalar1=float(NSLOT), scalar2=None,
                op0=ALU.min), incs=[(s_dve, 1)])
            wt("vector", s_dve, s_dve.v)
            op("vector", lambda tt=tt: E["vector"].tensor_copy(
                out=sixs[:, tt * 32:(tt + 1) * 32], in_=sl2[:]))
            op("vector", lambda tt=tt: E["vector"].tensor_scalar(
                out=tcolv[:, tt:tt + 1], in0=iotac,
                scalar1=float(tt * 128), scalar2=None, op0=ALU.add))
            for k in range(6):
                op("vector", lambda k=k: E["vector"].tensor_scalar(
                    out=P5[:, k * 32:(k + 1) * 32], in0=ksel[:],
                    scalar1=float(k + 1), scalar2=None, op0=ALU.is_equal),
                    incs=([(s_dve, 1)] if k == 5 else []))
            wt("vector", s_dve, s_dve.v)
            for k in range(6):
                op("vector", lambda k=k: E["vector"].tensor_mul(
                    P5[:, 192 + k * 32: 192 + (k + 1) * 32],
                    P5[:, k * 32:(k + 1) * 32], sl2[:]))
                op("vector", lambda k=k: E["vector"].tensor_mul(
                    P5[:, 384 + k * 32: 384 + (k + 1) * 32],
                    P5[:, k * 32:(k + 1) * 32], cmbb[:]),
                    incs=([(s_dve, 1)] if k == 5 else []))
            wt("vector", s_dve, s_dve.v)
            for k in range(6):
                op("vector", lambda k=k: E["vector"].reduce_sum(
                    tk6[:, k:k + 1], P5[:, 192 + k * 32: 192 + (k + 1) * 32],
                    axis=AX))
                op("vector", lambda k=k: E["vector"].reduce_sum(
                    ck6[:, k:k + 1], P5[:, 384 + k * 32: 384 + (k + 1) * 32],
                    axis=AX), incs=([(s_dve, 1)] if k == 5 else []))
            wt("vector", s_dve, s_dve.v)
            op("vector", lambda: E["vector"].engine_nop(), incs=[(s_dve, 1)])
            v_tk = s_dve.v
            wt("sync", s_dve, v_tk)
            op("sync", lambda tt=tt: E["sync"].dma_start(
                out=slot6_d[tt * 128:(tt + 1) * 128, 0:6],
                in_=tk6[:, 0:6]), incs=[(s_st, 16)])
            op("sync", lambda tt=tt: E["sync"].dma_start(
                out=combc_d[tt * 128:(tt + 1) * 128, 0:6],
                in_=ck6[:, 0:6]), incs=[(s_st, 16)])
        v_tables = s_st.v
        free_spl = [(s_st, v_tables)]
        v_sqred[0] = v_tk

        prev_end = [(s_dve, s_dve.v), (s_act, s_act.v), (s_pe, s_pe.v),
                    (s_pool, s_pool.v), (s_st, s_st.v), (s_ld, s_ld.v)]

    ctxM.close()
    ctxR = ExitStack()
    ctxF = ExitStack()
    sbtF = lambda name, shape, dt=F32: ctxF.enter_context(
        nc.sbuf_tensor("sb_" + name, shape, dt))
    shtm = ctxR.enter_context(nc.sbuf_tensor("sb_shtm", [128, 16 * 2048], BF16))
    actv = sbtF("act", [128, 2 * NFC * 1024], BF16)
    dns = sbtF("dns", [128, 2 * 12 * 512])
    sG = sbtF("sG", [128, 1024])
    rstg = sbtF("rstg", [128, 1024], BF16)
    ixf = sbtF("ixf", [128, 8])
    ixi = sbtF("ixi", [128, 8], mybir.dt.int32)
    ctxF0 = ExitStack()
    zbh = ctxF0.enter_context(nc.sbuf_tensor("sb_zbh", [128, NKC * 1024], BF16))
    gus = ctxF0.enter_context(nc.sbuf_tensor("sb_gus", [128, 2048], F32))

    v_sixs_done = s_dve.v
    v_F_dve = 0
    if KPH >= 8:
        for eng in ENGS:
            for sem, val in prev_end:
                wt(eng, sem, val)
        free_zbh = []
        free_gus = [[], []]
        free_act = [[], []]
        free_sG = []
        free_dns = [[], []]
        free_rstg = [[], []]
        free_zer = [[]]
        free_zeT = [[], []]
        gu_seq = [0]

        def gu_dma(e, fc, hf, mat):
            slot = gu_seq[0] % 2
            gu_seq[0] += 1
            wait_list("sync", free_gus[slot])
            if e < NEXP:
                src_h = wg_h if mat == 0 else wu_h
                src = src_h[e, :, fc * 128:(fc + 1) * 128].rearrange(
                    "(c p) f -> p c f", p=128)[:, hf * 8:(hf + 1) * 8, :]
            else:
                src_h = sg_h if mat == 0 else su_h
                sx = e - NEXP
                src = src_h[:, sx * FEXP + fc * 128: sx * FEXP + (fc + 1) * 128
                            ].rearrange("(c p) f -> p c f", p=128)[
                                :, hf * 8:(hf + 1) * 8, :]
            col = slot * 1024
            dst = gus[:, col:col + 1024].rearrange("p (c f) -> p c f", f=128)
            op("sync", lambda src=src, dst=dst: E["sync"].dma_start(
                out=dst, in_=src), incs=[(s_gu[slot], 16)])
            free_gus[slot] = []
            return slot, s_gu[slot].v

        for h in range(2):
            wait_list("sync", free_zbh)
            wt("sync", s_st, v_zb_spill)
            for kc in range(NKC):
                op("sync", lambda kc=kc, h=h: E["sync"].dma_start(
                    out=zbh[:, kc * 1024:(kc + 1) * 1024],
                    in_=zb_d[kc, :, h * TH:(h + 1) * TH]), incs=[(s_zb, 16)])
            v_zbh = s_zb.v
            free_zbh = []
            for ei, e in enumerate((NEXP, NEXP + 1)):
                for fc in range(NFC):
                    pe_claim("ps0")
                    pe_claim("ps1")
                    pe_claim("ps2")
                    pe_claim("ps3")
                    for mat in (0, 1):
                        pst0 = ps[mat * 2]
                        pst1 = ps[mat * 2 + 1]
                        for hf in (0, 1):
                            slot, vld = gu_dma(e, fc, hf, mat)
                            wt("tensor", s_gu[slot], vld)
                            if hf == 0 and fc == 0 and mat == 0 and ei == 0:
                                wt("tensor", s_zb, v_zbh)
                            for kc in range(hf * 8, hf * 8 + 8):
                                col = slot * 1024 + (kc % 8) * 128
                                lhsT = bhalf(gus[:], col, 128)
                                for sub in range(NSB):
                                    pstt = pst0 if sub == 0 else pst1
                                    op("tensor",
                                       lambda pstt=pstt, lhsT=lhsT, kc=kc, sub=sub:
                                       E["tensor"].matmul(
                                        pstt[:], lhsT,
                                        zbh[:, kc * 1024 + sub * 512:
                                            kc * 1024 + (sub + 1) * 512],
                                        start=(kc == 0), stop=(kc == NKC - 1),
                                        skip_group_check=True),
                                        incs=([(s_pe, 1)]
                                              if (kc % 8 == 7 and sub == NSB - 1)
                                              else []))
                            free_gus[slot] = [(s_pe, s_pe.v)]
                    v_gu = s_pe.v
                    wt("scalar", s_pe, v_gu)
                    wait_list("scalar", free_sG)
                    op("scalar", lambda: E["scalar"].activation(
                        out=sG[:, 0:512], in_=ps[0][:], func=AF.Silu))
                    op("scalar", lambda: E["scalar"].activation(
                        out=sG[:, 512:1024], in_=ps[1][:], func=AF.Silu),
                        incs=[(s_act, 1)])
                    bank_consumed("ps0", s_act)
                    bank_consumed("ps1", s_act)
                    free_sG = []
                    v_sg = s_act.v
                    acol = ei * NFC * 1024 + fc * 1024
                    wt("vector", s_act, v_sg)
                    wait_list("vector", free_act[0])
                    wt("vector", s_pe, v_gu)
                    op("vector", lambda acol=acol:
                       E["vector"].tensor_mul(
                        actv[:, acol:acol + 512],
                        sG[:, 0:512], ps[2][:]))
                    op("vector", lambda acol=acol:
                       E["vector"].tensor_mul(
                        actv[:, acol + 512:acol + 1024],
                        sG[:, 512:1024], ps[3][:]), incs=[(s_dve, 1)])
                    bank_consumed("ps2", s_dve)
                    bank_consumed("ps3", s_dve)
                    free_sG = [(s_dve, s_dve.v)]
                    free_act[0] = []
            v_act_sh = s_dve.v
            for dgrp in range(4):
                dslot = dgrp % 2
                wait_list("sync", free_dns[dslot])
                op("sync", lambda dgrp=dgrp, dslot=dslot:
                   E["sync"].dma_start(
                    out=dns[:, dslot * 6144:(dslot + 1) * 6144].rearrange(
                        "p (c d) -> p c d", d=512),
                    in_=sd_h[:, dgrp * 512:(dgrp + 1) * 512].rearrange(
                        "(c p) d -> p c d", p=128)), incs=[(s_dn, 16)])
                v_dnl = s_dn.v
                free_dns[dslot] = []
                for tloc in range(8):
                    tt = h * 8 + tloc
                    b = f"ps{4 + (tloc % 2)}"
                    pst = ps[4 + (tloc % 2)]
                    pe_claim(b)
                    if tloc == 0:
                        wt("tensor", s_dve, v_act_sh)
                        wt("tensor", s_dn, v_dnl)
                    nmm = 0
                    for fkc in range(12):
                        acol = (fkc // 6) * 6144 + (fkc % 6) * 1024 + \
                            (tloc // 4) * 512 + (tloc % 4) * 128
                        nmm += 1
                        op("tensor",
                           lambda pst=pst, acol=acol, dslot=dslot, fkc=fkc,
                           first=(nmm == 1), last=(nmm == 12):
                           E["tensor"].matmul(
                            pst[:], actv[:, acol:acol + 128],
                            bhalf(dns[:], dslot * 6144 + fkc * 512, 512),
                            start=first, stop=last,
                            skip_group_check=True),
                            incs=([(s_pe, 1)] if nmm == 12 else []))
                    v_dnmm = s_pe.v
                    wt("vector", s_pe, v_dnmm)
                    op("vector", lambda tt=tt, dgrp=dgrp, pst=pst:
                       E["vector"].tensor_copy(
                        out=shtm[:, tt * 2048 + dgrp * 512:
                                 tt * 2048 + (dgrp + 1) * 512],
                        in_=pst[:]), incs=[(s_dve, 1)])
                    bank_consumed(b, s_dve)
                free_dns[dslot] = [(s_pe, s_pe.v)]
            free_act[0] = [(s_pe, s_pe.v)]
            free_act[1] = [(s_pe, s_pe.v)]
            free_zbh = [(s_pe, s_pe.v)]

        v_F0_pe = s_pe.v
        for eng in ("gpsimd", "sync", "vector", "scalar", "tensor"):
            wt(eng, s_pe, v_F0_pe)
        ctxF0.close()
        ctxF1 = ExitStack()
        zeT = [ctxF1.enter_context(nc.sbuf_tensor(f"sb_zeT{i}", [128, NKC * 512], BF16))
               for i in range(2)]
        gslab = ctxF1.enter_context(nc.sbuf_tensor("sb_gslab", [128, 3 * 2 * 384], F32))
        zer = [ctxF1.enter_context(nc.sbuf_tensor("sb_zer0", [128, 2048], F32))]
        a32 = ctxF1.enter_context(nc.sbuf_tensor("sb_a32", [128, 2 * 384], F32))
        free_gsl = [[] for _ in range(3)]
        free_a32 = [[], []]
        v_scat = 0
        v_ixc_prev = [0]
        for e in range(NEXP):
            wt("gpsimd", s_dve, v_sixs_done)
            if e == 0:
                wt("gpsimd", s_st, v_prefill)
            else:
                wt("gpsimd", s_ix, s_ix.v)
            for tt in range(T // 128):
                op("gpsimd", lambda tt=tt, e=e:
                   E["gpsimd"].indirect_dma_start(
                    out=tok_d,
                    out_offset=bass.IndirectOffsetOnAxis(
                        ap=sixs[:, tt * 32 + e: tt * 32 + e + 1], axis=0),
                    in_=tcolv[:, tt:tt + 1], in_offset=None),
                    incs=[(s_gsc, 16)])
            v_scat = s_gsc.v
            wt("sync", s_gsc, v_scat)
            wt("sync", s_dve, v_ixc_prev[0])
            for j in range(4):
                op("sync", lambda e=e, j=j: E["sync"].dma_start(
                    out=ixf[:, j:j + 1],
                    in_=tok2d[e * 4 + j: e * 4 + j + 1, :].rearrange(
                        "a (p o) -> (a p) o", o=1)), incs=[(s_ix, 16)])
            v_ixl = s_ix.v
            wt("vector", s_ix, v_ixl)
            op("vector", lambda: E["vector"].tensor_copy(
                out=ixi[:, 0:4], in_=ixf[:, 0:4]), incs=[(s_dve, 1)])
            v_ixc = s_dve.v
            v_ixc_prev[0] = v_ixc
            esl = e % 2
            wait_list("gpsimd", free_zeT[esl])
            for j in range(4):
                zsl = 0
                wait_list("gpsimd", free_zer[zsl])
                wt("gpsimd", s_dve, v_ixc)
                op("gpsimd", lambda j=j, zsl=zsl: E["gpsimd"].indirect_dma_start(
                    out=zer[zsl][:], out_offset=None,
                    in_=ztok_d,
                    in_offset=bass.IndirectOffsetOnAxis(
                        ap=ixi[:, j:j + 1], axis=0)),
                    incs=[(s_gdma, 16)])
                v_gz = s_gdma.v
                free_zer[zsl] = []
                for kc in range(NKC):
                    bb = f"pt{kc % 2}"
                    pe_claim(bb)
                    wt("tensor", s_gdma, v_gz)
                    op("tensor", lambda kc=kc, zsl=zsl: E["tensor"].transpose(
                        pt[kc % 2], zer[zsl][:, kc * 128:(kc + 1) * 128],
                        iden[:]), incs=[(s_pe, 1)])
                    eng = "vector" if kc % 2 == 0 else "scalar"
                    wt(eng, s_pe, s_pe.v)
                    if eng == "vector":
                        op(eng, lambda kc=kc, j=j, esl=esl: E["vector"].tensor_copy(
                            out=zeT[esl][:, kc * 512 + j * 128:
                                         kc * 512 + (j + 1) * 128],
                            in_=pt[kc % 2]), incs=[(s_dve, 1)])
                        bank_consumed(bb, s_dve)
                    else:
                        op(eng, lambda kc=kc, j=j, esl=esl: E["scalar"].copy(
                            out=zeT[esl][:, kc * 512 + j * 128:
                                         kc * 512 + (j + 1) * 128],
                            in_=pt[kc % 2]), incs=[(s_act, 1)])
                        bank_consumed(bb, s_act)
                free_zer[zsl] = [(s_pe, s_pe.v)]
            v_zeT = [(s_dve, s_dve.v), (s_act, s_act.v)]
            aslot = e % 2
            ab = aslot * NFC * 512
            PSN = ["ps0", "ps1", "ps2", "ps3", "ps4", "ps5", "pt0", "pt1"]
            PSA = ps + ptf
            first_act = True
            gcnt = [0]
            gslm = {}
            for fh in (0, 1):
                for b in PSN:
                    pe_claim(b)
                for kcg in range(8):
                    for mat in (0, 1):
                        gsl = gcnt[0] % 3
                        gslm[(kcg, mat)] = gsl
                        gcnt[0] += 1
                        wait_list("sync", free_gsl[gsl])
                        src_h = wg_h if mat == 0 else wu_h
                        op("sync", lambda e=e, kcg=kcg, gsl=gsl, src_h=src_h, fh=fh:
                           E["sync"].dma_start(
                            out=gslab[:, gsl * 768:(gsl + 1) * 768].rearrange(
                                "p (c f) -> p c f", f=384),
                            in_=src_h[e, kcg * 256:(kcg + 1) * 256,
                                      fh * 384:(fh + 1) * 384].rearrange(
                                "(c p) f -> p c f", p=128)),
                            incs=[(s_gu[gsl], 16)])
                        free_gsl[gsl] = []
                    for mat in (0, 1):
                        gsl = gslm[(kcg, mat)]
                        wt("tensor", s_gu[gsl], s_gu[gsl].v)
                        if fh == 0 and kcg == 0 and mat == 0:
                            wait_list("tensor", v_zeT)
                        for kc in range(kcg * 2, (kcg + 1) * 2):
                            for j in range(4):
                                bank = PSA[j * 2 + mat]
                                op("tensor",
                                   lambda bank=bank, kc=kc, j=j, gsl=gsl, esl=esl:
                                   E["tensor"].matmul(
                                    bank[:, 0:384],
                                    zeT[esl][:, kc * 512 + j * 128:
                                             kc * 512 + (j + 1) * 128],
                                    bhalf(gslab[:],
                                          gsl * 768 + (kc % 2) * 384, 384),
                                    start=(kc == 0), stop=(kc == NKC - 1),
                                    skip_group_check=True),
                                    incs=([(s_pe, 1)]
                                          if (j == 3 and kc % 2 == 1) else []))
                        free_gsl[gsl] = [(s_pe, s_pe.v)]
                v_gu = s_pe.v
                if KDBG and e == NEXP - 1 and fh == 1:
                    wt("vector", s_pe, v_gu)
                    for bi in range(8):
                        op("vector", lambda bi=bi: E["vector"].tensor_copy(
                            out=dns[:, bi * 384:(bi + 1) * 384],
                            in_=PSA[bi][:, 0:384]),
                            incs=([(s_dve, 1)] if bi == 7 else []))
                    wt("sync", s_dve, s_dve.v)
                    op("sync", lambda: E["sync"].dma_start(
                        out=dbgp_d, in_=dns[:, 0:3072]), incs=[(s_st, 16)])
                    wt("vector", s_st, s_st.v)
                for j in (3, 2, 1, 0):
                    asl2 = j % 2
                    wt("scalar", s_pe, v_gu)
                    wait_list("scalar", free_sG)
                    op("scalar", lambda j=j: E["scalar"].activation(
                        out=sG[:, 0:384], in_=PSA[j * 2 + 0][:, 0:384],
                        func=AF.Silu), incs=[(s_act, 1)])
                    bank_consumed(PSN[j * 2 + 0], s_act)
                    free_sG = []
                    v_sg = s_act.v
                    wt("vector", s_act, v_sg)
                    wait_list("vector", free_a32[asl2])
                    wt("vector", s_pe, v_gu)
                    op("vector", lambda j=j, asl2=asl2: E["vector"].tensor_mul(
                        a32[:, asl2 * 384:(asl2 + 1) * 384],
                        sG[:, 0:384], PSA[j * 2 + 1][:, 0:384]),
                        incs=[(s_dve, 1)])
                    bank_consumed(PSN[j * 2 + 1], s_dve)
                    free_sG = [(s_dve, s_dve.v)]
                    free_a32[asl2] = []
                    v_a32 = s_dve.v
                    if KDBG and e == NEXP - 1 and fh == 1:
                        wt("sync", s_dve, v_a32)
                        op("sync", lambda j=j, asl2=asl2: E["sync"].dma_start(
                            out=dbgp_d[:, (4 + j) * 384:(5 + j) * 384],
                            in_=a32[:, asl2 * 384:(asl2 + 1) * 384]),
                            incs=[(s_st, 16)])
                        wt("vector", s_st, s_st.v)
                    for ft3 in range(3):
                        ft = fh * 3 + ft3
                        b = f"pt{ft3 % 2}"
                        pe_claim(b)
                        wt("tensor", s_dve, v_a32)
                        if first_act:
                            wait_list("tensor", free_act[aslot])
                            first_act = False
                        op("tensor", lambda ft3=ft3, asl2=asl2:
                           E["tensor"].transpose(
                            pt[ft3 % 2],
                            a32[:, asl2 * 384 + ft3 * 128:
                                asl2 * 384 + (ft3 + 1) * 128],
                            iden[:]), incs=[(s_pe, 1)])
                        eng = "vector" if ft3 % 2 == 0 else "scalar"
                        wt(eng, s_pe, s_pe.v)
                        if eng == "vector":
                            op(eng, lambda ft=ft, j=j, ab=ab, ft3=ft3:
                               E["vector"].tensor_copy(
                                out=actv[:, ab + ft * 512 + j * 128:
                                         ab + ft * 512 + (j + 1) * 128],
                                in_=pt[ft3 % 2]), incs=[(s_dve, 1)])
                            bank_consumed(b, s_dve)
                        else:
                            op(eng, lambda ft=ft, j=j, ab=ab, ft3=ft3:
                               E["scalar"].copy(
                                out=actv[:, ab + ft * 512 + j * 128:
                                         ab + ft * 512 + (j + 1) * 128],
                                in_=pt[ft3 % 2]), incs=[(s_act, 1)])
                            bank_consumed(b, s_act)
                    free_a32[asl2] = [(s_pe, s_pe.v)]
            op("vector", lambda: E["vector"].engine_nop(), incs=[(s_dve, 1)])
            wt("vector", s_act, s_act.v)
            op("vector", lambda: E["vector"].engine_nop(), incs=[(s_dve, 1)])
            v_act_e = s_dve.v
            for dgrp in range(4):
                dslot = dgrp % 2
                wait_list("sync", free_dns[dslot])
                op("sync", lambda e=e, dgrp=dgrp, dslot=dslot:
                   E["sync"].dma_start(
                    out=dns[:, dslot * 6144: dslot * 6144 + 3072].rearrange(
                        "p (c d) -> p c d", d=512),
                    in_=wd_h[e, :, dgrp * 512:(dgrp + 1) * 512].rearrange(
                        "(c p) d -> p c d", p=128)), incs=[(s_dn, 16)])
                v_dnl = s_dn.v
                free_dns[dslot] = []
                for j in range(4):
                    b = f"ps{4 + (j % 2)}"
                    pst = ps[4 + (j % 2)]
                    pe_claim(b)
                    if j == 0:
                        wt("tensor", s_dve, v_act_e)
                        wt("tensor", s_dn, v_dnl)
                    nmm = 0
                    for fkc in range(NFC):
                        nmm += 1
                        op("tensor",
                           lambda pst=pst, fkc=fkc, j=j, dslot=dslot,
                           ab=ab, first=(nmm == 1), last=(nmm == NFC):
                           E["tensor"].matmul(
                            pst[:],
                            actv[:, ab + fkc * 512 + j * 128:
                                 ab + fkc * 512 + (j + 1) * 128],
                            bhalf(dns[:], dslot * 6144 + fkc * 512, 512),
                            start=first, stop=last,
                            skip_group_check=True),
                            incs=([(s_pe, 1)] if nmm == NFC else []))
                    v_dnmm = s_pe.v
                    rsl = (dgrp * 4 + j) % 2
                    wt("vector", s_pe, v_dnmm)
                    wait_list("vector", free_rstg[rsl])
                    free_rstg[rsl] = []
                    op("vector", lambda rsl=rsl, pst=pst:
                       E["vector"].tensor_copy(
                        out=rstg[:, rsl * 512:(rsl + 1) * 512],
                        in_=pst[:]), incs=[(s_dve, 1)])
                    bank_consumed(b, s_dve)
                    wt("sync", s_dve, s_dve.v)
                    op("sync", lambda e=e, j=j, dgrp=dgrp, rsl=rsl:
                       E["sync"].dma_start(
                        out=rout_d[e * CAP + j * 128: e * CAP + (j + 1) * 128,
                                   dgrp * 512:(dgrp + 1) * 512],
                        in_=rstg[:, rsl * 512:(rsl + 1) * 512]),
                        incs=[(s_st, 16)])
                    free_rstg[rsl] = [(s_st, s_st.v)]
                free_dns[dslot] = [(s_pe, s_pe.v)]
            v_rdone = s_st.v
            free_act[aslot] = [(s_pe, s_pe.v)]
            free_zeT[esl] = [(s_pe, s_pe.v)]
        v_F_dve = s_dve.v
    v_rout_done = s_st.v

    if KDBG and KPH >= 8:
        wt("sync", s_dve, s_dve.v)
        wt("sync", s_act, s_act.v)
        op("sync", lambda: E["sync"].dma_start(out=dbga_d, in_=actv[:]),
           incs=[(s_st, 16)])
        op("sync", lambda: E["sync"].dma_start(out=dbgz_d, in_=zeT[1][:]),
           incs=[(s_st, 16)])

    if KPH >= 8:
        ctxF1.close()
    ctxF.close()
    ctxE = ExitStack()
    gath = [ctxE.enter_context(nc.sbuf_tensor(f"sb_gath{i}", [128, 6 * 2048], BF16))
            for i in range(2)]
    ostg = ctxE.enter_context(nc.sbuf_tensor("sb_ostg", [128, 4096], F32))
    sl6f = ctxE.enter_context(nc.sbuf_tensor("sb_sl6f", [128, 16], F32))
    sl6i = ctxE.enter_context(nc.sbuf_tensor("sb_sl6i", [128, 16], mybir.dt.int32))
    cbc = ctxE.enter_context(nc.sbuf_tensor("sb_cbc", [128, 16], F32))
    if KPH >= 9:
        free_xin2 = list(free_xin)
        free_ostg = [[], []]
        free_gath = [[], []]
        free_sl6 = [[], []]
        for tt in range(T // 128):
            osl = tt % 2
            wait_list("sync", free_xin2)
            wt("sync", s_st, v_x2_spill[tt])
            op("sync", lambda tt=tt, osl=osl: E["sync"].dma_start(
                out=xin[:], in_=x2_d[tt * 128:(tt + 1) * 128, :]),
                incs=[(s_ep[osl], 16)])
            wait_list("sync", free_sl6[osl])
            op("sync", lambda tt=tt, osl=osl: E["sync"].dma_start(
                out=sl6f[:, osl * 8:osl * 8 + 6],
                in_=slot6_d[tt * 128:(tt + 1) * 128, 0:6]), incs=[(s_ep[osl], 16)])
            op("sync", lambda tt=tt, osl=osl: E["sync"].dma_start(
                out=cbc[:, osl * 8:osl * 8 + 6],
                in_=combc_d[tt * 128:(tt + 1) * 128, 0:6]), incs=[(s_ep[osl], 16)])
            v_tbl = s_ep[osl].v
            v_x2r = v_tbl
            free_sl6[osl] = []
            wt("vector", s_ep[osl], v_tbl)
            op("vector", lambda osl=osl: E["vector"].tensor_copy(
                out=sl6i[:, osl * 8:osl * 8 + 6],
                in_=sl6f[:, osl * 8:osl * 8 + 6]), incs=[(s_dve, 1)])
            v_sli = s_dve.v
            wait_list("gpsimd", free_gath[osl])
            wt("gpsimd", s_dve, v_sli)
            if tt == 0:
                wt("gpsimd", s_st, v_rout_done)
            for k in range(6):
                op("gpsimd", lambda k=k, osl=osl: E["gpsimd"].indirect_dma_start(
                    out=gath[osl][:, k * 2048:(k + 1) * 2048], out_offset=None,
                    in_=rout_d,
                    in_offset=bass.IndirectOffsetOnAxis(
                        ap=sl6i[:, osl * 8 + k: osl * 8 + k + 1], axis=0)),
                    incs=[(s_ga[osl], 16)])
            v_g6 = s_ga[osl].v
            free_gath[osl] = []
            wait_list("vector", free_ostg[osl])
            wt("vector", s_ep[osl], v_x2r)
            if KPH >= 8:
                wt("vector", s_dve, v_F_dve)
            op("vector", lambda tt=tt, osl=osl: E["vector"].tensor_add(
                ostg[:, osl * 2048:(osl + 1) * 2048], xin[:],
                shtm[:, tt * 2048:(tt + 1) * 2048]))
            wt("vector", s_ga[osl], v_g6)
            for k in range(6):
                op("vector", lambda k=k, osl=osl:
                   E["vector"].scalar_tensor_tensor(
                    out=ostg[:, osl * 2048:(osl + 1) * 2048],
                    in0=gath[osl][:, k * 2048:(k + 1) * 2048],
                    scalar=cbc[:, osl * 8 + k: osl * 8 + k + 1],
                    in1=ostg[:, osl * 2048:(osl + 1) * 2048],
                    op0=ALU.mult, op1=ALU.add),
                    incs=([(s_dve, 1)] if k == 5 else []))
            v_acc = s_dve.v
            free_xin2 = [(s_dve, v_acc)]
            free_gath[osl] = [(s_dve, v_acc)]
            free_sl6[osl] = [(s_dve, v_acc)]
            wt("sync", s_dve, v_acc)
            op("sync", lambda tt=tt, osl=osl: E["sync"].dma_start(
                out=out_h[tt * 128:(tt + 1) * 128, :],
                in_=ostg[:, osl * 2048:(osl + 1) * 2048]), incs=[(s_st, 16)])
            free_ostg[osl] = [(s_st, s_st.v)]

    wt("sync", s_st, s_st.v)
    wt("sync", s_ld, s_ld.v)

    finals = {s: s.v for s in ALLSEMS}
    for sem, val in WAITS:
        assert val <= finals[sem], (sem.name, val, finals[sem])

    with nc.Block() as block:
        block.sync(lambda eng: [f() for f in EM["sync"]])
        block.tensor(lambda eng: [f() for f in EM["tensor"]])
        block.vector(lambda eng: [f() for f in EM["vector"]])
        block.scalar(lambda eng: [f() for f in EM["scalar"]])
        block.gpsimd(lambda eng: [f() for f in EM["gpsimd"]])
    ctxE.close()
    ctxR.close()
    ctxP.close()
    return nc


_NC_CACHE = []


def _cstf():
    c = np.zeros((128, 80), dtype=np.float32)
    c[:, 0:32] = (np.arange(32, dtype=np.float32) * CAP - 1.0)[None, :]
    c[:, 32:64] = 1.0
    c[:, 64] = np.arange(128, dtype=np.float32)
    return c


def kernel(L0_attn_x, L0_attn_conv_w, L0_attn_conv_b, L0_attn_xproj_w,
           L0_attn_dtproj_w, L0_attn_dtproj_b, L0_attn_A_log, L0_attn_D_param,
           L0_ffn_gate_w, L0_ffn_wg, L0_ffn_wu, L0_ffn_wd, L0_ffn_sg,
           L0_ffn_su, L0_ffn_sd):
    f32 = lambda a: np.ascontiguousarray(np.asarray(a), dtype=np.float32)
    x = f32(L0_attn_x)
    base = dict(
        conv_w=f32(L0_attn_conv_w).reshape(D, 4),
        conv_b=f32(L0_attn_conv_b),
        xproj_w=f32(L0_attn_xproj_w),
        dtproj_w=f32(L0_attn_dtproj_w),
        dtproj_b=f32(L0_attn_dtproj_b),
        A_log=f32(L0_attn_A_log),
        D_param=f32(L0_attn_D_param),
        gate_w=f32(L0_ffn_gate_w),
        wg=f32(L0_ffn_wg), wu=f32(L0_ffn_wu), wd=f32(L0_ffn_wd),
        sg=f32(L0_ffn_sg), su=f32(L0_ffn_su), sd=f32(L0_ffn_sd),
        iden=np.eye(128, dtype=np.float32),
        triu=np.triu(np.ones((128, 128), dtype=np.float32)),
        ones1=np.ones((1, 128), dtype=np.float32),
        cstf=_cstf(),
    )
    if not _NC_CACHE:
        _NC_CACHE.append(build_nc())
    nc = _NC_CACHE[0]
    in_maps = [dict(base, x=np.ascontiguousarray(x[i])) for i in range(8)]
    res = run_bass_kernel_spmd(nc, in_maps, core_ids=list(range(8)))
    out = np.stack([np.asarray(res.results[i]["out"]) for i in range(8)],
                   axis=0)
    return out


# version 13
# speedup vs baseline: 1.0124x; 1.0124x over previous
import os
import numpy as np
import concourse.bass as bass
import concourse.mybir as mybir
from concourse.bass_utils import run_bass_kernel_spmd
from contextlib import ExitStack

F32 = mybir.dt.float32
BF16 = mybir.dt.bfloat16
AF = mybir.ActivationFunctionType
ALU = mybir.AluOpType
AX = mybir.AxisListType.X

KCH = int(os.environ.get("KCH", "0")) or None
KPH = int(os.environ.get("KPH", "99"))
KDBG = int(os.environ.get("KDBG", "0"))
T, D = 2048, 2048
TC = 512
NCH = T // TC
NTG = TC // 128
NDC = D // 128
NKC = D // 128
NST = 16
DTR = 128
NEXP = 32
FEXP = 768
NFC = FEXP // 128
FSH = 1536
EPS = 1e-5
W1 = 516
TH = 1024
NSB = TH // 512
CAP = 512
NSLOT = NEXP * CAP
BIGF = 4.0e6

ENGS = ("sync", "tensor", "vector", "scalar", "gpsimd")


class Sem:
    def __init__(self, nc, name):
        self.h = nc.alloc_semaphore(name=name)
        self.v = 0
        self.name = name


def build_nc():
    nc = bass.Bass("TRN2", target_bir_lowering=False)
    nc.detect_race_conditions = False

    x_h = nc.dram_tensor("x", [T, D], F32, kind="ExternalInput").ap()
    cw_h = nc.dram_tensor("conv_w", [D, 4], F32, kind="ExternalInput").ap()
    cb_h = nc.dram_tensor("conv_b", [D], F32, kind="ExternalInput").ap()
    xpw_h = nc.dram_tensor("xproj_w", [160, D], F32, kind="ExternalInput").ap()
    dtw_h = nc.dram_tensor("dtproj_w", [D, DTR], F32, kind="ExternalInput").ap()
    dtb_h = nc.dram_tensor("dtproj_b", [D], F32, kind="ExternalInput").ap()
    al_h = nc.dram_tensor("A_log", [D, NST], F32, kind="ExternalInput").ap()
    dp_h = nc.dram_tensor("D_param", [D], F32, kind="ExternalInput").ap()
    gw_h = nc.dram_tensor("gate_w", [D, NEXP], F32, kind="ExternalInput").ap()
    wg_h = nc.dram_tensor("wg", [NEXP, D, FEXP], F32, kind="ExternalInput").ap()
    wu_h = nc.dram_tensor("wu", [NEXP, D, FEXP], F32, kind="ExternalInput").ap()
    wd_h = nc.dram_tensor("wd", [NEXP, FEXP, D], F32, kind="ExternalInput").ap()
    sg_h = nc.dram_tensor("sg", [D, FSH], F32, kind="ExternalInput").ap()
    su_h = nc.dram_tensor("su", [D, FSH], F32, kind="ExternalInput").ap()
    sd_h = nc.dram_tensor("sd", [FSH, D], F32, kind="ExternalInput").ap()
    iden_h = nc.dram_tensor("iden", [128, 128], F32, kind="ExternalInput").ap()
    triu_h = nc.dram_tensor("triu", [128, 128], F32, kind="ExternalInput").ap()
    ones1_h = nc.dram_tensor("ones1", [1, 128], F32, kind="ExternalInput").ap()
    cstf_h = nc.dram_tensor("cstf", [128, 80], F32, kind="ExternalInput").ap()
    out_h = nc.dram_tensor("out", [T, D], F32, kind="ExternalOutput").ap()
    dbg = "ExternalOutput" if KDBG else "Internal"
    bc_d = nc.dram_tensor("bc_d", [32, T], F32, kind="Internal").ap()
    x2_d = nc.dram_tensor("x2_d", [T, D], F32, kind=dbg).ap()
    zb_d = nc.dram_tensor("zb_d", [NKC, 128, T], BF16, kind="Internal").ap()
    ztok_d = nc.dram_tensor("ztok_d", [T + 1, D], F32, kind=dbg).ap()
    tok_d = nc.dram_tensor("tok_d", [NSLOT + 128, 1], F32, kind=dbg).ap()
    slot6_d = nc.dram_tensor("slot6_d", [T, 8], F32, kind=dbg).ap()
    combc_d = nc.dram_tensor("combc_d", [T, 8], F32, kind=dbg).ap()
    rout_d = nc.dram_tensor("rout_d", [NSLOT + 1, D], BF16, kind=dbg).ap()
    dbga_d = nc.dram_tensor("dbga_d", [128, 12288], BF16, kind=dbg).ap()
    dbgz_d = nc.dram_tensor("dbgz_d", [128, 8192], BF16, kind=dbg).ap()
    dbgp_d = nc.dram_tensor("dbgp_d", [128, 8 * 384], F32, kind=dbg).ap()

    ctxP = ExitStack()
    ctxM = ExitStack()
    sbtP = lambda name, shape, dt=F32: ctxP.enter_context(
        nc.sbuf_tensor("sb_" + name, shape, dt))
    sbtM = lambda name, shape, dt=F32: ctxM.enter_context(
        nc.sbuf_tensor("sb_" + name, shape, dt))
    psb = lambda name, shape: ctxP.enter_context(nc.psum_tensor(name, shape, F32))

    iden = sbtP("iden", [128, 128])
    xin = sbtP("xin", [128, 2048])
    stats = sbtP("stats", [128, 16])
    sixs = sbtP("sixs", [128, (T // 128) * 32], mybir.dt.int32)
    tcolv = sbtP("tcolv", [128, T // 128])

    xpwT = sbtM("xpwT", [128, NKC * 160])
    dtpwT = sbtM("dtpwT", [128, D])
    aneg = sbtM("aneg", [128, NDC * NST])
    cw = sbtM("cw", [128, NDC * 4])
    cb = sbtM("cbv", [128, NDC])
    db = sbtM("dbv", [128, NDC])
    dpv = sbtM("dpv", [128, NDC])
    gwT = sbtM("gwT", [128, NKC * 32])
    P1 = sbtM("P1", [128, NDC * W1])
    P2 = sbtM("P2", [128, NDC * 512])
    P3 = sbtM("P3", [128, NDC * 512])
    P4 = sbtM("P4", [128, NDC * 512])
    P5 = sbtM("P5", [128, 2048])
    a2 = sbtM("a2", [128, 1024])
    bbuf = sbtM("bbuf", [128, 512])
    h2 = sbtM("h2", [128, 1024])
    ytb = sbtM("ytb", [128, 1024])
    state = sbtM("state", [128, 256])
    brep = sbtM("brep", [128, 2048])
    dtT = sbtM("dtT", [128, 512])
    zt2 = sbtM("zt2", [128, 1024], BF16)
    bcT = sbtM("bcT", [32, 512])
    halo = sbtM("halo", [128, NDC * 3])
    spl = sbtM("spl", [128, 1024])
    triu = sbtM("triu", [128, 128])
    ones1 = sbtM("ones1", [1, 128])
    cst = sbtM("cst", [128, 80])
    eoffm1 = cst[:, 0:32]
    onesf = cst[:, 32:64]
    iotac = cst[:, 64:65]
    scrT = spl[0:32, 0:512]
    tk6 = spl[:, 512:520]
    ck6 = spl[:, 520:528]
    scr_tm = a2[:, 0:128]
    m8t = a2[:, 256:288]
    selb = a2[:, 288:320]
    cmbb = a2[:, 320:352]
    ksel = a2[:, 352:384]
    msk = a2[:, 384:416]
    sl1 = a2[:, 416:448]
    sl2 = a2[:, 448:480]
    notsel = a2[:, 480:512]

    ps = [psb(f"ps{i}", [128, 512]) for i in range(6)]
    ptf = [psb(f"pt{i}", [128, 512]) for i in range(2)]
    pt = [p[:, 0:128] for p in ptf]

    s_ld = Sem(nc, "s_ld")
    s_st = Sem(nc, "s_st")
    s_pe = Sem(nc, "s_pe")
    s_dve = Sem(nc, "s_dve")
    s_act = Sem(nc, "s_act")
    s_pool = Sem(nc, "s_pool")
    s_gdma = Sem(nc, "s_gdma")
    s_gsc = Sem(nc, "s_gsc")
    s_zb = Sem(nc, "s_zb")
    s_gu = [Sem(nc, f"s_gu{i}") for i in range(6)]
    s_dn = Sem(nc, "s_dn")
    s_ix = Sem(nc, "s_ix")
    s_ep = [Sem(nc, "s_ep0"), Sem(nc, "s_ep1")]
    s_ga = [Sem(nc, "s_ga0"), Sem(nc, "s_ga1")]
    ALLSEMS = (s_ld, s_st, s_pe, s_dve, s_act, s_pool, s_gdma, s_gsc, s_zb,
               s_gu[0], s_gu[1], s_gu[2], s_gu[3], s_gu[4], s_gu[5],
               s_dn, s_ix, s_ep[0], s_ep[1], s_ga[0], s_ga[1])

    for sm in ALLSEMS:
        nc.gpsimd.sem_clear(sm.h)
    nc.all_engine_barrier()

    EM = {e: [] for e in ENGS}
    WAITS = []

    def op(eng, fn, incs=()):
        def thunk(fn=fn, incs=tuple(incs)):
            ins = fn()
            for s, k in incs:
                ins = ins.then_inc(s.h, k)
        EM[eng].append(thunk)
        for s, k in incs:
            s.v += k

    def wt(eng, sem, val):
        if val <= 0:
            return
        WAITS.append((sem, val))
        EM[eng].append(lambda sem=sem, val=val, eng=eng:
                       getattr(nc, eng).wait_ge(sem.h, val))

    E = {e: getattr(nc, e) for e in ENGS}

    bank_free = {}

    def pe_claim(bank):
        for sem, val in bank_free.get(bank, ()):
            wt("tensor", sem, val)
        bank_free[bank] = []

    def bank_consumed(bank, sem):
        bank_free.setdefault(bank, []).append((sem, sem.v))

    bfc = lambda apx: apx.bitcast(BF16)

    def bhalf(tile_ap, c0, width):
        return bfc(tile_ap)[:, 2 * c0 + 1: 2 * (c0 + width): 2]

    def wait_list(eng, lst):
        for sem, val in lst:
            wt(eng, sem, val)

    basev = sbtM("basev", [1, 32])

    op("vector", lambda: E["vector"].memset(state[:], 0.0))
    op("vector", lambda: E["vector"].memset(basev[:], 0.0))
    op("vector", lambda: E["vector"].memset(P5[:], 0.0), incs=[(s_dve, 1)])
    v_ztm0 = s_dve.v
    op("vector", lambda: E["vector"].memset(spl[:], float(T)),
       incs=[(s_dve, 1)])
    v_spl0 = s_dve.v
    wt("sync", s_dve, v_spl0)
    tok2d = tok_d.rearrange("(a b) o -> a (b o)", b=128)
    NROW = NSLOT // 128 + 1
    for r0 in range(0, NROW, 16):
        r1 = min(r0 + 16, NROW)
        op("sync", lambda r0=r0, r1=r1: E["sync"].dma_start(
            out=tok2d[r0:r1, :], in_=spl[0:r1 - r0, 0:128]),
            incs=[(s_st, 16)])
    wt("sync", s_dve, v_ztm0)
    op("sync", lambda: E["sync"].dma_start(
        out=ztok_d[T:T + 1, :], in_=P5[0:1, 0:2048]), incs=[(s_st, 16)])
    op("sync", lambda: E["sync"].dma_start(
        out=rout_d[NSLOT:NSLOT + 1, :], in_=bfc(P5[:])[0:1, 0:2048]),
        incs=[(s_st, 16)])
    v_prefill = s_st.v
    op("sync", lambda: E["sync"].dma_start(out=triu[:], in_=triu_h),
       incs=[(s_ld, 16)])
    op("sync", lambda: E["sync"].dma_start(out=ones1[:], in_=ones1_h),
       incs=[(s_ld, 16)])
    op("sync", lambda: E["sync"].dma_start(out=cst[:], in_=cstf_h),
       incs=[(s_ld, 16)])
    v_cst = s_ld.v
    for dc in range(NDC):
        last = dc == NDC - 1
        op("vector", lambda dc=dc: E["vector"].memset(
            P1[:, dc * W1: dc * W1 + 4], 0.0),
            incs=([(s_dve, 1)] if last else []))

    op("sync", lambda: E["sync"].dma_start(out=iden[:], in_=iden_h),
       incs=[(s_ld, 16)])
    op("sync", lambda: E["sync"].dma_start(
        out=cw[:].rearrange("p (c k) -> p c k", k=4),
        in_=cw_h.rearrange("(c p) k -> p c k", p=128)), incs=[(s_ld, 16)])
    op("sync", lambda: E["sync"].dma_start(
        out=aneg[:].rearrange("p (c n) -> p c n", n=NST),
        in_=al_h.rearrange("(c p) n -> p c n", p=128)), incs=[(s_ld, 16)])
    op("sync", lambda: E["sync"].dma_start(
        out=gwT[:].rearrange("p (c e) -> p c e", e=NEXP),
        in_=gw_h.rearrange("(c p) e -> p c e", p=128)), incs=[(s_ld, 16)])
    v0 = s_ld.v

    wt("scalar", s_ld, v0)
    op("scalar", lambda: E["scalar"].activation(
        out=aneg[:], in_=aneg[:], func=AF.Exp), incs=[(s_act, 1)])
    wt("vector", s_act, s_act.v)
    op("vector", lambda: E["vector"].tensor_scalar(
        out=aneg[:], in0=aneg[:], scalar1=-1.0, scalar2=None, op0=ALU.mult),
        incs=[(s_dve, 1)])

    for vec_h, dst in ((cb_h, cb), (dtb_h, db), (dp_h, dpv)):
        wt("sync", s_pe, s_pe.v)
        op("sync", lambda vec_h=vec_h: E["sync"].dma_start(
            out=bcT[0:16, 0:128], in_=vec_h.rearrange("(c p) -> c p", p=128)),
            incs=[(s_ld, 16)])
        wt("tensor", s_ld, s_ld.v)
        pe_claim("pt0")
        op("tensor", lambda: E["tensor"].transpose(
            pt[0][:, 0:16], bcT[0:16, 0:128], iden[0:16, 0:16]),
            incs=[(s_pe, 1)])
        wt("vector", s_pe, s_pe.v)
        op("vector", lambda dst=dst: E["vector"].tensor_copy(
            out=dst[:], in_=pt[0][:, 0:16]), incs=[(s_dve, 1)])
        bank_consumed("pt0", s_dve)

    op("sync", lambda: E["sync"].dma_start(out=xin[:], in_=xpw_h[0:128, :]),
       incs=[(s_ld, 16)])
    v = s_ld.v
    for kc in range(NKC):
        b = f"pt{kc % 2}"
        wt("tensor", s_ld, v)
        pe_claim(b)
        op("tensor", lambda kc=kc: E["tensor"].transpose(
            pt[kc % 2], xin[:, kc * 128:(kc + 1) * 128], iden[:]),
            incs=[(s_pe, 1)])
        wt("vector", s_pe, s_pe.v)
        op("vector", lambda kc=kc: E["vector"].tensor_copy(
            out=xpwT[:, kc * 160: kc * 160 + 128], in_=pt[kc % 2]),
            incs=[(s_dve, 1)])
        bank_consumed(b, s_dve)
    wt("sync", s_pe, s_pe.v)
    op("sync", lambda: E["sync"].dma_start(
        out=xin[0:32, :], in_=xpw_h[128:160, :]), incs=[(s_ld, 16)])
    v = s_ld.v
    for kc in range(NKC):
        b = f"pt{kc % 2}"
        wt("tensor", s_ld, v)
        pe_claim(b)
        op("tensor", lambda kc=kc: E["tensor"].transpose(
            pt[kc % 2][:, 0:32], xin[0:32, kc * 128:(kc + 1) * 128],
            iden[0:32, 0:32]), incs=[(s_pe, 1)])
        wt("vector", s_pe, s_pe.v)
        op("vector", lambda kc=kc: E["vector"].tensor_copy(
            out=xpwT[:, kc * 160 + 128: (kc + 1) * 160],
            in_=pt[kc % 2][:, 0:32]), incs=[(s_dve, 1)])
        bank_consumed(b, s_dve)

    for kc in range(NKC):
        half = (kc % 2) * 1024
        wt("sync", s_pe, s_pe.v)
        op("sync", lambda kc=kc, half=half: E["sync"].dma_start(
            out=xin[:, half:half + 128],
            in_=dtw_h[kc * 128:(kc + 1) * 128, :]), incs=[(s_ld, 16)])
        b = f"pt{kc % 2}"
        wt("tensor", s_ld, s_ld.v)
        pe_claim(b)
        op("tensor", lambda kc=kc, half=half: E["tensor"].transpose(
            pt[kc % 2], xin[:, half:half + 128], iden[:]), incs=[(s_pe, 1)])
        wt("vector", s_pe, s_pe.v)
        op("vector", lambda kc=kc: E["vector"].tensor_copy(
            out=dtpwT[:, kc * 128:(kc + 1) * 128], in_=pt[kc % 2]),
            incs=[(s_dve, 1)])
        bank_consumed(b, s_dve)

    free_xin = [(s_pe, s_pe.v)]
    free_a = [[], []]
    free_h = [[], []]
    free_yt = [[], []]
    free_brep = [[], []]
    free_spl = [(s_st, v_prefill)]
    free_dtT = []
    free_zt2 = [[], []]
    free_bcT = [(s_dve, s_dve.v)]

    v_sqred = [0]
    v_x2_spill = [0] * (NCH * NTG)
    v_zb_spill = 0
    prev_end = []

    for c in range(KCH or NCH):
        if c > 0:
            for eng in ENGS:
                for sem, val in prev_end:
                    wt(eng, sem, val)

        if KPH < 1:
            continue
        for tg in range(NTG):
            wait_list("sync", free_xin)
            rows = c * TC + tg * 128
            op("sync", lambda rows=rows: E["sync"].dma_start(
                out=xin[:], in_=x_h[rows:rows + 128, :]), incs=[(s_ld, 16)])
            v_x = s_ld.v
            wt("scalar", s_ld, v_x)
            wt("scalar", s_dve, v_sqred[0])
            if c == 0 and tg == 0:
                wt("scalar", s_st, v_prefill)
            op("scalar", lambda: E["scalar"].activation(
                out=P5[:, 0:2048], in_=xin[:], func=AF.Square),
                incs=[(s_act, 1)])
            v_sq = s_act.v
            wt("vector", s_ld, v_x)
            op("vector", lambda: E["vector"].reduce_sum(
                stats[:, 0:1], xin[:], axis=AX), incs=[(s_dve, 1)])
            wt("vector", s_dve, s_dve.v)
            op("vector", lambda: E["vector"].tensor_scalar(
                out=stats[:, 0:1], in0=stats[:, 0:1], scalar1=1.0 / D,
                scalar2=None, op0=ALU.mult))
            wt("vector", s_act, v_sq)
            op("vector", lambda: E["vector"].reduce_sum(
                stats[:, 4:5], P5[:, 0:2048], axis=AX), incs=[(s_dve, 1)])
            v_sqred[0] = s_dve.v
            wt("vector", s_dve, s_dve.v)
            op("vector", lambda: E["vector"].tensor_scalar(
                out=stats[:, 4:5], in0=stats[:, 4:5], scalar1=1.0 / D,
                scalar2=None, op0=ALU.mult))
            op("vector", lambda: E["vector"].tensor_tensor(
                out=stats[:, 8:9], in0=stats[:, 0:1], in1=stats[:, 0:1],
                op=ALU.mult), incs=[(s_dve, 1)])
            wt("vector", s_dve, s_dve.v)
            op("vector", lambda: E["vector"].tensor_tensor(
                out=stats[:, 4:5], in0=stats[:, 4:5], in1=stats[:, 8:9],
                op=ALU.subtract), incs=[(s_dve, 1)])
            wt("scalar", s_dve, s_dve.v)
            op("scalar", lambda: E["scalar"].activation(
                out=stats[:, 8:9], in_=stats[:, 4:5], func=AF.Ln, bias=np.float32(EPS)))
            op("scalar", lambda: E["scalar"].activation(
                out=stats[:, 8:9], in_=stats[:, 8:9], func=AF.Exp, scale=np.float32(-0.5)),
                incs=[(s_act, 1)])
            wt("vector", s_act, s_act.v)
            op("vector", lambda: E["vector"].tensor_scalar(
                out=xin[:], in0=xin[:], scalar1=stats[:, 0:1],
                scalar2=stats[:, 8:9], op0=ALU.subtract, op1=ALU.mult),
                incs=[(s_dve, 1)])
            v_ln = s_dve.v
            for dc in range(NDC):
                b = f"pt{dc % 2}"
                wt("tensor", s_dve, v_ln)
                pe_claim(b)
                op("tensor", lambda dc=dc: E["tensor"].transpose(
                    pt[dc % 2], xin[:, dc * 128:(dc + 1) * 128], iden[:]),
                    incs=[(s_pe, 1)])
                wt("vector", s_pe, s_pe.v)
                op("vector", lambda dc=dc, tg=tg: E["vector"].tensor_copy(
                    out=P1[:, dc * W1 + 4 + tg * 128:
                            dc * W1 + 4 + (tg + 1) * 128],
                    in_=pt[dc % 2]), incs=[(s_dve, 1)])
                bank_consumed(b, s_dve)
            free_xin = [(s_pe, s_pe.v)]

        if c > 0:
            for dc in range(NDC):
                op("vector", lambda dc=dc: E["vector"].tensor_copy(
                    out=P1[:, dc * W1 + 1: dc * W1 + 4],
                    in_=halo[:, dc * 3:(dc + 1) * 3]),
                    incs=([(s_dve, 1)] if dc == NDC - 1 else []))

        if KPH < 2:
            continue
        v_conv_act = 0
        for dc in range(NDC):
            base = dc * W1
            wait_list("vector", free_spl)
            op("vector", lambda base=base, dc=dc: E["vector"].tensor_scalar(
                out=spl[:, 0:512], in0=P1[:, base + 1: base + 513],
                scalar1=cw[:, dc * 4: dc * 4 + 1], scalar2=None, op0=ALU.mult))
            for k in range(1, 4):
                op("vector", lambda base=base, dc=dc, k=k:
                   E["vector"].scalar_tensor_tensor(
                    out=spl[:, 0:512], in0=P1[:, base + 1 + k: base + 513 + k],
                    scalar=cw[:, dc * 4 + k: dc * 4 + k + 1],
                    in1=spl[:, 0:512], op0=ALU.mult, op1=ALU.add),
                    incs=([(s_dve, 1)] if k == 3 else []))
            wt("scalar", s_dve, s_dve.v)
            op("scalar", lambda dc=dc: E["scalar"].activation(
                out=P2[:, dc * 512:(dc + 1) * 512], in_=spl[:, 0:512],
                func=AF.Silu, bias=cb[:, dc:dc + 1]), incs=[(s_act, 1)])
            free_spl = [(s_act, s_act.v)]
            v_conv_act = s_act.v
            op("vector", lambda dc=dc, base=base: E["vector"].tensor_scalar(
                out=P4[:, dc * 512:(dc + 1) * 512],
                in0=P1[:, base + 4: base + 516],
                scalar1=dpv[:, dc:dc + 1], scalar2=None, op0=ALU.mult))
            op("vector", lambda dc=dc, base=base: E["vector"].tensor_copy(
                out=halo[:, dc * 3:(dc + 1) * 3],
                in_=P1[:, base + 513: base + 516]),
                incs=([(s_dve, 1)] if dc == NDC - 1 else []))

        if KPH < 3:
            continue
        pe_claim("ps0")
        pe_claim("ps1")
        wt("tensor", s_act, v_conv_act)
        for kc in range(NKC):
            op("tensor", lambda kc=kc: E["tensor"].matmul(
                ps[0][:], xpwT[:, kc * 160: kc * 160 + 128],
                P2[:, kc * 512:(kc + 1) * 512],
                start=(kc == 0), stop=(kc == NKC - 1), skip_group_check=True),
                incs=([(s_pe, 1)] if kc == NKC - 1 else []))
            op("tensor", lambda kc=kc: E["tensor"].matmul(
                ps[1][0:32, :], xpwT[:, kc * 160 + 128: (kc + 1) * 160],
                P2[:, kc * 512:(kc + 1) * 512],
                start=(kc == 0), stop=(kc == NKC - 1), skip_group_check=True),
                incs=([(s_pe, 1)] if kc == NKC - 1 else []))
        v_xproj = s_pe.v
        wt("vector", s_pe, v_xproj)
        wait_list("vector", free_dtT)
        op("vector", lambda: E["vector"].tensor_copy(out=dtT[:], in_=ps[0][:]),
           incs=[(s_dve, 1)])
        bank_consumed("ps0", s_dve)
        v_dtT = s_dve.v
        wait_list("vector", free_bcT)
        op("vector", lambda: E["vector"].tensor_copy(
            out=bcT[:], in_=ps[1][0:32, :]), incs=[(s_dve, 1)])
        bank_consumed("ps1", s_dve)
        v_bcT = s_dve.v

        wt("sync", s_dve, v_bcT)
        op("sync", lambda c=c: E["sync"].dma_start(
            out=bc_d[:, c * TC:(c + 1) * TC], in_=bcT[:]), incs=[(s_st, 16)])
        v_bc_store = s_st.v
        free_bcT = [(s_st, v_bc_store)]

        if KPH < 4:
            continue
        v_dt_ready = 0
        for dc in range(NDC):
            b = f"ps{2 + dc % 2}"
            wt("tensor", s_dve, v_dtT)
            pe_claim(b)
            op("tensor", lambda dc=dc: E["tensor"].matmul(
                ps[2 + dc % 2][:], dtpwT[:, dc * 128:(dc + 1) * 128], dtT[:],
                start=True, stop=True), incs=[(s_pe, 1)])
            v_mm = s_pe.v
            psd = ps[2 + dc % 2]
            wt("scalar", s_pe, v_mm)
            wait_list("scalar", free_spl)
            op("scalar", lambda psd=psd, dc=dc: E["scalar"].activation(
                out=spl[:, 0:512], in_=psd[:], func=AF.Relu,
                bias=db[:, dc:dc + 1]))
            op("scalar", lambda psd=psd, dc=dc: E["scalar"].activation(
                out=spl[:, 512:1024], in_=psd[:], func=AF.Abs,
                bias=db[:, dc:dc + 1]), incs=[(s_act, 1)])
            bank_consumed(b, s_act)
            op("scalar", lambda: E["scalar"].activation(
                out=spl[:, 512:1024], in_=spl[:, 512:1024], func=AF.Exp,
                scale=np.float32(-1.0)))
            op("scalar", lambda: E["scalar"].activation(
                out=spl[:, 512:1024], in_=spl[:, 512:1024], func=AF.Ln,
                bias=np.float32(1.0)), incs=[(s_act, 1)])
            wt("vector", s_act, s_act.v)
            op("vector", lambda dc=dc: E["vector"].tensor_add(
                P3[:, dc * 512:(dc + 1) * 512], spl[:, 0:512],
                spl[:, 512:1024]), incs=[(s_dve, 1)])
            free_spl = [(s_dve, s_dve.v)]
            v_dt_ready = s_dve.v

        for dc in range(NDC):
            op("vector", lambda dc=dc: E["vector"].tensor_mul(
                P2[:, dc * 512:(dc + 1) * 512],
                P3[:, dc * 512:(dc + 1) * 512],
                P2[:, dc * 512:(dc + 1) * 512]),
                incs=([(s_dve, 1)] if dc == NDC - 1 else []))
        v_dx = s_dve.v

        if KPH < 5:
            continue
        it = 0
        v_scan_act = 0
        for n in range(NST):
            sl = n % 2
            wait_list("sync", free_brep[sl])
            wt("sync", s_st, v_bc_store)
            op("sync", lambda n=n, sl=sl, c=c: E["sync"].dma_start(
                out=brep[:, sl * 1024: sl * 1024 + 512],
                in_=bc_d[n:n + 1, c * TC:(c + 1) * TC].partition_broadcast(128)),
                incs=[(s_ld, 16)])
            op("sync", lambda n=n, sl=sl, c=c: E["sync"].dma_start(
                out=brep[:, sl * 1024 + 512: (sl + 1) * 1024],
                in_=bc_d[16 + n:17 + n, c * TC:(c + 1) * TC].partition_broadcast(128)),
                incs=[(s_ld, 16)])
            v_brep = s_ld.v
            free_brep[sl] = []
            last_b = last_c = 0
            for dc in range(NDC):
                asl = it % 2
                hsl = it % 2
                ysl = it % 2
                wait_list("scalar", free_a[asl])
                if it == 0:
                    wt("scalar", s_dve, v_dt_ready)
                op("scalar", lambda dc=dc, n=n, asl=asl:
                   E["scalar"].activation(
                    out=a2[:, asl * 512:(asl + 1) * 512],
                    in_=P3[:, dc * 512:(dc + 1) * 512], func=AF.Exp,
                    scale=aneg[:, dc * NST + n: dc * NST + n + 1]),
                    incs=[(s_act, 1)])
                free_a[asl] = []
                v_a = s_act.v
                v_scan_act = v_a
                if dc == 0:
                    wt("vector", s_ld, v_brep)
                op("vector", lambda dc=dc, sl=sl: E["vector"].tensor_mul(
                    bbuf[:], P2[:, dc * 512:(dc + 1) * 512],
                    brep[:, sl * 1024: sl * 1024 + 512]))
                wt("vector", s_act, v_a)
                wait_list("vector", free_h[hsl])
                op("vector", lambda dc=dc, n=n, asl=asl, hsl=hsl:
                   E["vector"].tensor_tensor_scan(
                    out=h2[:, hsl * 512:(hsl + 1) * 512],
                    data0=a2[:, asl * 512:(asl + 1) * 512], data1=bbuf[:],
                    initial=state[:, dc * NST + n: dc * NST + n + 1],
                    op0=ALU.mult, op1=ALU.add), incs=[(s_dve, 1)])
                free_a[asl] = [(s_dve, s_dve.v)]
                free_h[hsl] = []
                v_scan = s_dve.v
                last_b = s_dve.v
                wt("gpsimd", s_dve, v_scan)
                wait_list("gpsimd", free_yt[ysl])
                op("gpsimd", lambda sl=sl, hsl=hsl, ysl=ysl:
                   E["gpsimd"].tensor_mul(
                    ytb[:, ysl * 512:(ysl + 1) * 512],
                    h2[:, hsl * 512:(hsl + 1) * 512],
                    brep[:, sl * 1024 + 512: (sl + 1) * 1024]))
                free_yt[ysl] = []
                op("gpsimd", lambda dc=dc, n=n, hsl=hsl:
                   E["gpsimd"].tensor_copy(
                    out=state[:, dc * NST + n: dc * NST + n + 1],
                    in_=h2[:, hsl * 512 + 511: hsl * 512 + 512]),
                    incs=[(s_pool, 1)])
                free_h[hsl] = [(s_pool, s_pool.v)]
                last_c = s_pool.v
                if dc % 4 == 3:
                    op("gpsimd", lambda dc=dc, ysl=ysl: E["gpsimd"].tensor_add(
                        P4[:, dc * 512:(dc + 1) * 512],
                        P4[:, dc * 512:(dc + 1) * 512],
                        ytb[:, ysl * 512:(ysl + 1) * 512]), incs=[(s_pool, 1)])
                    free_yt[ysl] = [(s_pool, s_pool.v)]
                    last_c = s_pool.v
                else:
                    wt("vector", s_pool, s_pool.v)
                    op("vector", lambda dc=dc, ysl=ysl: E["vector"].tensor_add(
                        P4[:, dc * 512:(dc + 1) * 512],
                        P4[:, dc * 512:(dc + 1) * 512],
                        ytb[:, ysl * 512:(ysl + 1) * 512]), incs=[(s_dve, 1)])
                    free_yt[ysl] = [(s_dve, s_dve.v)]
                    last_b = s_dve.v
                it += 1
            free_brep[sl] = [(s_dve, last_b), (s_pool, last_c)]
        v_y_done_v = s_dve.v
        v_y_done_p = s_pool.v

        if KPH < 6:
            continue
        v_x2_list = [0] * NTG
        for tg in range(NTG):
            wait_list("sync", free_xin)
            rows = c * TC + tg * 128
            op("sync", lambda rows=rows: E["sync"].dma_start(
                out=xin[:], in_=x_h[rows:rows + 128, :]), incs=[(s_ld, 16)])
            v_x = s_ld.v
            for dc in range(NDC):
                b = f"pt{dc % 2}"
                wt("tensor", s_dve, v_y_done_v)
                if tg == 0 and dc == 0:
                    wt("tensor", s_pool, v_y_done_p)
                pe_claim(b)
                op("tensor", lambda dc=dc, tg=tg: E["tensor"].transpose(
                    pt[dc % 2],
                    P4[:, dc * 512 + tg * 128: dc * 512 + (tg + 1) * 128],
                    iden[:]), incs=[(s_pe, 1)])
                wt("vector", s_pe, s_pe.v)
                if dc == 0:
                    wt("vector", s_ld, v_x)
                    if tg == 0:
                        wt("vector", s_act, v_scan_act)
                op("vector", lambda dc=dc, tg=tg: E["vector"].tensor_add(
                    P3[:, tg * 2048 + dc * 128: tg * 2048 + (dc + 1) * 128],
                    pt[dc % 2], xin[:, dc * 128:(dc + 1) * 128]),
                    incs=[(s_dve, 1)])
                bank_consumed(b, s_dve)
            free_xin = [(s_dve, s_dve.v)]
            v_x2_list[tg] = s_dve.v
            wt("sync", s_dve, v_x2_list[tg])
            op("sync", lambda tg=tg, c=c: E["sync"].dma_start(
                out=x2_d[c * TC + tg * 128: c * TC + (tg + 1) * 128, :],
                in_=P3[:, tg * 2048:(tg + 1) * 2048]), incs=[(s_st, 16)])
            v_x2_spill[c * NTG + tg] = s_st.v
        v_yT_done = s_pe.v

        v_z = 0
        v_ln2_act = 0
        for tg in range(NTG):
            wt("scalar", s_dve, v_x2_list[tg])
            wt("scalar", s_dve, v_sqred[0])
            op("scalar", lambda tg=tg: E["scalar"].activation(
                out=P5[:, 0:2048], in_=P3[:, tg * 2048:(tg + 1) * 2048],
                func=AF.Square), incs=[(s_act, 1)])
            v_sq2 = s_act.v
            op("vector", lambda tg=tg: E["vector"].reduce_sum(
                stats[:, 0:1], P3[:, tg * 2048:(tg + 1) * 2048], axis=AX),
                incs=[(s_dve, 1)])
            wt("vector", s_dve, s_dve.v)
            op("vector", lambda: E["vector"].tensor_scalar(
                out=stats[:, 0:1], in0=stats[:, 0:1], scalar1=1.0 / D,
                scalar2=None, op0=ALU.mult))
            wt("vector", s_act, v_sq2)
            op("vector", lambda: E["vector"].reduce_sum(
                stats[:, 4:5], P5[:, 0:2048], axis=AX), incs=[(s_dve, 1)])
            v_sqred[0] = s_dve.v
            wt("vector", s_dve, s_dve.v)
            op("vector", lambda: E["vector"].tensor_scalar(
                out=stats[:, 4:5], in0=stats[:, 4:5], scalar1=1.0 / D,
                scalar2=None, op0=ALU.mult))
            op("vector", lambda: E["vector"].tensor_tensor(
                out=stats[:, 8:9], in0=stats[:, 0:1], in1=stats[:, 0:1],
                op=ALU.mult), incs=[(s_dve, 1)])
            wt("vector", s_dve, s_dve.v)
            op("vector", lambda: E["vector"].tensor_tensor(
                out=stats[:, 4:5], in0=stats[:, 4:5], in1=stats[:, 8:9],
                op=ALU.subtract), incs=[(s_dve, 1)])
            wt("scalar", s_dve, s_dve.v)
            op("scalar", lambda: E["scalar"].activation(
                out=stats[:, 8:9], in_=stats[:, 4:5], func=AF.Ln, bias=np.float32(EPS)))
            op("scalar", lambda: E["scalar"].activation(
                out=stats[:, 8:9], in_=stats[:, 8:9], func=AF.Exp, scale=np.float32(-0.5)),
                incs=[(s_act, 1)])
            v_ln2_act = s_act.v
            wt("vector", s_act, s_act.v)
            if tg == 0:
                wt("vector", s_pe, v_yT_done)
            op("vector", lambda tg=tg: E["vector"].tensor_scalar(
                out=P4[:, tg * 2048:(tg + 1) * 2048],
                in0=P3[:, tg * 2048:(tg + 1) * 2048],
                scalar1=stats[:, 0:1], scalar2=stats[:, 8:9],
                op0=ALU.subtract, op1=ALU.mult), incs=[(s_dve, 1)])
            v_z = s_dve.v
            wt("sync", s_dve, v_z)
            op("sync", lambda tg=tg, c=c: E["sync"].dma_start(
                out=ztok_d[c * TC + tg * 128: c * TC + (tg + 1) * 128, :],
                in_=P4[:, tg * 2048:(tg + 1) * 2048]), incs=[(s_st, 16)])

        if KPH < 7:
            continue
        v_spill_last = 0
        for tg in range(NTG):
            for dc in range(NDC):
                b = f"pt{dc % 2}"
                wt("tensor", s_dve, v_z)
                pe_claim(b)
                op("tensor", lambda tg=tg, dc=dc: E["tensor"].transpose(
                    pt[dc % 2],
                    P4[:, tg * 2048 + dc * 128: tg * 2048 + (dc + 1) * 128],
                    iden[:]), incs=[(s_pe, 1)])
                wt("vector", s_pe, s_pe.v)
                if tg == 0 and dc == 0:
                    wt("vector", s_st, v_x2_spill[c * NTG + NTG - 1])
                    wt("vector", s_act, v_ln2_act)
                op("vector", lambda tg=tg, dc=dc: E["vector"].tensor_copy(
                    out=P3[:, dc * 512 + tg * 128: dc * 512 + (tg + 1) * 128],
                    in_=pt[dc % 2]), incs=[(s_dve, 1)])
                bank_consumed(b, s_dve)
        v_zf = s_dve.v
        for dc in range(NDC):
            zsl = dc % 2
            wait_list("vector", free_zt2[zsl])
            op("vector", lambda dc=dc, zsl=zsl: E["vector"].tensor_copy(
                out=zt2[:, zsl * 512:(zsl + 1) * 512],
                in_=P3[:, dc * 512:(dc + 1) * 512]), incs=[(s_dve, 1)])
            free_zt2[zsl] = []
            wt("sync", s_dve, s_dve.v)
            op("sync", lambda dc=dc, zsl=zsl, c=c: E["sync"].dma_start(
                out=zb_d[dc, :, c * TC:(c + 1) * TC],
                in_=zt2[:, zsl * 512:(zsl + 1) * 512]), incs=[(s_st, 16)])
            free_zt2[zsl] = [(s_st, s_st.v)]
        v_zb_spill = s_st.v

        pe_claim("ps1")
        wt("tensor", s_dve, v_zf)
        for kc in range(NKC):
            op("tensor", lambda kc=kc: E["tensor"].matmul(
                ps[1][0:32, :], gwT[:, kc * 32:(kc + 1) * 32],
                P3[:, kc * 512:(kc + 1) * 512],
                start=(kc == 0), stop=(kc == NKC - 1)),
                incs=([(s_pe, 1)] if kc == NKC - 1 else []))
        v_gate = s_pe.v
        wt("scalar", s_pe, v_gate)
        op("scalar", lambda: E["scalar"].activation(
            out=scrT, in_=ps[1][0:32, :], func=AF.Sigmoid),
            incs=[(s_act, 1)])
        bank_consumed("ps1", s_act)
        v_scr = s_act.v
        for tg in range(NTG):
            tt = c * NTG + tg
            b = f"pt{tg % 2}"
            wt("tensor", s_act, v_scr)
            pe_claim(b)
            op("tensor", lambda tg=tg: E["tensor"].transpose(
                pt[tg % 2][:, 0:32], scrT[:, tg * 128:(tg + 1) * 128],
                iden[0:32, 0:32]), incs=[(s_pe, 1)])
            wt("vector", s_pe, s_pe.v)
            if c == 0 and tg == 0:
                wt("vector", s_ld, v_cst)
            op("vector", lambda tg=tg: E["vector"].tensor_copy(
                out=scr_tm[:, 0:32],
                in_=pt[tg % 2][:, 0:32]), incs=[(s_dve, 1)])
            bank_consumed(b, s_dve)
            wt("vector", s_dve, s_dve.v)
            op("vector", lambda tg=tg: E["vector"].max(
                out=m8t[:, 0:8],
                in_=scr_tm[:, 0:32]), incs=[(s_dve, 1)])
            wt("vector", s_dve, s_dve.v)
            op("vector", lambda tg=tg: E["vector"].tensor_scalar(
                out=selb[:], in0=scr_tm[:, 0:32],
                scalar1=m8t[:, 5:6],
                scalar2=None, op0=ALU.is_ge), incs=[(s_dve, 1)])
            v_selb = s_dve.v
            wt("vector", s_dve, s_dve.v)
            op("vector", lambda tg=tg: E["vector"].tensor_mul(
                cmbb[:], selb[:], scr_tm[:, 0:32]),
                incs=[(s_dve, 1)])
            wt("vector", s_dve, s_dve.v)
            op("vector", lambda tg=tg: E["vector"].reduce_sum(
                stats[:, 8:9], cmbb[:], axis=AX), incs=[(s_dve, 1)])
            wt("vector", s_dve, s_dve.v)
            op("vector", lambda tg=tg: E["vector"].tensor_scalar(
                out=stats[:, 8:9], in0=stats[:, 8:9],
                scalar1=1e-20, scalar2=None, op0=ALU.add), incs=[(s_dve, 1)])
            wt("vector", s_dve, s_dve.v)
            op("vector", lambda tg=tg: E["vector"].reciprocal(
                out=stats[:, 12:13], in_=stats[:, 8:9]),
                incs=[(s_dve, 1)])
            wt("vector", s_dve, s_dve.v)
            op("vector", lambda tg=tg: E["vector"].tensor_scalar(
                out=cmbb[:], in0=cmbb[:], scalar1=stats[:, 12:13],
                scalar2=None, op0=ALU.mult), incs=[(s_dve, 1)])
            v_cmbb = s_dve.v
            pe_claim("pt0")
            pe_claim("pt1")
            wt("tensor", s_dve, v_selb)
            op("tensor", lambda: E["tensor"].matmul(
                pt[0][:, 0:32], triu[:], selb[:],
                start=True, stop=False, skip_group_check=True))
            op("tensor", lambda: E["tensor"].matmul(
                pt[0][:, 0:32], ones1[:], basev[:],
                start=False, stop=True, skip_group_check=True),
                incs=[(s_pe, 1)])
            op("tensor", lambda: E["tensor"].matmul(
                pt[1][0:1, 0:32], triu[:, 127:128], selb[:],
                start=True, stop=True, skip_group_check=True),
                incs=[(s_pe, 1)])
            v_pref = s_pe.v
            wt("vector", s_pe, v_pref)
            op("vector", lambda: E["vector"].tensor_add(
                basev[:], basev[:], pt[1][0:1, 0:32]), incs=[(s_dve, 1)])
            bank_consumed("pt1", s_dve)
            v_base_upd = s_dve.v
            op("vector", lambda: E["vector"].tensor_tensor_scan(
                out=ksel[:], data0=onesf, data1=selb[:], initial=0.0,
                op0=ALU.mult, op1=ALU.add), incs=[(s_dve, 1)])
            wt("vector", s_dve, s_dve.v)
            op("vector", lambda: E["vector"].tensor_mul(
                ksel[:], ksel[:], selb[:]), incs=[(s_dve, 1)])
            wt("vector", s_dve, s_dve.v)
            op("vector", lambda: E["vector"].tensor_scalar(
                out=notsel[:], in0=selb[:], scalar1=-1.0, scalar2=1.0,
                op0=ALU.mult, op1=ALU.add))
            op("vector", lambda: E["vector"].tensor_tensor(
                out=sl1[:], in0=pt[0][:, 0:32], in1=eoffm1, op=ALU.add))
            op("vector", lambda: E["vector"].tensor_scalar(
                out=msk[:], in0=pt[0][:, 0:32], scalar1=CAP + 0.5,
                scalar2=None, op0=ALU.is_ge), incs=[(s_dve, 1)])
            bank_consumed("pt0", s_dve)
            wt("vector", s_dve, s_dve.v)
            op("vector", lambda: E["vector"].scalar_tensor_tensor(
                out=sl2[:], in0=notsel[:], scalar=BIGF, in1=sl1[:],
                op0=ALU.mult, op1=ALU.add), incs=[(s_dve, 1)])
            wt("vector", s_dve, s_dve.v)
            op("vector", lambda: E["vector"].scalar_tensor_tensor(
                out=sl1[:], in0=msk[:], scalar=BIGF, in1=sl2[:],
                op0=ALU.mult, op1=ALU.add), incs=[(s_dve, 1)])
            wt("vector", s_dve, s_dve.v)
            op("vector", lambda: E["vector"].tensor_scalar(
                out=sl2[:], in0=sl1[:], scalar1=float(NSLOT), scalar2=None,
                op0=ALU.min), incs=[(s_dve, 1)])
            wt("vector", s_dve, s_dve.v)
            op("vector", lambda tt=tt: E["vector"].tensor_copy(
                out=sixs[:, tt * 32:(tt + 1) * 32], in_=sl2[:]))
            op("vector", lambda tt=tt: E["vector"].tensor_scalar(
                out=tcolv[:, tt:tt + 1], in0=iotac,
                scalar1=float(tt * 128), scalar2=None, op0=ALU.add))
            for k in range(6):
                op("vector", lambda k=k: E["vector"].tensor_scalar(
                    out=P5[:, k * 32:(k + 1) * 32], in0=ksel[:],
                    scalar1=float(k + 1), scalar2=None, op0=ALU.is_equal),
                    incs=([(s_dve, 1)] if k == 5 else []))
            wt("vector", s_dve, s_dve.v)
            for k in range(6):
                op("vector", lambda k=k: E["vector"].tensor_mul(
                    P5[:, 192 + k * 32: 192 + (k + 1) * 32],
                    P5[:, k * 32:(k + 1) * 32], sl2[:]))
                op("vector", lambda k=k: E["vector"].tensor_mul(
                    P5[:, 384 + k * 32: 384 + (k + 1) * 32],
                    P5[:, k * 32:(k + 1) * 32], cmbb[:]),
                    incs=([(s_dve, 1)] if k == 5 else []))
            wt("vector", s_dve, s_dve.v)
            for k in range(6):
                op("vector", lambda k=k: E["vector"].reduce_sum(
                    tk6[:, k:k + 1], P5[:, 192 + k * 32: 192 + (k + 1) * 32],
                    axis=AX))
                op("vector", lambda k=k: E["vector"].reduce_sum(
                    ck6[:, k:k + 1], P5[:, 384 + k * 32: 384 + (k + 1) * 32],
                    axis=AX), incs=([(s_dve, 1)] if k == 5 else []))
            wt("vector", s_dve, s_dve.v)
            op("vector", lambda: E["vector"].engine_nop(), incs=[(s_dve, 1)])
            v_tk = s_dve.v
            wt("sync", s_dve, v_tk)
            op("sync", lambda tt=tt: E["sync"].dma_start(
                out=slot6_d[tt * 128:(tt + 1) * 128, 0:6],
                in_=tk6[:, 0:6]), incs=[(s_st, 16)])
            op("sync", lambda tt=tt: E["sync"].dma_start(
                out=combc_d[tt * 128:(tt + 1) * 128, 0:6],
                in_=ck6[:, 0:6]), incs=[(s_st, 16)])
        v_tables = s_st.v
        free_spl = [(s_st, v_tables)]
        v_sqred[0] = v_tk

        prev_end = [(s_dve, s_dve.v), (s_act, s_act.v), (s_pe, s_pe.v),
                    (s_pool, s_pool.v), (s_st, s_st.v), (s_ld, s_ld.v)]

    ctxM.close()
    ctxR = ExitStack()
    ctxF = ExitStack()
    sbtF = lambda name, shape, dt=F32: ctxF.enter_context(
        nc.sbuf_tensor("sb_" + name, shape, dt))
    shtm = ctxR.enter_context(nc.sbuf_tensor("sb_shtm", [128, 16 * 2048], BF16))
    actv = sbtF("act", [128, 2 * NFC * 1024], BF16)
    dns = sbtF("dns", [128, 2 * 12 * 512])
    sG = sbtF("sG", [128, 1024])
    rstg = sbtF("rstg", [128, 1024], BF16)
    ixf = sbtF("ixf", [128, 8])
    ixi = sbtF("ixi", [128, 8], mybir.dt.int32)
    ctxF0 = ExitStack()
    zbh = ctxF0.enter_context(nc.sbuf_tensor("sb_zbh", [128, NKC * 1024], BF16))
    gus = ctxF0.enter_context(nc.sbuf_tensor("sb_gus", [128, 2048], F32))

    v_sixs_done = s_dve.v
    v_F_dve = 0
    if KPH >= 8:
        for eng in ENGS:
            for sem, val in prev_end:
                wt(eng, sem, val)
        free_zbh = []
        free_gus = [[], []]
        free_act = [[], []]
        free_sG = []
        free_dns = [[], []]
        free_rstg = [[], []]
        free_zer = [[]]
        free_zeT = [[], []]
        gu_seq = [0]

        def gu_dma(e, fc, hf, mat):
            slot = gu_seq[0] % 2
            gu_seq[0] += 1
            wait_list("sync", free_gus[slot])
            if e < NEXP:
                src_h = wg_h if mat == 0 else wu_h
                src = src_h[e, :, fc * 128:(fc + 1) * 128].rearrange(
                    "(c p) f -> p c f", p=128)[:, hf * 8:(hf + 1) * 8, :]
            else:
                src_h = sg_h if mat == 0 else su_h
                sx = e - NEXP
                src = src_h[:, sx * FEXP + fc * 128: sx * FEXP + (fc + 1) * 128
                            ].rearrange("(c p) f -> p c f", p=128)[
                                :, hf * 8:(hf + 1) * 8, :]
            col = slot * 1024
            dst = gus[:, col:col + 1024].rearrange("p (c f) -> p c f", f=128)
            op("sync", lambda src=src, dst=dst: E["sync"].dma_start(
                out=dst, in_=src), incs=[(s_gu[slot], 16)])
            free_gus[slot] = []
            return slot, s_gu[slot].v

        for h in range(2):
            wait_list("sync", free_zbh)
            wt("sync", s_st, v_zb_spill)
            for kc in range(NKC):
                op("sync", lambda kc=kc, h=h: E["sync"].dma_start(
                    out=zbh[:, kc * 1024:(kc + 1) * 1024],
                    in_=zb_d[kc, :, h * TH:(h + 1) * TH]), incs=[(s_zb, 16)])
            v_zbh = s_zb.v
            free_zbh = []
            for ei, e in enumerate((NEXP, NEXP + 1)):
                for fc in range(NFC):
                    pe_claim("ps0")
                    pe_claim("ps1")
                    pe_claim("ps2")
                    pe_claim("ps3")
                    for mat in (0, 1):
                        pst0 = ps[mat * 2]
                        pst1 = ps[mat * 2 + 1]
                        for hf in (0, 1):
                            slot, vld = gu_dma(e, fc, hf, mat)
                            wt("tensor", s_gu[slot], vld)
                            if hf == 0 and fc == 0 and mat == 0 and ei == 0:
                                wt("tensor", s_zb, v_zbh)
                            for kc in range(hf * 8, hf * 8 + 8):
                                col = slot * 1024 + (kc % 8) * 128
                                lhsT = bhalf(gus[:], col, 128)
                                for sub in range(NSB):
                                    pstt = pst0 if sub == 0 else pst1
                                    op("tensor",
                                       lambda pstt=pstt, lhsT=lhsT, kc=kc, sub=sub:
                                       E["tensor"].matmul(
                                        pstt[:], lhsT,
                                        zbh[:, kc * 1024 + sub * 512:
                                            kc * 1024 + (sub + 1) * 512],
                                        start=(kc == 0), stop=(kc == NKC - 1),
                                        skip_group_check=True),
                                        incs=([(s_pe, 1)]
                                              if (kc % 8 == 7 and sub == NSB - 1)
                                              else []))
                            free_gus[slot] = [(s_pe, s_pe.v)]
                    v_gu = s_pe.v
                    wt("scalar", s_pe, v_gu)
                    wait_list("scalar", free_sG)
                    op("scalar", lambda: E["scalar"].activation(
                        out=sG[:, 0:512], in_=ps[0][:], func=AF.Silu))
                    op("scalar", lambda: E["scalar"].activation(
                        out=sG[:, 512:1024], in_=ps[1][:], func=AF.Silu),
                        incs=[(s_act, 1)])
                    bank_consumed("ps0", s_act)
                    bank_consumed("ps1", s_act)
                    free_sG = []
                    v_sg = s_act.v
                    acol = ei * NFC * 1024 + fc * 1024
                    wt("vector", s_act, v_sg)
                    wait_list("vector", free_act[0])
                    wt("vector", s_pe, v_gu)
                    op("vector", lambda acol=acol:
                       E["vector"].tensor_mul(
                        actv[:, acol:acol + 512],
                        sG[:, 0:512], ps[2][:]))
                    op("vector", lambda acol=acol:
                       E["vector"].tensor_mul(
                        actv[:, acol + 512:acol + 1024],
                        sG[:, 512:1024], ps[3][:]), incs=[(s_dve, 1)])
                    bank_consumed("ps2", s_dve)
                    bank_consumed("ps3", s_dve)
                    free_sG = [(s_dve, s_dve.v)]
                    free_act[0] = []
            v_act_sh = s_dve.v
            for dgrp in range(4):
                dslot = dgrp % 2
                wait_list("sync", free_dns[dslot])
                op("sync", lambda dgrp=dgrp, dslot=dslot:
                   E["sync"].dma_start(
                    out=dns[:, dslot * 6144:(dslot + 1) * 6144].rearrange(
                        "p (c d) -> p c d", d=512),
                    in_=sd_h[:, dgrp * 512:(dgrp + 1) * 512].rearrange(
                        "(c p) d -> p c d", p=128)), incs=[(s_dn, 16)])
                v_dnl = s_dn.v
                free_dns[dslot] = []
                for tloc in range(8):
                    tt = h * 8 + tloc
                    b = f"ps{4 + (tloc % 2)}"
                    pst = ps[4 + (tloc % 2)]
                    pe_claim(b)
                    if tloc == 0:
                        wt("tensor", s_dve, v_act_sh)
                        wt("tensor", s_dn, v_dnl)
                    nmm = 0
                    for fkc in range(12):
                        acol = (fkc // 6) * 6144 + (fkc % 6) * 1024 + \
                            (tloc // 4) * 512 + (tloc % 4) * 128
                        nmm += 1
                        op("tensor",
                           lambda pst=pst, acol=acol, dslot=dslot, fkc=fkc,
                           first=(nmm == 1), last=(nmm == 12):
                           E["tensor"].matmul(
                            pst[:], actv[:, acol:acol + 128],
                            bhalf(dns[:], dslot * 6144 + fkc * 512, 512),
                            start=first, stop=last,
                            skip_group_check=True),
                            incs=([(s_pe, 1)] if nmm == 12 else []))
                    v_dnmm = s_pe.v
                    wt("vector", s_pe, v_dnmm)
                    op("vector", lambda tt=tt, dgrp=dgrp, pst=pst:
                       E["vector"].tensor_copy(
                        out=shtm[:, tt * 2048 + dgrp * 512:
                                 tt * 2048 + (dgrp + 1) * 512],
                        in_=pst[:]), incs=[(s_dve, 1)])
                    bank_consumed(b, s_dve)
                free_dns[dslot] = [(s_pe, s_pe.v)]
            free_act[0] = [(s_pe, s_pe.v)]
            free_act[1] = [(s_pe, s_pe.v)]
            free_zbh = [(s_pe, s_pe.v)]

        v_F0_pe = s_pe.v
        for eng in ("gpsimd", "sync", "vector", "scalar", "tensor"):
            wt(eng, s_pe, v_F0_pe)
        ctxF0.close()
        ctxF1 = ExitStack()
        zeT = [ctxF1.enter_context(nc.sbuf_tensor(f"sb_zeT{i}", [128, NKC * 512], BF16))
               for i in range(2)]
        gslab = ctxF1.enter_context(nc.sbuf_tensor("sb_gslab", [128, 3 * 2 * 384], F32))
        zer = [ctxF1.enter_context(nc.sbuf_tensor("sb_zer0", [128, 2048], F32))]
        a32 = ctxF1.enter_context(nc.sbuf_tensor("sb_a32", [128, 2 * 384], F32))
        free_gsl = [[] for _ in range(3)]
        free_a32 = [[], []]
        v_scat = 0
        v_ixc_prev = [0]
        for e in range(NEXP):
            wt("gpsimd", s_dve, v_sixs_done)
            if e == 0:
                wt("gpsimd", s_st, v_prefill)
            else:
                wt("gpsimd", s_ix, s_ix.v)
            for tt in range(T // 128):
                op("gpsimd", lambda tt=tt, e=e:
                   E["gpsimd"].indirect_dma_start(
                    out=tok_d,
                    out_offset=bass.IndirectOffsetOnAxis(
                        ap=sixs[:, tt * 32 + e: tt * 32 + e + 1], axis=0),
                    in_=tcolv[:, tt:tt + 1], in_offset=None),
                    incs=[(s_gsc, 16)])
            v_scat = s_gsc.v
            wt("sync", s_gsc, v_scat)
            wt("sync", s_pool, v_ixc_prev[0])
            for j in range(4):
                op("sync", lambda e=e, j=j: E["sync"].dma_start(
                    out=ixf[:, j:j + 1],
                    in_=tok2d[e * 4 + j: e * 4 + j + 1, :].rearrange(
                        "a (p o) -> (a p) o", o=1)), incs=[(s_ix, 16)])
            v_ixl = s_ix.v
            wt("gpsimd", s_ix, v_ixl)
            op("gpsimd", lambda: E["gpsimd"].tensor_copy(
                out=ixi[:, 0:4], in_=ixf[:, 0:4]), incs=[(s_pool, 1)])
            v_ixc = s_pool.v
            v_ixc_prev[0] = v_ixc
            wt("gpsimd", s_pool, v_ixc)
            esl = e % 2
            wait_list("gpsimd", free_zeT[esl])
            for j in range(4):
                zsl = 0
                wait_list("gpsimd", free_zer[zsl])
                op("gpsimd", lambda j=j, zsl=zsl: E["gpsimd"].indirect_dma_start(
                    out=zer[zsl][:], out_offset=None,
                    in_=ztok_d,
                    in_offset=bass.IndirectOffsetOnAxis(
                        ap=ixi[:, j:j + 1], axis=0)),
                    incs=[(s_gdma, 16)])
                v_gz = s_gdma.v
                free_zer[zsl] = []
                for kc in range(NKC):
                    bb = f"pt{kc % 2}"
                    pe_claim(bb)
                    wt("tensor", s_gdma, v_gz)
                    op("tensor", lambda kc=kc, zsl=zsl: E["tensor"].transpose(
                        pt[kc % 2], zer[zsl][:, kc * 128:(kc + 1) * 128],
                        iden[:]), incs=[(s_pe, 1)])
                    eng = "vector" if kc % 2 == 0 else "scalar"
                    wt(eng, s_pe, s_pe.v)
                    if eng == "vector":
                        op(eng, lambda kc=kc, j=j, esl=esl: E["vector"].tensor_copy(
                            out=zeT[esl][:, kc * 512 + j * 128:
                                         kc * 512 + (j + 1) * 128],
                            in_=pt[kc % 2]), incs=[(s_dve, 1)])
                        bank_consumed(bb, s_dve)
                    else:
                        op(eng, lambda kc=kc, j=j, esl=esl: E["scalar"].copy(
                            out=zeT[esl][:, kc * 512 + j * 128:
                                         kc * 512 + (j + 1) * 128],
                            in_=pt[kc % 2]), incs=[(s_act, 1)])
                        bank_consumed(bb, s_act)
                free_zer[zsl] = [(s_pe, s_pe.v)]
            v_zeT = [(s_dve, s_dve.v), (s_act, s_act.v)]
            aslot = e % 2
            ab = aslot * NFC * 512
            PSN = ["ps0", "ps1", "ps2", "ps3", "ps4", "ps5", "pt0", "pt1"]
            PSA = ps + ptf
            first_act = True
            gcnt = [0]
            gslm = {}
            for fh in (0, 1):
                for b in PSN:
                    pe_claim(b)
                for kcg in range(8):
                    for mat in (0, 1):
                        gsl = gcnt[0] % 3
                        gslm[(kcg, mat)] = gsl
                        gcnt[0] += 1
                        wait_list("sync", free_gsl[gsl])
                        src_h = wg_h if mat == 0 else wu_h
                        op("sync", lambda e=e, kcg=kcg, gsl=gsl, src_h=src_h, fh=fh:
                           E["sync"].dma_start(
                            out=gslab[:, gsl * 768:(gsl + 1) * 768].rearrange(
                                "p (c f) -> p c f", f=384),
                            in_=src_h[e, kcg * 256:(kcg + 1) * 256,
                                      fh * 384:(fh + 1) * 384].rearrange(
                                "(c p) f -> p c f", p=128)),
                            incs=[(s_gu[gsl], 16)])
                        free_gsl[gsl] = []
                    for mat in (0, 1):
                        gsl = gslm[(kcg, mat)]
                        wt("tensor", s_gu[gsl], s_gu[gsl].v)
                        if fh == 0 and kcg == 0 and mat == 0:
                            wait_list("tensor", v_zeT)
                        for kc in range(kcg * 2, (kcg + 1) * 2):
                            for j in range(4):
                                bank = PSA[j * 2 + mat]
                                op("tensor",
                                   lambda bank=bank, kc=kc, j=j, gsl=gsl, esl=esl:
                                   E["tensor"].matmul(
                                    bank[:, 0:384],
                                    zeT[esl][:, kc * 512 + j * 128:
                                             kc * 512 + (j + 1) * 128],
                                    bhalf(gslab[:],
                                          gsl * 768 + (kc % 2) * 384, 384),
                                    start=(kc == 0), stop=(kc == NKC - 1),
                                    skip_group_check=True),
                                    incs=([(s_pe, 1)]
                                          if (j == 3 and kc % 2 == 1) else []))
                        free_gsl[gsl] = [(s_pe, s_pe.v)]
                v_gu = s_pe.v
                if KDBG and e == NEXP - 1 and fh == 1:
                    wt("vector", s_pe, v_gu)
                    for bi in range(8):
                        op("vector", lambda bi=bi: E["vector"].tensor_copy(
                            out=dns[:, bi * 384:(bi + 1) * 384],
                            in_=PSA[bi][:, 0:384]),
                            incs=([(s_dve, 1)] if bi == 7 else []))
                    wt("sync", s_dve, s_dve.v)
                    op("sync", lambda: E["sync"].dma_start(
                        out=dbgp_d, in_=dns[:, 0:3072]), incs=[(s_st, 16)])
                    wt("vector", s_st, s_st.v)
                for j in (3, 2, 1, 0):
                    asl2 = j % 2
                    wt("scalar", s_pe, v_gu)
                    wait_list("scalar", free_sG)
                    op("scalar", lambda j=j: E["scalar"].activation(
                        out=sG[:, 0:384], in_=PSA[j * 2 + 0][:, 0:384],
                        func=AF.Silu), incs=[(s_act, 1)])
                    bank_consumed(PSN[j * 2 + 0], s_act)
                    free_sG = []
                    v_sg = s_act.v
                    wt("vector", s_act, v_sg)
                    wait_list("vector", free_a32[asl2])
                    wt("vector", s_pe, v_gu)
                    op("vector", lambda j=j, asl2=asl2: E["vector"].tensor_mul(
                        a32[:, asl2 * 384:(asl2 + 1) * 384],
                        sG[:, 0:384], PSA[j * 2 + 1][:, 0:384]),
                        incs=[(s_dve, 1)])
                    bank_consumed(PSN[j * 2 + 1], s_dve)
                    free_sG = [(s_dve, s_dve.v)]
                    free_a32[asl2] = []
                    v_a32 = s_dve.v
                    if KDBG and e == NEXP - 1 and fh == 1:
                        wt("sync", s_dve, v_a32)
                        op("sync", lambda j=j, asl2=asl2: E["sync"].dma_start(
                            out=dbgp_d[:, (4 + j) * 384:(5 + j) * 384],
                            in_=a32[:, asl2 * 384:(asl2 + 1) * 384]),
                            incs=[(s_st, 16)])
                        wt("vector", s_st, s_st.v)
                    for ft3 in range(3):
                        ft = fh * 3 + ft3
                        b = f"pt{ft3 % 2}"
                        pe_claim(b)
                        wt("tensor", s_dve, v_a32)
                        if first_act:
                            wait_list("tensor", free_act[aslot])
                            first_act = False
                        op("tensor", lambda ft3=ft3, asl2=asl2:
                           E["tensor"].transpose(
                            pt[ft3 % 2],
                            a32[:, asl2 * 384 + ft3 * 128:
                                asl2 * 384 + (ft3 + 1) * 128],
                            iden[:]), incs=[(s_pe, 1)])
                        eng = "vector" if ft3 % 2 == 0 else "scalar"
                        wt(eng, s_pe, s_pe.v)
                        if eng == "vector":
                            op(eng, lambda ft=ft, j=j, ab=ab, ft3=ft3:
                               E["vector"].tensor_copy(
                                out=actv[:, ab + ft * 512 + j * 128:
                                         ab + ft * 512 + (j + 1) * 128],
                                in_=pt[ft3 % 2]), incs=[(s_dve, 1)])
                            bank_consumed(b, s_dve)
                        else:
                            op(eng, lambda ft=ft, j=j, ab=ab, ft3=ft3:
                               E["scalar"].copy(
                                out=actv[:, ab + ft * 512 + j * 128:
                                         ab + ft * 512 + (j + 1) * 128],
                                in_=pt[ft3 % 2]), incs=[(s_act, 1)])
                            bank_consumed(b, s_act)
                    free_a32[asl2] = [(s_pe, s_pe.v)]
            op("vector", lambda: E["vector"].engine_nop(), incs=[(s_dve, 1)])
            wt("vector", s_act, s_act.v)
            op("vector", lambda: E["vector"].engine_nop(), incs=[(s_dve, 1)])
            v_act_e = s_dve.v
            for dgrp in range(4):
                dslot = dgrp % 2
                wait_list("sync", free_dns[dslot])
                op("sync", lambda e=e, dgrp=dgrp, dslot=dslot:
                   E["sync"].dma_start(
                    out=dns[:, dslot * 6144: dslot * 6144 + 3072].rearrange(
                        "p (c d) -> p c d", d=512),
                    in_=wd_h[e, :, dgrp * 512:(dgrp + 1) * 512].rearrange(
                        "(c p) d -> p c d", p=128)), incs=[(s_dn, 16)])
                v_dnl = s_dn.v
                free_dns[dslot] = []
                for j in range(4):
                    b = f"ps{4 + (j % 2)}"
                    pst = ps[4 + (j % 2)]
                    pe_claim(b)
                    if j == 0:
                        wt("tensor", s_dve, v_act_e)
                        wt("tensor", s_dn, v_dnl)
                    nmm = 0
                    for fkc in range(NFC):
                        nmm += 1
                        op("tensor",
                           lambda pst=pst, fkc=fkc, j=j, dslot=dslot,
                           ab=ab, first=(nmm == 1), last=(nmm == NFC):
                           E["tensor"].matmul(
                            pst[:],
                            actv[:, ab + fkc * 512 + j * 128:
                                 ab + fkc * 512 + (j + 1) * 128],
                            bhalf(dns[:], dslot * 6144 + fkc * 512, 512),
                            start=first, stop=last,
                            skip_group_check=True),
                            incs=([(s_pe, 1)] if nmm == NFC else []))
                    v_dnmm = s_pe.v
                    rsl = (dgrp * 4 + j) % 2
                    wt("vector", s_pe, v_dnmm)
                    wait_list("vector", free_rstg[rsl])
                    free_rstg[rsl] = []
                    op("vector", lambda rsl=rsl, pst=pst:
                       E["vector"].tensor_copy(
                        out=rstg[:, rsl * 512:(rsl + 1) * 512],
                        in_=pst[:]), incs=[(s_dve, 1)])
                    bank_consumed(b, s_dve)
                    wt("sync", s_dve, s_dve.v)
                    op("sync", lambda e=e, j=j, dgrp=dgrp, rsl=rsl:
                       E["sync"].dma_start(
                        out=rout_d[e * CAP + j * 128: e * CAP + (j + 1) * 128,
                                   dgrp * 512:(dgrp + 1) * 512],
                        in_=rstg[:, rsl * 512:(rsl + 1) * 512]),
                        incs=[(s_st, 16)])
                    free_rstg[rsl] = [(s_st, s_st.v)]
                free_dns[dslot] = [(s_pe, s_pe.v)]
            v_rdone = s_st.v
            free_act[aslot] = [(s_pe, s_pe.v)]
            free_zeT[esl] = [(s_pe, s_pe.v)]
        v_F_dve = s_dve.v
    v_rout_done = s_st.v

    if KDBG and KPH >= 8:
        wt("sync", s_dve, s_dve.v)
        wt("sync", s_act, s_act.v)
        op("sync", lambda: E["sync"].dma_start(out=dbga_d, in_=actv[:]),
           incs=[(s_st, 16)])
        op("sync", lambda: E["sync"].dma_start(out=dbgz_d, in_=zeT[1][:]),
           incs=[(s_st, 16)])

    if KPH >= 8:
        ctxF1.close()
    ctxF.close()
    ctxE = ExitStack()
    gath = [ctxE.enter_context(nc.sbuf_tensor(f"sb_gath{i}", [128, 6 * 2048], BF16))
            for i in range(2)]
    ostg = ctxE.enter_context(nc.sbuf_tensor("sb_ostg", [128, 4096], F32))
    sl6f = ctxE.enter_context(nc.sbuf_tensor("sb_sl6f", [128, 16], F32))
    sl6i = ctxE.enter_context(nc.sbuf_tensor("sb_sl6i", [128, 16], mybir.dt.int32))
    cbc = ctxE.enter_context(nc.sbuf_tensor("sb_cbc", [128, 16], F32))
    if KPH >= 9:
        free_xin2 = list(free_xin)
        free_ostg = [[], []]
        free_gath = [[], []]
        free_sl6 = [[], []]
        for tt in range(T // 128):
            osl = tt % 2
            wait_list("sync", free_xin2)
            wt("sync", s_st, v_x2_spill[tt])
            op("sync", lambda tt=tt, osl=osl: E["sync"].dma_start(
                out=xin[:], in_=x2_d[tt * 128:(tt + 1) * 128, :]),
                incs=[(s_ep[osl], 16)])
            wait_list("sync", free_sl6[osl])
            op("sync", lambda tt=tt, osl=osl: E["sync"].dma_start(
                out=sl6f[:, osl * 8:osl * 8 + 6],
                in_=slot6_d[tt * 128:(tt + 1) * 128, 0:6]), incs=[(s_ep[osl], 16)])
            op("sync", lambda tt=tt, osl=osl: E["sync"].dma_start(
                out=cbc[:, osl * 8:osl * 8 + 6],
                in_=combc_d[tt * 128:(tt + 1) * 128, 0:6]), incs=[(s_ep[osl], 16)])
            v_tbl = s_ep[osl].v
            v_x2r = v_tbl
            free_sl6[osl] = []
            wt("vector", s_ep[osl], v_tbl)
            op("vector", lambda osl=osl: E["vector"].tensor_copy(
                out=sl6i[:, osl * 8:osl * 8 + 6],
                in_=sl6f[:, osl * 8:osl * 8 + 6]), incs=[(s_dve, 1)])
            v_sli = s_dve.v
            wait_list("gpsimd", free_gath[osl])
            wt("gpsimd", s_dve, v_sli)
            if tt == 0:
                wt("gpsimd", s_st, v_rout_done)
            for k in range(6):
                op("gpsimd", lambda k=k, osl=osl: E["gpsimd"].indirect_dma_start(
                    out=gath[osl][:, k * 2048:(k + 1) * 2048], out_offset=None,
                    in_=rout_d,
                    in_offset=bass.IndirectOffsetOnAxis(
                        ap=sl6i[:, osl * 8 + k: osl * 8 + k + 1], axis=0)),
                    incs=[(s_ga[osl], 16)])
            v_g6 = s_ga[osl].v
            free_gath[osl] = []
            wait_list("vector", free_ostg[osl])
            wt("vector", s_ep[osl], v_x2r)
            if KPH >= 8:
                wt("vector", s_dve, v_F_dve)
            op("vector", lambda tt=tt, osl=osl: E["vector"].tensor_add(
                ostg[:, osl * 2048:(osl + 1) * 2048], xin[:],
                shtm[:, tt * 2048:(tt + 1) * 2048]))
            wt("vector", s_ga[osl], v_g6)
            for k in range(6):
                op("vector", lambda k=k, osl=osl:
                   E["vector"].scalar_tensor_tensor(
                    out=ostg[:, osl * 2048:(osl + 1) * 2048],
                    in0=gath[osl][:, k * 2048:(k + 1) * 2048],
                    scalar=cbc[:, osl * 8 + k: osl * 8 + k + 1],
                    in1=ostg[:, osl * 2048:(osl + 1) * 2048],
                    op0=ALU.mult, op1=ALU.add),
                    incs=([(s_dve, 1)] if k == 5 else []))
            v_acc = s_dve.v
            free_xin2 = [(s_dve, v_acc)]
            free_gath[osl] = [(s_dve, v_acc)]
            free_sl6[osl] = [(s_dve, v_acc)]
            wt("sync", s_dve, v_acc)
            op("sync", lambda tt=tt, osl=osl: E["sync"].dma_start(
                out=out_h[tt * 128:(tt + 1) * 128, :],
                in_=ostg[:, osl * 2048:(osl + 1) * 2048]), incs=[(s_st, 16)])
            free_ostg[osl] = [(s_st, s_st.v)]

    wt("sync", s_st, s_st.v)
    wt("sync", s_ld, s_ld.v)

    finals = {s: s.v for s in ALLSEMS}
    for sem, val in WAITS:
        assert val <= finals[sem], (sem.name, val, finals[sem])

    with nc.Block() as block:
        block.sync(lambda eng: [f() for f in EM["sync"]])
        block.tensor(lambda eng: [f() for f in EM["tensor"]])
        block.vector(lambda eng: [f() for f in EM["vector"]])
        block.scalar(lambda eng: [f() for f in EM["scalar"]])
        block.gpsimd(lambda eng: [f() for f in EM["gpsimd"]])
    ctxE.close()
    ctxR.close()
    ctxP.close()
    return nc


_NC_CACHE = []


def _cstf():
    c = np.zeros((128, 80), dtype=np.float32)
    c[:, 0:32] = (np.arange(32, dtype=np.float32) * CAP - 1.0)[None, :]
    c[:, 32:64] = 1.0
    c[:, 64] = np.arange(128, dtype=np.float32)
    return c


def kernel(L0_attn_x, L0_attn_conv_w, L0_attn_conv_b, L0_attn_xproj_w,
           L0_attn_dtproj_w, L0_attn_dtproj_b, L0_attn_A_log, L0_attn_D_param,
           L0_ffn_gate_w, L0_ffn_wg, L0_ffn_wu, L0_ffn_wd, L0_ffn_sg,
           L0_ffn_su, L0_ffn_sd):
    f32 = lambda a: np.ascontiguousarray(np.asarray(a), dtype=np.float32)
    x = f32(L0_attn_x)
    base = dict(
        conv_w=f32(L0_attn_conv_w).reshape(D, 4),
        conv_b=f32(L0_attn_conv_b),
        xproj_w=f32(L0_attn_xproj_w),
        dtproj_w=f32(L0_attn_dtproj_w),
        dtproj_b=f32(L0_attn_dtproj_b),
        A_log=f32(L0_attn_A_log),
        D_param=f32(L0_attn_D_param),
        gate_w=f32(L0_ffn_gate_w),
        wg=f32(L0_ffn_wg), wu=f32(L0_ffn_wu), wd=f32(L0_ffn_wd),
        sg=f32(L0_ffn_sg), su=f32(L0_ffn_su), sd=f32(L0_ffn_sd),
        iden=np.eye(128, dtype=np.float32),
        triu=np.triu(np.ones((128, 128), dtype=np.float32)),
        ones1=np.ones((1, 128), dtype=np.float32),
        cstf=_cstf(),
    )
    if not _NC_CACHE:
        _NC_CACHE.append(build_nc())
    nc = _NC_CACHE[0]
    in_maps = [dict(base, x=np.ascontiguousarray(x[i])) for i in range(8)]
    res = run_bass_kernel_spmd(nc, in_maps, core_ids=list(range(8)))
    out = np.stack([np.asarray(res.results[i]["out"]) for i in range(8)],
                   axis=0)
    return out


# version 15 (final)
# speedup vs baseline: 1.7123x; 1.6914x over previous
import os
import numpy as np
import concourse.bass as bass
import concourse.mybir as mybir
from concourse.bass_utils import run_bass_kernel_spmd
from contextlib import ExitStack

F32 = mybir.dt.float32
BF16 = mybir.dt.bfloat16
AF = mybir.ActivationFunctionType
ALU = mybir.AluOpType
AX = mybir.AxisListType.X

KCH = int(os.environ.get("KCH", "0")) or None
KPH = int(os.environ.get("KPH", "99"))
T, D = 2048, 2048
TC = 512
NCH = T // TC
NTG = TC // 128
NDC = D // 128
NKC = D // 128
NST = 16
DTR = 128
NEXP = 32
FEXP = 768
NFC = FEXP // 128
FSH = 1536
EPS = 1e-5
W1 = 516
TH = 1024
NSB = TH // 512

ENGS = ("sync", "tensor", "vector", "scalar", "gpsimd")


class Sem:
    def __init__(self, nc, name):
        self.h = nc.alloc_semaphore(name=name)
        self.v = 0
        self.name = name


def build_nc():
    nc = bass.Bass("TRN2", target_bir_lowering=False)
    nc.detect_race_conditions = False

    x_h = nc.dram_tensor("x", [T, D], F32, kind="ExternalInput").ap()
    cw_h = nc.dram_tensor("conv_w", [D, 4], F32, kind="ExternalInput").ap()
    cb_h = nc.dram_tensor("conv_b", [D], F32, kind="ExternalInput").ap()
    xpw_h = nc.dram_tensor("xproj_w", [160, D], F32, kind="ExternalInput").ap()
    dtw_h = nc.dram_tensor("dtproj_w", [D, DTR], F32, kind="ExternalInput").ap()
    dtb_h = nc.dram_tensor("dtproj_b", [D], F32, kind="ExternalInput").ap()
    al_h = nc.dram_tensor("A_log", [D, NST], F32, kind="ExternalInput").ap()
    dp_h = nc.dram_tensor("D_param", [D], F32, kind="ExternalInput").ap()
    gw_h = nc.dram_tensor("gate_w", [D, NEXP], F32, kind="ExternalInput").ap()
    wg_h = nc.dram_tensor("wg", [NEXP, D, FEXP], F32, kind="ExternalInput").ap()
    wu_h = nc.dram_tensor("wu", [NEXP, D, FEXP], F32, kind="ExternalInput").ap()
    wd_h = nc.dram_tensor("wd", [NEXP, FEXP, D], F32, kind="ExternalInput").ap()
    sg_h = nc.dram_tensor("sg", [D, FSH], F32, kind="ExternalInput").ap()
    su_h = nc.dram_tensor("su", [D, FSH], F32, kind="ExternalInput").ap()
    sd_h = nc.dram_tensor("sd", [FSH, D], F32, kind="ExternalInput").ap()
    iden_h = nc.dram_tensor("iden", [128, 128], F32, kind="ExternalInput").ap()
    out_h = nc.dram_tensor("out", [T, D], F32, kind="ExternalOutput").ap()
    bc_d = nc.dram_tensor("bc_d", [32, T], F32, kind="Internal").ap()
    comb_d = nc.dram_tensor("comb_d", [NEXP, T], BF16, kind="Internal").ap()
    x2_d = nc.dram_tensor("x2_d", [T, D], F32, kind="Internal").ap()
    zb_d = nc.dram_tensor("zb_d", [NKC, 128, T], BF16, kind="Internal").ap()

    ctxP = ExitStack()
    ctxM = ExitStack()
    sbtP = lambda name, shape, dt=F32: ctxP.enter_context(
        nc.sbuf_tensor("sb_" + name, shape, dt))
    sbtM = lambda name, shape, dt=F32: ctxM.enter_context(
        nc.sbuf_tensor("sb_" + name, shape, dt))
    psb = lambda name, shape: ctxP.enter_context(nc.psum_tensor(name, shape, F32))

    iden = sbtP("iden", [128, 128])
    xin = sbtP("xin", [128, 2048])
    stats = sbtP("stats", [128, 16])
    cvt = sbtP("cvt", [128, 128])

    xpwT = sbtM("xpwT", [128, NKC * 160])
    dtpwT = sbtM("dtpwT", [128, D])
    aneg = sbtM("aneg", [128, NDC * NST])
    cw = sbtM("cw", [128, NDC * 4])
    cb = sbtM("cbv", [128, NDC])
    db = sbtM("dbv", [128, NDC])
    dpv = sbtM("dpv", [128, NDC])
    gwT = sbtM("gwT", [128, NKC * 32])
    P1 = sbtM("P1", [128, NDC * W1])
    P2 = sbtM("P2", [128, NDC * 512])
    P3 = sbtM("P3", [128, NDC * 512])
    P4 = sbtM("P4", [128, NDC * 512])
    P5 = sbtM("P5", [128, 2048])
    a2 = sbtM("a2", [128, 1024])
    bbuf = sbtM("bbuf", [128, 512])
    h2 = sbtM("h2", [128, 1024])
    ytb = sbtM("ytb", [128, 1024])
    state = sbtM("state", [128, 256])
    brep = sbtM("brep", [128, 2048])
    dtT = sbtM("dtT", [128, 512])
    zt2 = sbtM("zt2", [128, 1024], BF16)
    bcT = sbtM("bcT", [32, 512])
    halo = sbtM("halo", [128, NDC * 3])
    spl = sbtM("spl", [128, 1024])
    scrT = spl[0:32, 0:512]
    cmbT = spl[0:32, 512:1024]
    cmbTb = zt2[0:32, 0:512]
    scr_tm = a2[:, 0:128]
    m8t = a2[:, 256:288]
    selb = a2[:, 288:320]
    cmbb = a2[:, 320:352]

    ps = [psb(f"ps{i}", [128, 512]) for i in range(6)]
    pt = [psb(f"pt{i}", [128, 128]) for i in range(2)]

    s_ld = Sem(nc, "s_ld")
    s_st = Sem(nc, "s_st")
    s_pe = Sem(nc, "s_pe")
    s_dve = Sem(nc, "s_dve")
    s_act = Sem(nc, "s_act")
    s_pool = Sem(nc, "s_pool")
    ALLSEMS = (s_ld, s_st, s_pe, s_dve, s_act, s_pool)

    for sm in ALLSEMS:
        nc.gpsimd.sem_clear(sm.h)
    nc.all_engine_barrier()

    EM = {e: [] for e in ENGS}
    WAITS = []

    def op(eng, fn, incs=()):
        def thunk(fn=fn, incs=tuple(incs)):
            ins = fn()
            for s, k in incs:
                ins = ins.then_inc(s.h, k)
        EM[eng].append(thunk)
        for s, k in incs:
            s.v += k

    def wt(eng, sem, val):
        if val <= 0:
            return
        WAITS.append((sem, val))
        EM[eng].append(lambda sem=sem, val=val, eng=eng:
                       getattr(nc, eng).wait_ge(sem.h, val))

    E = {e: getattr(nc, e) for e in ENGS}

    bank_free = {}

    def pe_claim(bank):
        for sem, val in bank_free.get(bank, ()):
            wt("tensor", sem, val)
        bank_free[bank] = []

    def bank_consumed(bank, sem):
        bank_free.setdefault(bank, []).append((sem, sem.v))

    bfc = lambda apx: apx.bitcast(BF16)

    def bhalf(tile_ap, c0, width):
        return bfc(tile_ap)[:, 2 * c0 + 1: 2 * (c0 + width): 2]

    def wait_list(eng, lst):
        for sem, val in lst:
            wt(eng, sem, val)

    op("vector", lambda: E["vector"].memset(state[:], 0.0))
    for dc in range(NDC):
        last = dc == NDC - 1
        op("vector", lambda dc=dc: E["vector"].memset(
            P1[:, dc * W1: dc * W1 + 4], 0.0),
            incs=([(s_dve, 1)] if last else []))

    op("sync", lambda: E["sync"].dma_start(out=iden[:], in_=iden_h),
       incs=[(s_ld, 16)])
    op("sync", lambda: E["sync"].dma_start(
        out=cw[:].rearrange("p (c k) -> p c k", k=4),
        in_=cw_h.rearrange("(c p) k -> p c k", p=128)), incs=[(s_ld, 16)])
    op("sync", lambda: E["sync"].dma_start(
        out=aneg[:].rearrange("p (c n) -> p c n", n=NST),
        in_=al_h.rearrange("(c p) n -> p c n", p=128)), incs=[(s_ld, 16)])
    op("sync", lambda: E["sync"].dma_start(
        out=gwT[:].rearrange("p (c e) -> p c e", e=NEXP),
        in_=gw_h.rearrange("(c p) e -> p c e", p=128)), incs=[(s_ld, 16)])
    v0 = s_ld.v

    wt("scalar", s_ld, v0)
    op("scalar", lambda: E["scalar"].activation(
        out=aneg[:], in_=aneg[:], func=AF.Exp), incs=[(s_act, 1)])
    wt("vector", s_act, s_act.v)
    op("vector", lambda: E["vector"].tensor_scalar(
        out=aneg[:], in0=aneg[:], scalar1=-1.0, scalar2=None, op0=ALU.mult),
        incs=[(s_dve, 1)])

    for vec_h, dst in ((cb_h, cb), (dtb_h, db), (dp_h, dpv)):
        wt("sync", s_pe, s_pe.v)
        op("sync", lambda vec_h=vec_h: E["sync"].dma_start(
            out=bcT[0:16, 0:128], in_=vec_h.rearrange("(c p) -> c p", p=128)),
            incs=[(s_ld, 16)])
        wt("tensor", s_ld, s_ld.v)
        pe_claim("pt0")
        op("tensor", lambda: E["tensor"].transpose(
            pt[0][:, 0:16], bcT[0:16, 0:128], iden[0:16, 0:16]),
            incs=[(s_pe, 1)])
        wt("vector", s_pe, s_pe.v)
        op("vector", lambda dst=dst: E["vector"].tensor_copy(
            out=dst[:], in_=pt[0][:, 0:16]), incs=[(s_dve, 1)])
        bank_consumed("pt0", s_dve)

    op("sync", lambda: E["sync"].dma_start(out=xin[:], in_=xpw_h[0:128, :]),
       incs=[(s_ld, 16)])
    v = s_ld.v
    for kc in range(NKC):
        b = f"pt{kc % 2}"
        wt("tensor", s_ld, v)
        pe_claim(b)
        op("tensor", lambda kc=kc: E["tensor"].transpose(
            pt[kc % 2][:], xin[:, kc * 128:(kc + 1) * 128], iden[:]),
            incs=[(s_pe, 1)])
        wt("vector", s_pe, s_pe.v)
        op("vector", lambda kc=kc: E["vector"].tensor_copy(
            out=xpwT[:, kc * 160: kc * 160 + 128], in_=pt[kc % 2][:]),
            incs=[(s_dve, 1)])
        bank_consumed(b, s_dve)
    wt("sync", s_pe, s_pe.v)
    op("sync", lambda: E["sync"].dma_start(
        out=xin[0:32, :], in_=xpw_h[128:160, :]), incs=[(s_ld, 16)])
    v = s_ld.v
    for kc in range(NKC):
        b = f"pt{kc % 2}"
        wt("tensor", s_ld, v)
        pe_claim(b)
        op("tensor", lambda kc=kc: E["tensor"].transpose(
            pt[kc % 2][:, 0:32], xin[0:32, kc * 128:(kc + 1) * 128],
            iden[0:32, 0:32]), incs=[(s_pe, 1)])
        wt("vector", s_pe, s_pe.v)
        op("vector", lambda kc=kc: E["vector"].tensor_copy(
            out=xpwT[:, kc * 160 + 128: (kc + 1) * 160],
            in_=pt[kc % 2][:, 0:32]), incs=[(s_dve, 1)])
        bank_consumed(b, s_dve)

    for kc in range(NKC):
        half = (kc % 2) * 1024
        wt("sync", s_pe, s_pe.v)
        op("sync", lambda kc=kc, half=half: E["sync"].dma_start(
            out=xin[:, half:half + 128],
            in_=dtw_h[kc * 128:(kc + 1) * 128, :]), incs=[(s_ld, 16)])
        b = f"pt{kc % 2}"
        wt("tensor", s_ld, s_ld.v)
        pe_claim(b)
        op("tensor", lambda kc=kc, half=half: E["tensor"].transpose(
            pt[kc % 2][:], xin[:, half:half + 128], iden[:]), incs=[(s_pe, 1)])
        wt("vector", s_pe, s_pe.v)
        op("vector", lambda kc=kc: E["vector"].tensor_copy(
            out=dtpwT[:, kc * 128:(kc + 1) * 128], in_=pt[kc % 2][:]),
            incs=[(s_dve, 1)])
        bank_consumed(b, s_dve)

    free_xin = [(s_pe, s_pe.v)]
    free_a = [[], []]
    free_h = [[], []]
    free_yt = [[], []]
    free_brep = [[], []]
    free_spl = []
    free_dtT = []
    free_zt2 = [[], []]
    free_bcT = [(s_dve, s_dve.v)]

    v_sqred = [0]
    v_x2_spill = [0] * (NCH * NTG)
    v_comb_store = 0
    v_zb_spill = 0
    prev_end = []

    for c in range(KCH or NCH):
        if c > 0:
            for eng in ENGS:
                for sem, val in prev_end:
                    wt(eng, sem, val)

        if KPH < 1:
            continue
        for tg in range(NTG):
            wait_list("sync", free_xin)
            rows = c * TC + tg * 128
            op("sync", lambda rows=rows: E["sync"].dma_start(
                out=xin[:], in_=x_h[rows:rows + 128, :]), incs=[(s_ld, 16)])
            v_x = s_ld.v
            wt("scalar", s_ld, v_x)
            wt("scalar", s_dve, v_sqred[0])
            op("scalar", lambda: E["scalar"].activation(
                out=P5[:, 0:2048], in_=xin[:], func=AF.Square),
                incs=[(s_act, 1)])
            v_sq = s_act.v
            wt("vector", s_ld, v_x)
            op("vector", lambda: E["vector"].reduce_sum(
                stats[:, 0:1], xin[:], axis=AX), incs=[(s_dve, 1)])
            wt("vector", s_dve, s_dve.v)
            op("vector", lambda: E["vector"].tensor_scalar(
                out=stats[:, 0:1], in0=stats[:, 0:1], scalar1=1.0 / D,
                scalar2=None, op0=ALU.mult))
            wt("vector", s_act, v_sq)
            op("vector", lambda: E["vector"].reduce_sum(
                stats[:, 4:5], P5[:, 0:2048], axis=AX), incs=[(s_dve, 1)])
            v_sqred[0] = s_dve.v
            wt("vector", s_dve, s_dve.v)
            op("vector", lambda: E["vector"].tensor_scalar(
                out=stats[:, 4:5], in0=stats[:, 4:5], scalar1=1.0 / D,
                scalar2=None, op0=ALU.mult))
            op("vector", lambda: E["vector"].tensor_tensor(
                out=stats[:, 8:9], in0=stats[:, 0:1], in1=stats[:, 0:1],
                op=ALU.mult), incs=[(s_dve, 1)])
            wt("vector", s_dve, s_dve.v)
            op("vector", lambda: E["vector"].tensor_tensor(
                out=stats[:, 4:5], in0=stats[:, 4:5], in1=stats[:, 8:9],
                op=ALU.subtract), incs=[(s_dve, 1)])
            wt("scalar", s_dve, s_dve.v)
            op("scalar", lambda: E["scalar"].activation(
                out=stats[:, 8:9], in_=stats[:, 4:5], func=AF.Ln, bias=np.float32(EPS)))
            op("scalar", lambda: E["scalar"].activation(
                out=stats[:, 8:9], in_=stats[:, 8:9], func=AF.Exp, scale=np.float32(-0.5)),
                incs=[(s_act, 1)])
            wt("vector", s_act, s_act.v)
            op("vector", lambda: E["vector"].tensor_scalar(
                out=xin[:], in0=xin[:], scalar1=stats[:, 0:1],
                scalar2=stats[:, 8:9], op0=ALU.subtract, op1=ALU.mult),
                incs=[(s_dve, 1)])
            v_ln = s_dve.v
            for dc in range(NDC):
                b = f"pt{dc % 2}"
                wt("tensor", s_dve, v_ln)
                pe_claim(b)
                op("tensor", lambda dc=dc: E["tensor"].transpose(
                    pt[dc % 2][:], xin[:, dc * 128:(dc + 1) * 128], iden[:]),
                    incs=[(s_pe, 1)])
                wt("vector", s_pe, s_pe.v)
                op("vector", lambda dc=dc, tg=tg: E["vector"].tensor_copy(
                    out=P1[:, dc * W1 + 4 + tg * 128:
                            dc * W1 + 4 + (tg + 1) * 128],
                    in_=pt[dc % 2][:]), incs=[(s_dve, 1)])
                bank_consumed(b, s_dve)
            free_xin = [(s_pe, s_pe.v)]

        if c > 0:
            for dc in range(NDC):
                op("vector", lambda dc=dc: E["vector"].tensor_copy(
                    out=P1[:, dc * W1 + 1: dc * W1 + 4],
                    in_=halo[:, dc * 3:(dc + 1) * 3]),
                    incs=([(s_dve, 1)] if dc == NDC - 1 else []))

        if KPH < 2:
            continue
        v_conv_act = 0
        for dc in range(NDC):
            base = dc * W1
            wait_list("vector", free_spl)
            op("vector", lambda base=base, dc=dc: E["vector"].tensor_scalar(
                out=spl[:, 0:512], in0=P1[:, base + 1: base + 513],
                scalar1=cw[:, dc * 4: dc * 4 + 1], scalar2=None, op0=ALU.mult))
            for k in range(1, 4):
                op("vector", lambda base=base, dc=dc, k=k:
                   E["vector"].scalar_tensor_tensor(
                    out=spl[:, 0:512], in0=P1[:, base + 1 + k: base + 513 + k],
                    scalar=cw[:, dc * 4 + k: dc * 4 + k + 1],
                    in1=spl[:, 0:512], op0=ALU.mult, op1=ALU.add),
                    incs=([(s_dve, 1)] if k == 3 else []))
            wt("scalar", s_dve, s_dve.v)
            op("scalar", lambda dc=dc: E["scalar"].activation(
                out=P2[:, dc * 512:(dc + 1) * 512], in_=spl[:, 0:512],
                func=AF.Silu, bias=cb[:, dc:dc + 1]), incs=[(s_act, 1)])
            free_spl = [(s_act, s_act.v)]
            v_conv_act = s_act.v
            op("vector", lambda dc=dc, base=base: E["vector"].tensor_scalar(
                out=P4[:, dc * 512:(dc + 1) * 512],
                in0=P1[:, base + 4: base + 516],
                scalar1=dpv[:, dc:dc + 1], scalar2=None, op0=ALU.mult))
            op("vector", lambda dc=dc, base=base: E["vector"].tensor_copy(
                out=halo[:, dc * 3:(dc + 1) * 3],
                in_=P1[:, base + 513: base + 516]),
                incs=([(s_dve, 1)] if dc == NDC - 1 else []))

        if KPH < 3:
            continue
        pe_claim("ps0")
        pe_claim("ps1")
        wt("tensor", s_act, v_conv_act)
        for kc in range(NKC):
            op("tensor", lambda kc=kc: E["tensor"].matmul(
                ps[0][:], xpwT[:, kc * 160: kc * 160 + 128],
                P2[:, kc * 512:(kc + 1) * 512],
                start=(kc == 0), stop=(kc == NKC - 1), skip_group_check=True),
                incs=([(s_pe, 1)] if kc == NKC - 1 else []))
            op("tensor", lambda kc=kc: E["tensor"].matmul(
                ps[1][0:32, :], xpwT[:, kc * 160 + 128: (kc + 1) * 160],
                P2[:, kc * 512:(kc + 1) * 512],
                start=(kc == 0), stop=(kc == NKC - 1), skip_group_check=True),
                incs=([(s_pe, 1)] if kc == NKC - 1 else []))
        v_xproj = s_pe.v
        wt("vector", s_pe, v_xproj)
        wait_list("vector", free_dtT)
        op("vector", lambda: E["vector"].tensor_copy(out=dtT[:], in_=ps[0][:]),
           incs=[(s_dve, 1)])
        bank_consumed("ps0", s_dve)
        v_dtT = s_dve.v
        wait_list("vector", free_bcT)
        op("vector", lambda: E["vector"].tensor_copy(
            out=bcT[:], in_=ps[1][0:32, :]), incs=[(s_dve, 1)])
        bank_consumed("ps1", s_dve)
        v_bcT = s_dve.v

        wt("sync", s_dve, v_bcT)
        op("sync", lambda c=c: E["sync"].dma_start(
            out=bc_d[:, c * TC:(c + 1) * TC], in_=bcT[:]), incs=[(s_st, 16)])
        v_bc_store = s_st.v
        free_bcT = [(s_st, v_bc_store)]

        if KPH < 4:
            continue
        v_dt_ready = 0
        for dc in range(NDC):
            b = f"ps{2 + dc % 2}"
            wt("tensor", s_dve, v_dtT)
            pe_claim(b)
            op("tensor", lambda dc=dc: E["tensor"].matmul(
                ps[2 + dc % 2][:], dtpwT[:, dc * 128:(dc + 1) * 128], dtT[:],
                start=True, stop=True), incs=[(s_pe, 1)])
            v_mm = s_pe.v
            psd = ps[2 + dc % 2]
            wt("scalar", s_pe, v_mm)
            wait_list("scalar", free_spl)
            op("scalar", lambda psd=psd, dc=dc: E["scalar"].activation(
                out=spl[:, 0:512], in_=psd[:], func=AF.Relu,
                bias=db[:, dc:dc + 1]))
            op("scalar", lambda psd=psd, dc=dc: E["scalar"].activation(
                out=spl[:, 512:1024], in_=psd[:], func=AF.Abs,
                bias=db[:, dc:dc + 1]), incs=[(s_act, 1)])
            bank_consumed(b, s_act)
            op("scalar", lambda: E["scalar"].activation(
                out=spl[:, 512:1024], in_=spl[:, 512:1024], func=AF.Exp,
                scale=np.float32(-1.0)))
            op("scalar", lambda: E["scalar"].activation(
                out=spl[:, 512:1024], in_=spl[:, 512:1024], func=AF.Ln,
                bias=np.float32(1.0)), incs=[(s_act, 1)])
            wt("vector", s_act, s_act.v)
            op("vector", lambda dc=dc: E["vector"].tensor_add(
                P3[:, dc * 512:(dc + 1) * 512], spl[:, 0:512],
                spl[:, 512:1024]), incs=[(s_dve, 1)])
            free_spl = [(s_dve, s_dve.v)]
            v_dt_ready = s_dve.v

        for dc in range(NDC):
            op("vector", lambda dc=dc: E["vector"].tensor_mul(
                P2[:, dc * 512:(dc + 1) * 512],
                P3[:, dc * 512:(dc + 1) * 512],
                P2[:, dc * 512:(dc + 1) * 512]),
                incs=([(s_dve, 1)] if dc == NDC - 1 else []))
        v_dx = s_dve.v

        if KPH < 5:
            continue
        it = 0
        v_scan_act = 0
        for n in range(NST):
            sl = n % 2
            wait_list("sync", free_brep[sl])
            wt("sync", s_st, v_bc_store)
            op("sync", lambda n=n, sl=sl, c=c: E["sync"].dma_start(
                out=brep[:, sl * 1024: sl * 1024 + 512],
                in_=bc_d[n:n + 1, c * TC:(c + 1) * TC].partition_broadcast(128)),
                incs=[(s_ld, 16)])
            op("sync", lambda n=n, sl=sl, c=c: E["sync"].dma_start(
                out=brep[:, sl * 1024 + 512: (sl + 1) * 1024],
                in_=bc_d[16 + n:17 + n, c * TC:(c + 1) * TC].partition_broadcast(128)),
                incs=[(s_ld, 16)])
            v_brep = s_ld.v
            free_brep[sl] = []
            last_b = last_c = 0
            for dc in range(NDC):
                asl = it % 2
                hsl = it % 2
                ysl = it % 2
                wait_list("scalar", free_a[asl])
                if it == 0:
                    wt("scalar", s_dve, v_dt_ready)
                op("scalar", lambda dc=dc, n=n, asl=asl:
                   E["scalar"].activation(
                    out=a2[:, asl * 512:(asl + 1) * 512],
                    in_=P3[:, dc * 512:(dc + 1) * 512], func=AF.Exp,
                    scale=aneg[:, dc * NST + n: dc * NST + n + 1]),
                    incs=[(s_act, 1)])
                free_a[asl] = []
                v_a = s_act.v
                v_scan_act = v_a
                if dc == 0:
                    wt("vector", s_ld, v_brep)
                op("vector", lambda dc=dc, sl=sl: E["vector"].tensor_mul(
                    bbuf[:], P2[:, dc * 512:(dc + 1) * 512],
                    brep[:, sl * 1024: sl * 1024 + 512]))
                wt("vector", s_act, v_a)
                wait_list("vector", free_h[hsl])
                op("vector", lambda dc=dc, n=n, asl=asl, hsl=hsl:
                   E["vector"].tensor_tensor_scan(
                    out=h2[:, hsl * 512:(hsl + 1) * 512],
                    data0=a2[:, asl * 512:(asl + 1) * 512], data1=bbuf[:],
                    initial=state[:, dc * NST + n: dc * NST + n + 1],
                    op0=ALU.mult, op1=ALU.add), incs=[(s_dve, 1)])
                free_a[asl] = [(s_dve, s_dve.v)]
                free_h[hsl] = []
                v_scan = s_dve.v
                last_b = s_dve.v
                wt("gpsimd", s_dve, v_scan)
                wait_list("gpsimd", free_yt[ysl])
                op("gpsimd", lambda sl=sl, hsl=hsl, ysl=ysl:
                   E["gpsimd"].tensor_mul(
                    ytb[:, ysl * 512:(ysl + 1) * 512],
                    h2[:, hsl * 512:(hsl + 1) * 512],
                    brep[:, sl * 1024 + 512: (sl + 1) * 1024]))
                free_yt[ysl] = []
                op("gpsimd", lambda dc=dc, n=n, hsl=hsl:
                   E["gpsimd"].tensor_copy(
                    out=state[:, dc * NST + n: dc * NST + n + 1],
                    in_=h2[:, hsl * 512 + 511: hsl * 512 + 512]),
                    incs=[(s_pool, 1)])
                free_h[hsl] = [(s_pool, s_pool.v)]
                last_c = s_pool.v
                if dc % 2 == 1:
                    op("gpsimd", lambda dc=dc, ysl=ysl: E["gpsimd"].tensor_add(
                        P4[:, dc * 512:(dc + 1) * 512],
                        P4[:, dc * 512:(dc + 1) * 512],
                        ytb[:, ysl * 512:(ysl + 1) * 512]), incs=[(s_pool, 1)])
                    free_yt[ysl] = [(s_pool, s_pool.v)]
                    last_c = s_pool.v
                else:
                    wt("vector", s_pool, s_pool.v)
                    op("vector", lambda dc=dc, ysl=ysl: E["vector"].tensor_add(
                        P4[:, dc * 512:(dc + 1) * 512],
                        P4[:, dc * 512:(dc + 1) * 512],
                        ytb[:, ysl * 512:(ysl + 1) * 512]), incs=[(s_dve, 1)])
                    free_yt[ysl] = [(s_dve, s_dve.v)]
                    last_b = s_dve.v
                it += 1
            free_brep[sl] = [(s_dve, last_b), (s_pool, last_c)]
        v_y_done_v = s_dve.v
        v_y_done_p = s_pool.v

        if KPH < 6:
            continue
        v_x2_list = [0] * NTG
        for tg in range(NTG):
            wait_list("sync", free_xin)
            rows = c * TC + tg * 128
            op("sync", lambda rows=rows: E["sync"].dma_start(
                out=xin[:], in_=x_h[rows:rows + 128, :]), incs=[(s_ld, 16)])
            v_x = s_ld.v
            for dc in range(NDC):
                b = f"pt{dc % 2}"
                wt("tensor", s_dve, v_y_done_v)
                if tg == 0 and dc == 0:
                    wt("tensor", s_pool, v_y_done_p)
                pe_claim(b)
                op("tensor", lambda dc=dc, tg=tg: E["tensor"].transpose(
                    pt[dc % 2][:],
                    P4[:, dc * 512 + tg * 128: dc * 512 + (tg + 1) * 128],
                    iden[:]), incs=[(s_pe, 1)])
                wt("vector", s_pe, s_pe.v)
                if dc == 0:
                    wt("vector", s_ld, v_x)
                    if tg == 0:
                        wt("vector", s_act, v_scan_act)
                op("vector", lambda dc=dc, tg=tg: E["vector"].tensor_add(
                    P3[:, tg * 2048 + dc * 128: tg * 2048 + (dc + 1) * 128],
                    pt[dc % 2][:], xin[:, dc * 128:(dc + 1) * 128]),
                    incs=[(s_dve, 1)])
                bank_consumed(b, s_dve)
            free_xin = [(s_dve, s_dve.v)]
            v_x2_list[tg] = s_dve.v
            wt("sync", s_dve, v_x2_list[tg])
            op("sync", lambda tg=tg, c=c: E["sync"].dma_start(
                out=x2_d[c * TC + tg * 128: c * TC + (tg + 1) * 128, :],
                in_=P3[:, tg * 2048:(tg + 1) * 2048]), incs=[(s_st, 16)])
            v_x2_spill[c * NTG + tg] = s_st.v
        v_yT_done = s_pe.v

        v_z = 0
        v_ln2_act = 0
        for tg in range(NTG):
            wt("scalar", s_dve, v_x2_list[tg])
            wt("scalar", s_dve, v_sqred[0])
            op("scalar", lambda tg=tg: E["scalar"].activation(
                out=P5[:, 0:2048], in_=P3[:, tg * 2048:(tg + 1) * 2048],
                func=AF.Square), incs=[(s_act, 1)])
            v_sq2 = s_act.v
            op("vector", lambda tg=tg: E["vector"].reduce_sum(
                stats[:, 0:1], P3[:, tg * 2048:(tg + 1) * 2048], axis=AX),
                incs=[(s_dve, 1)])
            wt("vector", s_dve, s_dve.v)
            op("vector", lambda: E["vector"].tensor_scalar(
                out=stats[:, 0:1], in0=stats[:, 0:1], scalar1=1.0 / D,
                scalar2=None, op0=ALU.mult))
            wt("vector", s_act, v_sq2)
            op("vector", lambda: E["vector"].reduce_sum(
                stats[:, 4:5], P5[:, 0:2048], axis=AX), incs=[(s_dve, 1)])
            v_sqred[0] = s_dve.v
            wt("vector", s_dve, s_dve.v)
            op("vector", lambda: E["vector"].tensor_scalar(
                out=stats[:, 4:5], in0=stats[:, 4:5], scalar1=1.0 / D,
                scalar2=None, op0=ALU.mult))
            op("vector", lambda: E["vector"].tensor_tensor(
                out=stats[:, 8:9], in0=stats[:, 0:1], in1=stats[:, 0:1],
                op=ALU.mult), incs=[(s_dve, 1)])
            wt("vector", s_dve, s_dve.v)
            op("vector", lambda: E["vector"].tensor_tensor(
                out=stats[:, 4:5], in0=stats[:, 4:5], in1=stats[:, 8:9],
                op=ALU.subtract), incs=[(s_dve, 1)])
            wt("scalar", s_dve, s_dve.v)
            op("scalar", lambda: E["scalar"].activation(
                out=stats[:, 8:9], in_=stats[:, 4:5], func=AF.Ln, bias=np.float32(EPS)))
            op("scalar", lambda: E["scalar"].activation(
                out=stats[:, 8:9], in_=stats[:, 8:9], func=AF.Exp, scale=np.float32(-0.5)),
                incs=[(s_act, 1)])
            v_ln2_act = s_act.v
            wt("vector", s_act, s_act.v)
            if tg == 0:
                wt("vector", s_pe, v_yT_done)
            op("vector", lambda tg=tg: E["vector"].tensor_scalar(
                out=P4[:, tg * 2048:(tg + 1) * 2048],
                in0=P3[:, tg * 2048:(tg + 1) * 2048],
                scalar1=stats[:, 0:1], scalar2=stats[:, 8:9],
                op0=ALU.subtract, op1=ALU.mult), incs=[(s_dve, 1)])
            v_z = s_dve.v

        if KPH < 7:
            continue
        v_spill_last = 0
        for tg in range(NTG):
            for dc in range(NDC):
                b = f"pt{dc % 2}"
                wt("tensor", s_dve, v_z)
                pe_claim(b)
                op("tensor", lambda tg=tg, dc=dc: E["tensor"].transpose(
                    pt[dc % 2][:],
                    P4[:, tg * 2048 + dc * 128: tg * 2048 + (dc + 1) * 128],
                    iden[:]), incs=[(s_pe, 1)])
                wt("vector", s_pe, s_pe.v)
                if tg == 0 and dc == 0:
                    wt("vector", s_st, v_x2_spill[c * NTG + NTG - 1])
                    wt("vector", s_act, v_ln2_act)
                op("vector", lambda tg=tg, dc=dc: E["vector"].tensor_copy(
                    out=P3[:, dc * 512 + tg * 128: dc * 512 + (tg + 1) * 128],
                    in_=pt[dc % 2][:]), incs=[(s_dve, 1)])
                bank_consumed(b, s_dve)
        v_zf = s_dve.v
        for dc in range(NDC):
            zsl = dc % 2
            wait_list("vector", free_zt2[zsl])
            op("vector", lambda dc=dc, zsl=zsl: E["vector"].tensor_copy(
                out=zt2[:, zsl * 512:(zsl + 1) * 512],
                in_=P3[:, dc * 512:(dc + 1) * 512]), incs=[(s_dve, 1)])
            free_zt2[zsl] = []
            wt("sync", s_dve, s_dve.v)
            op("sync", lambda dc=dc, zsl=zsl, c=c: E["sync"].dma_start(
                out=zb_d[dc, :, c * TC:(c + 1) * TC],
                in_=zt2[:, zsl * 512:(zsl + 1) * 512]), incs=[(s_st, 16)])
            free_zt2[zsl] = [(s_st, s_st.v)]
        v_zb_spill = s_st.v

        pe_claim("ps1")
        wt("tensor", s_dve, v_zf)
        for kc in range(NKC):
            op("tensor", lambda kc=kc: E["tensor"].matmul(
                ps[1][0:32, :], gwT[:, kc * 32:(kc + 1) * 32],
                P3[:, kc * 512:(kc + 1) * 512],
                start=(kc == 0), stop=(kc == NKC - 1)),
                incs=([(s_pe, 1)] if kc == NKC - 1 else []))
        v_gate = s_pe.v
        wt("scalar", s_pe, v_gate)
        op("scalar", lambda: E["scalar"].activation(
            out=scrT, in_=ps[1][0:32, :], func=AF.Sigmoid),
            incs=[(s_act, 1)])
        bank_consumed("ps1", s_act)
        v_scr = s_act.v
        for tg in range(NTG):
            b = f"pt{tg % 2}"
            wt("tensor", s_act, v_scr)
            pe_claim(b)
            op("tensor", lambda tg=tg: E["tensor"].transpose(
                pt[tg % 2][:, 0:32], scrT[:, tg * 128:(tg + 1) * 128],
                iden[0:32, 0:32]), incs=[(s_pe, 1)])
            wt("vector", s_pe, s_pe.v)
            op("vector", lambda tg=tg: E["vector"].tensor_copy(
                out=scr_tm[:, 0:32],
                in_=pt[tg % 2][:, 0:32]), incs=[(s_dve, 1)])
            bank_consumed(b, s_dve)
            wt("vector", s_dve, s_dve.v)
            op("vector", lambda tg=tg: E["vector"].max(
                out=m8t[:, 0:8],
                in_=scr_tm[:, 0:32]), incs=[(s_dve, 1)])
            wt("vector", s_dve, s_dve.v)
            op("vector", lambda tg=tg: E["vector"].tensor_scalar(
                out=selb[:], in0=scr_tm[:, 0:32],
                scalar1=m8t[:, 5:6],
                scalar2=None, op0=ALU.is_ge), incs=[(s_dve, 1)])
            wt("vector", s_dve, s_dve.v)
            op("vector", lambda tg=tg: E["vector"].tensor_mul(
                selb[:], selb[:], scr_tm[:, 0:32]),
                incs=[(s_dve, 1)])
            wt("vector", s_dve, s_dve.v)
            op("vector", lambda tg=tg: E["vector"].reduce_sum(
                stats[:, 8:9], selb[:], axis=AX), incs=[(s_dve, 1)])
            wt("vector", s_dve, s_dve.v)
            op("vector", lambda tg=tg: E["vector"].tensor_scalar(
                out=stats[:, 8:9], in0=stats[:, 8:9],
                scalar1=1e-20, scalar2=None, op0=ALU.add), incs=[(s_dve, 1)])
            wt("vector", s_dve, s_dve.v)
            op("vector", lambda tg=tg: E["vector"].reciprocal(
                out=stats[:, 12:13], in_=stats[:, 8:9]),
                incs=[(s_dve, 1)])
            wt("vector", s_dve, s_dve.v)
            op("vector", lambda tg=tg: E["vector"].tensor_scalar(
                out=cmbb[:], in0=selb[:], scalar1=stats[:, 12:13],
                scalar2=None, op0=ALU.mult), incs=[(s_dve, 1)])
            b = f"pt{tg % 2}"
            wt("tensor", s_dve, s_dve.v)
            pe_claim(b)
            op("tensor", lambda tg=tg: E["tensor"].transpose(
                pt[tg % 2][0:32, :], cmbb[:], iden[:]), incs=[(s_pe, 1)])
            wt("vector", s_pe, s_pe.v)
            op("vector", lambda tg=tg: E["vector"].tensor_copy(
                out=cmbT[:, tg * 128:(tg + 1) * 128],
                in_=pt[tg % 2][0:32, 0:128]), incs=[(s_dve, 1)])
            bank_consumed(b, s_dve)
        v_comb = s_dve.v
        wt("vector", s_dve, v_comb)
        wait_list("vector", free_zt2[0])
        op("vector", lambda: E["vector"].tensor_copy(out=cmbTb, in_=cmbT),
           incs=[(s_dve, 1)])
        free_zt2[0] = []
        wt("sync", s_dve, s_dve.v)
        op("sync", lambda c=c: E["sync"].dma_start(
            out=comb_d[:, c * TC:(c + 1) * TC], in_=cmbTb), incs=[(s_st, 16)])
        free_zt2[0] = [(s_st, s_st.v)]
        v_comb_store = s_st.v

        prev_end = [(s_dve, s_dve.v), (s_act, s_act.v), (s_pe, s_pe.v),
                    (s_pool, s_pool.v), (s_st, s_st.v), (s_ld, s_ld.v)]

    ctxM.close()
    ctxR = ExitStack()
    ctxF = ExitStack()
    sbtF = lambda name, shape, dt=F32: ctxF.enter_context(
        nc.sbuf_tensor("sb_" + name, shape, dt))
    routv = ctxR.enter_context(nc.sbuf_tensor("sb_routv", [128, NDC * 2048], BF16))
    zbh = sbtF("zbh", [128, NKC * 1024], BF16)
    actv = [sbtF(f"act{i}", [128, 2 * NFC * 1024], BF16) for i in range(2)]
    gus = sbtF("gus", [128, 2048])
    dns = sbtF("dns", [128, 2 * 12 * 256])
    wrep = sbtF("wrep", [128, 2 * 2048], BF16)
    sG = sbtF("sG", [128, 1024])
    uW = sbtF("uW", [128, 1024])

    v_F_end = [0, 0, 0, 0]
    if KPH >= 8:
        free_zbh = []
        free_gus = [[], []]
        free_dns = [[], []]
        free_wrep = [[], []]
        free_act = [[], []]
        free_sG = []
        free_uW = []
        gu_seq = [0]

        def gu_dma(e, fc, hf, mat):
            slot = gu_seq[0] % 2
            gu_seq[0] += 1
            wait_list("sync", free_gus[slot])
            if e < NEXP:
                src_h = wg_h if mat == 0 else wu_h
                src = src_h[e, :, fc * 128:(fc + 1) * 128].rearrange(
                    "(c p) f -> p c f", p=128)[:, hf * 8:(hf + 1) * 8, :]
            else:
                src_h = sg_h if mat == 0 else su_h
                s = e - NEXP
                src = src_h[:, s * FEXP + fc * 128: s * FEXP + (fc + 1) * 128
                            ].rearrange("(c p) f -> p c f", p=128)[
                                :, hf * 8:(hf + 1) * 8, :]
            col = slot * 1024
            dst = gus[:, col:col + 1024].rearrange("p (c f) -> p c f", f=128)
            op("sync", lambda src=src, dst=dst: E["sync"].dma_start(
                out=dst, in_=src), incs=[(s_ld, 16)])
            free_gus[slot] = []
            return slot, s_ld.v

        def dn_dma(dslot, idx, e, fkc, dg):
            if e < NEXP:
                src = wd_h[e, fkc * 128:(fkc + 1) * 128,
                           dg * 256:(dg + 1) * 256]
            else:
                s = e - NEXP
                src = sd_h[s * FEXP + fkc * 128: s * FEXP + (fkc + 1) * 128,
                           dg * 256:(dg + 1) * 256]
            col = dslot * 12 * 256 + idx * 256
            op("sync", lambda src=src, col=col: E["sync"].dma_start(
                out=dns[:, col:col + 256], in_=src), incs=[(s_ld, 16)])
            return s_ld.v

        first_gu = True
        for h in range(2):
            wait_list("sync", free_zbh)
            wt("sync", s_st, v_zb_spill)
            for kc in range(NKC):
                op("sync", lambda kc=kc, h=h: E["sync"].dma_start(
                    out=zbh[:, kc * 1024:(kc + 1) * 1024],
                    in_=zb_d[kc, :, h * TH:(h + 1) * TH]), incs=[(s_ld, 16)])
            v_zbh = s_ld.v
            free_zbh = []
            v_zbh_lastmm = 0

            for grp in range(NEXP // 2 + 1):
                es = [2 * grp, 2 * grp + 1]
                aslot = grp % 2
                wsl = grp % 2
                v_wrep = [0, 0]
                for ei, e in enumerate(es):
                    if e < NEXP:
                        if ei == 0:
                            wait_list("sync", free_wrep[wsl])
                            wt("sync", s_st, v_comb_store)
                        op("sync", lambda e=e, ei=ei, h=h, wsl=wsl:
                           E["sync"].dma_start(
                            out=wrep[:, wsl * 2048 + ei * 1024:
                                     wsl * 2048 + (ei + 1) * 1024],
                            in_=comb_d[e:e + 1, h * TH:(h + 1) * TH
                                       ].partition_broadcast(128)),
                            incs=[(s_ld, 16)])
                        v_wrep[ei] = s_ld.v
                        if ei == 1:
                            free_wrep[wsl] = []
                for ei, e in enumerate(es):
                    for fc in range(NFC):
                        pe_claim("ps0")
                        pe_claim("ps1")
                        pe_claim("ps2")
                        pe_claim("ps3")
                        for mat in (0, 1):
                            pst0 = ps[mat * 2]
                            pst1 = ps[mat * 2 + 1]
                            for hf in (0, 1):
                                slot, vld = gu_dma(e, fc, hf, mat)
                                wt("tensor", s_ld, vld)
                                if first_gu:
                                    wt("tensor", s_ld, v_zbh)
                                    first_gu = False
                                elif hf == 0 and fc == 0 and mat == 0 and ei == 0 and grp == 0:
                                    wt("tensor", s_ld, v_zbh)
                                for kc in range(hf * 8, hf * 8 + 8):
                                    col = slot * 1024 + (kc % 8) * 128
                                    lhsT = bhalf(gus[:], col, 128)
                                    for sub in range(NSB):
                                        pstt = pst0 if sub == 0 else pst1
                                        op("tensor",
                                           lambda pstt=pstt, lhsT=lhsT, kc=kc, sub=sub:
                                           E["tensor"].matmul(
                                            pstt[:], lhsT,
                                            zbh[:, kc * 1024 + sub * 512:
                                                kc * 1024 + (sub + 1) * 512],
                                            start=(kc == 0), stop=(kc == NKC - 1),
                                            skip_group_check=True),
                                            incs=([(s_pe, 1)]
                                                  if (kc % 8 == 7 and sub == NSB - 1)
                                                  else []))
                                free_gus[slot] = [(s_pe, s_pe.v)]
                        v_gu = s_pe.v
                        v_zbh_lastmm = s_pe.v
                        wt("scalar", s_pe, v_gu)
                        wait_list("scalar", free_sG)
                        op("scalar", lambda: E["scalar"].activation(
                            out=sG[:, 0:512], in_=ps[0][:], func=AF.Silu))
                        op("scalar", lambda: E["scalar"].activation(
                            out=sG[:, 512:1024], in_=ps[1][:], func=AF.Silu),
                            incs=[(s_act, 1)])
                        bank_consumed("ps0", s_act)
                        bank_consumed("ps1", s_act)
                        free_sG = []
                        v_sg = s_act.v
                        acol = ei * NFC * 1024 + fc * 1024
                        if e < NEXP:
                            wt("vector", s_ld, v_wrep[ei])
                            wait_list("vector", free_uW)
                            wt("vector", s_pe, v_gu)
                            op("vector", lambda ei=ei, wsl=wsl: E["vector"].tensor_mul(
                                uW[:, 0:512], ps[2][:],
                                wrep[:, wsl * 2048 + ei * 1024:
                                     wsl * 2048 + ei * 1024 + 512]))
                            op("vector", lambda ei=ei, wsl=wsl: E["vector"].tensor_mul(
                                uW[:, 512:1024], ps[3][:],
                                wrep[:, wsl * 2048 + ei * 1024 + 512:
                                     wsl * 2048 + (ei + 1) * 1024]),
                                incs=[(s_dve, 1)])
                            bank_consumed("ps2", s_dve)
                            bank_consumed("ps3", s_dve)
                            free_uW = []
                            wt("vector", s_act, v_sg)
                            wait_list("vector", free_act[aslot])
                            op("vector", lambda aslot=aslot, acol=acol:
                               E["vector"].tensor_mul(
                                actv[aslot][:, acol:acol + 1024], sG[:], uW[:]),
                                incs=[(s_dve, 1)])
                            free_uW = [(s_dve, s_dve.v)]
                        else:
                            wt("vector", s_act, v_sg)
                            wait_list("vector", free_act[aslot])
                            wt("vector", s_pe, v_gu)
                            op("vector", lambda aslot=aslot, acol=acol:
                               E["vector"].tensor_mul(
                                actv[aslot][:, acol:acol + 512],
                                sG[:, 0:512], ps[2][:]))
                            op("vector", lambda aslot=aslot, acol=acol:
                               E["vector"].tensor_mul(
                                actv[aslot][:, acol + 512:acol + 1024],
                                sG[:, 512:1024], ps[3][:]), incs=[(s_dve, 1)])
                            bank_consumed("ps2", s_dve)
                            bank_consumed("ps3", s_dve)
                        free_sG = [(s_dve, s_dve.v)]
                        free_act[aslot] = []
                    if e < NEXP and ei == 1:
                        free_wrep[wsl] = [(s_dve, s_dve.v)]
                v_act_grp = s_dve.v

                for dg in range(8):
                    dslot = dg % 2
                    v_dn = [0] * 12
                    wait_list("sync", free_dns[dslot])
                    for ei in range(2):
                        for fkc in range(NFC):
                            idx = ei * NFC + fkc
                            v_dn[idx] = dn_dma(dslot, idx, es[ei], fkc, dg)
                    free_dns[dslot] = []
                    for dcl in range(2):
                        dc = dg * 2 + dcl
                        for sub in range(NSB):
                            b = f"ps{4 + ((dcl * NSB + sub) % 2)}"
                            pst = ps[4 + ((dcl * NSB + sub) % 2)]
                            pe_claim(b)
                            if dcl == 0 and sub == 0:
                                wt("tensor", s_dve, v_act_grp)
                                wt("tensor", s_ld, v_dn[11])
                            nmm = 0
                            for ei in range(2):
                                for fkc in range(NFC):
                                    idx = ei * NFC + fkc
                                    col = dslot * 12 * 256 + idx * 256 + dcl * 128
                                    lhsT = bhalf(dns[:], col, 128)
                                    rhs_lo = ei * NFC * 1024 + fkc * 1024 + sub * 512
                                    nmm += 1
                                    op("tensor",
                                       lambda pst=pst, lhsT=lhsT, rhs_lo=rhs_lo,
                                       aslot=aslot,
                                       first=(nmm == 1), last=(nmm == 12):
                                       E["tensor"].matmul(
                                        pst[:], lhsT,
                                        actv[aslot][:, rhs_lo:rhs_lo + 512],
                                        start=first, stop=last,
                                        skip_group_check=True),
                                        incs=([(s_pe, 1)] if nmm == 12 else []))
                            v_dnmm = s_pe.v
                            wt("vector", s_pe, v_dnmm)
                            rcol = dc * 2048 + h * 1024 + sub * 512
                            if grp == 0:
                                op("vector", lambda rcol=rcol, pst=pst:
                                   E["vector"].tensor_copy(
                                    out=routv[:, rcol:rcol + 512],
                                    in_=pst[:]), incs=[(s_dve, 1)])
                            else:
                                op("vector", lambda rcol=rcol, pst=pst:
                                   E["vector"].tensor_tensor(
                                    out=routv[:, rcol:rcol + 512],
                                    in0=pst[:],
                                    in1=routv[:, rcol:rcol + 512],
                                    op=ALU.add), incs=[(s_dve, 1)])
                            bank_consumed(b, s_dve)
                    free_dns[dslot] = [(s_pe, s_pe.v)]
                free_act[aslot] = [(s_pe, s_pe.v)]
            free_zbh = [(s_pe, v_zbh_lastmm)]
        v_F_end = [s_pe.v, s_dve.v, s_act.v, s_st.v]

    ctxF.close()
    ctxF2 = ExitStack()
    ostg = ctxF2.enter_context(nc.sbuf_tensor("sb_ostg", [128, 4096], F32))
    if KPH >= 9:
        v_routed = v_F_end[1]
        free_xin2 = list(free_xin)
        free_ostg = [[], []]
        free_cvt = []
        for tt in range(T // 128):
            osl = tt % 2
            wait_list("sync", free_xin2)
            wt("sync", s_st, v_x2_spill[tt])
            op("sync", lambda tt=tt: E["sync"].dma_start(
                out=xin[:], in_=x2_d[tt * 128:(tt + 1) * 128, :]),
                incs=[(s_ld, 16)])
            v_x2r = s_ld.v
            for dc in range(NDC):
                wait_list("vector", free_cvt)
                if tt == 0 and dc == 0:
                    wt("vector", s_dve, v_routed)
                op("vector", lambda dc=dc, tt=tt: E["vector"].tensor_copy(
                    out=cvt[:],
                    in_=routv[:, dc * 2048 + tt * 128:
                              dc * 2048 + (tt + 1) * 128]),
                    incs=[(s_dve, 1)])
                v_cvt = s_dve.v
                b = f"pt{dc % 2}"
                wt("tensor", s_dve, v_cvt)
                pe_claim(b)
                op("tensor", lambda dc=dc: E["tensor"].transpose(
                    pt[dc % 2][:], cvt[:], iden[:]), incs=[(s_pe, 1)])
                free_cvt = [(s_pe, s_pe.v)]
                wt("vector", s_pe, s_pe.v)
                if dc == 0:
                    wt("vector", s_ld, v_x2r)
                    wait_list("vector", free_ostg[osl])
                op("vector", lambda dc=dc, osl=osl: E["vector"].tensor_add(
                    ostg[:, osl * 2048 + dc * 128: osl * 2048 + (dc + 1) * 128],
                    pt[dc % 2][:], xin[:, dc * 128:(dc + 1) * 128]),
                    incs=[(s_dve, 1)])
                bank_consumed(b, s_dve)
            free_xin2 = [(s_dve, s_dve.v)]
            free_ostg[osl] = []
            wt("sync", s_dve, s_dve.v)
            op("sync", lambda tt=tt, osl=osl: E["sync"].dma_start(
                out=out_h[tt * 128:(tt + 1) * 128, :],
                in_=ostg[:, osl * 2048:(osl + 1) * 2048]), incs=[(s_st, 16)])
            free_ostg[osl] = [(s_st, s_st.v)]

    wt("sync", s_st, s_st.v)
    wt("sync", s_ld, s_ld.v)

    finals = {s: s.v for s in ALLSEMS}
    for sem, val in WAITS:
        assert val <= finals[sem], (sem.name, val, finals[sem])

    with nc.Block() as block:
        block.sync(lambda eng: [f() for f in EM["sync"]])
        block.tensor(lambda eng: [f() for f in EM["tensor"]])
        block.vector(lambda eng: [f() for f in EM["vector"]])
        block.scalar(lambda eng: [f() for f in EM["scalar"]])
        block.gpsimd(lambda eng: [f() for f in EM["gpsimd"]])
    ctxF2.close()
    ctxR.close()
    ctxP.close()
    return nc


_NC_CACHE = []


def kernel(L0_attn_x, L0_attn_conv_w, L0_attn_conv_b, L0_attn_xproj_w,
           L0_attn_dtproj_w, L0_attn_dtproj_b, L0_attn_A_log, L0_attn_D_param,
           L0_ffn_gate_w, L0_ffn_wg, L0_ffn_wu, L0_ffn_wd, L0_ffn_sg,
           L0_ffn_su, L0_ffn_sd):
    f32 = lambda a: np.ascontiguousarray(np.asarray(a), dtype=np.float32)
    x = f32(L0_attn_x)
    base = dict(
        conv_w=f32(L0_attn_conv_w).reshape(D, 4),
        conv_b=f32(L0_attn_conv_b),
        xproj_w=f32(L0_attn_xproj_w),
        dtproj_w=f32(L0_attn_dtproj_w),
        dtproj_b=f32(L0_attn_dtproj_b),
        A_log=f32(L0_attn_A_log),
        D_param=f32(L0_attn_D_param),
        gate_w=f32(L0_ffn_gate_w),
        wg=f32(L0_ffn_wg), wu=f32(L0_ffn_wu), wd=f32(L0_ffn_wd),
        sg=f32(L0_ffn_sg), su=f32(L0_ffn_su), sd=f32(L0_ffn_sd),
        iden=np.eye(128, dtype=np.float32),
    )
    if not _NC_CACHE:
        _NC_CACHE.append(build_nc())
    nc = _NC_CACHE[0]
    in_maps = [dict(base, x=np.ascontiguousarray(x[i])) for i in range(8)]
    res = run_bass_kernel_spmd(nc, in_maps, core_ids=list(range(8)))
    out = np.stack([np.asarray(res.results[i]["out"]) for i in range(8)],
                   axis=0)
    return out
